# version 27
# baseline (speedup 1.0000x reference)
import numpy as np
import concourse.bass as bass
import concourse.mybir as mybir
from concourse.bass_utils import run_bass_kernel_spmd

F32 = mybir.dt.float32
BF16 = mybir.dt.bfloat16
I32 = mybir.dt.int32
AF = mybir.ActivationFunctionType
ALU = mybir.AluOpType
AX = mybir.AxisListType

D = 1024
DI = 2048
DS = 64
FF0 = 2816
FF1 = 4096
NST = 16
BIG_NEG = -1.0e30
PAD = 8


STAGE_RANGES = []


def build_program(T=4096, n_cores=8):
    nc = bass.Bass("TRN2", target_bir_lowering=False)
    NT = T // 512
    NCH = T // 128
    TQ = T // 4
    TQ = T // 4
    SW = T + 2 * PAD

    ein = lambda n, shp, dt=F32: nc.dram_tensor(n, shp, dt, kind="ExternalInput")
    xT = ein("xT", [D, T])
    w0_in = ein("w0_in", [D, 2 * DI])
    w0_conv = ein("w0_conv", [DI, 4])
    w0_convb = ein("w0_convb", [DI, 1])
    w0_x = ein("w0_x", [DI, 129])
    w0_A = ein("w0_A", [1, DS])
    w0_D = ein("w0_D", [1, DI])
    w0_out = ein("w0_out", [DI, D])
    f0_clipx = ein("f0_clipx", [1, 1])
    f0_clipd = ein("f0_clipd", [1, 1])
    f0_wg = ein("f0_wg", [D, FF0])
    f0_wu = ein("f0_wu", [D, FF0])
    f0_wd = ein("f0_wd", [FF0, D])
    w1_in = ein("w1_in", [D, 2 * D])
    w1_conv = ein("w1_conv", [D, 4])
    w1_convb = ein("w1_convb", [D, 1])
    w1_x = ein("w1_x", [D, D + 2 * NST])
    w1_D = ein("w1_D", [1, D])
    w1_out = ein("w1_out", [D, D])
    f1_w1 = ein("f1_w1", [D, FF1])
    f1_conv = ein("f1_conv", [FF1, 3])
    f1_convb = ein("f1_convb", [FF1, 1])
    f1_w2 = ein("f1_w2", [FF1, D])
    cst_mask = ein("cst_mask", [1, 1024])
    cst_eyeb = ein("cst_eyeb", [128, 128])

    out = nc.dram_tensor("out", [D, T], F32, kind="ExternalOutput")

    dram = lambda n, shp, dt=BF16: nc.dram_tensor(n, shp, dt)
    d_xn = [dram("d_xn%d" % l, [D, T]) for l in range(2)]
    d_xc0 = dram("d_xc0", [DI, T])
    d_xct0 = dram("d_xct0", [T, DI])
    d_y2f0 = dram("d_y2f0", [DI, T])
    d_s = [dram("d_s%d" % l, [D, T]) for l in range(2)]
    d_h0 = dram("d_h0", [FF0, T])
    d_f = [dram("d_f%d" % l, [D, T]) for l in range(2)]
    d_x1 = dram("d_x1", [D, T], F32)
    d_xc1 = dram("d_xc1", [D, T])
    d_z1 = dram("d_z1", [D, T])
    d_dt1 = dram("d_dt1", [D, T], F32)
    d_u1 = dram("d_u1", [D, T])
    d_y1 = dram("d_y1", [D, T])
    d_h1 = dram("d_h1", [FF1, T])
    d_bc = dram("d_bc", [32, T])
    d_wgq = dram("d_wgq", [D, FF0])
    d_wuq = dram("d_wuq", [D, FF0])
    d_wdq = dram("d_wdq", [FF0, D])
    d_w1q = dram("d_w1q", [D, FF1])
    d_w2q = dram("d_w2q", [FF1, D])
    d_h1q = dram("d_h1q", [FF1, T])
    cc_in = nc.dram_tensor("cc_in", [1, 16], F32)
    cc_out = nc.dram_tensor("cc_out", [1, 16], F32)

    ctx = []
    def sbuf(name, shp, dt):
        h = nc.sbuf_tensor(name, shp, dt).__enter__()
        ctx.append(h)
        return h
    A1W = max(4 * SW, 16448)
    EXT = max(3 * SW, 4608)
    YO = max(EXT + 2048, 8192)
    A3W = YO + 2048
    FW = max(T, 4096)
    A1 = sbuf("A1", [128, A1W], BF16)
    A2 = sbuf("A2", [128, A1W], BF16)
    A3 = sbuf("A3", [128, A3W], BF16)
    WA = sbuf("WA", [128, 8448], BF16)
    WB = sbuf("WB", [128, 6144], BF16)
    FA = sbuf("FA", [128, FW], F32)
    FY = sbuf("FY", [128, FW], F32)
    FB = FA[:, 3072:4096]
    ccbv = None
    H0s = FA[0:64, 0:2048]
    PP = nc.psum_tensor("PP", [128, 3584], F32).__enter__()
    ctx.append(PP)
    PPb = nc.psum_tensor("PPb", [128, 1024], BF16).__enter__()
    ctx.append(PPb)

    def slot(arena, i):
        return arena[:, i * SW + PAD: i * SW + PAD + T]

    def slotn(arena, i, sh, n):
        return arena[:, i * SW + PAD + sh: i * SW + PAD + sh + n]

    def wtile_store_ap(dst, nK, jbase, nj, k):
        return bass.AP(dst, jbase * nK * 16384 + k * 128,
                       [[nK * 128, 128], [nK * 16384, nj], [1, 128]])

    def wtile_load_ap(dst, nK, po):
        return bass.AP(dst, po * nK * 16384, [[nK * 128, 128], [1, nK * 128]])

    onesb = sbuf("onesb", [128, 8], BF16)
    onesr = sbuf("onesr", [1, 128], BF16)
    onesf = sbuf("onesf", [1, 128], F32)
    ones33b = sbuf("ones33b", [65, 128], BF16)
    ones33f = sbuf("ones33f", [65, 128], F32)
    eyeb = sbuf("eyeb", [128, 128], BF16)
    eyef = sbuf("eyef", [128, 128], F32)
    convw0 = sbuf("convw0", [128, 16 * 4], F32)
    convb0 = sbuf("convb0", [128, 16], F32)
    convw1 = sbuf("convw1", [128, 8 * 4], F32)
    convb1 = sbuf("convb1", [128, 8], F32)
    convw2 = sbuf("convw2", [128, 32 * 3], F32)
    convb2 = sbuf("convb2", [128, 32], F32)
    arow = sbuf("arow", [1, DS], F32)
    drep0 = WB[:, 2048:4096]
    d1col = sbuf("d1col", [128, 8], F32)
    scal = sbuf("scal", [1, 64], F32)
    colsc = sbuf("colsc", [128, 16], F32)
    ROWS = sbuf("ROWS", [128, FW], F32)
    crow = ROWS[0:1, :]
    dtrow = ROWS[32:33, :]
    maskrow = ROWS[64:65, :]
    mflat = ROWS[96:97, 0:4096]
    qscr = ROWS[0:1, :]
    qinr = ROWS[32:33, :]
    srow = [ROWS[0:1, :], ROWS[32:33, :], ROWS[64:65, :], ROWS[96:97, :]]
    srowbA = A2[0:1, 8240:8240 + FW]
    srowbB = A2[32:33, 8240:8240 + FW]
    ssmBC = sbuf("ssmBC", [128, T], BF16)
    bdt0 = WA[0:64, 4352:4352 + T]
    crowD = sbuf("crowD", [1, 576], F32)
    erow = sbuf("erow", [1, NCH * 4], F32)
    brow = sbuf("brow", [1, NCH * 4], F32)
    Mfull = sbuf("Mfull", [128, 128], BF16)
    Cbt = sbuf("Cbt", [64, 128], BF16)
    Bbt = sbuf("Bbt", [64, 128], BF16)
    CbM = sbuf("CbM", [64, 128], BF16)
    Cct = sbuf("Cct", [64, 128], BF16)
    Bht = sbuf("Bht", [128, 64], BF16)
    mid6 = sbuf("mid6", [64, 8], F32)
    ediag = A3[0:64, 0:4096]
    mask64 = A3[0:64, EXT:EXT + 1024]
    trstB = A3[:, EXT + 1024:EXT + 1536]
    trst = A3[:, EXT + 1536:EXT + 2048]
    y2s = A3[:, YO:YO + 2048]
    accs = sbuf("accs", [64, 128], F32)
    growA = sbuf("growA", [1, 64], F32)
    growB = sbuf("growB", [1, 64], F32)
    gmaxc = sbuf("gmaxc", [128, 8], F32)
    sgn = None
    itl = None
    ytl = [FB[:, 0:512], FB[:, 512:1024]]
    rmode = sbuf("rmode", [128, 1], F32)
    rmi = sbuf("rmi", [128, 1], I32)
    sgn1 = sbuf("sgn1", [128, 512], F32)
    itl1 = sbuf("itl1", [128, 2048], I32)
    sgn = [sgn1, sgn1]
    itl = [itl1, itl1]
    carry = sbuf("carry", [128, 130], F32)

    blk = nc.Block().__enter__()
    _BB = nc.m.functions[0].blocks[-1]
    import re as _re
    _SEMRE = _re.compile(r"S\[(s\d+)\]\+=(\d+)")

    PERSISTENT = ("bar", "qr_y", "qr_s", "qr_v", "bc_mm", "bc_ev",
                  "fc_mm", "fc_ev", "cc_v", "ar_cc", "ini_g", "ini_s", "ini_v", "ini_t")

    class Em:
        def __init__(self):
            self.name2sem = {}
            self.base = {}
            self.cnt = {}
            self.physcnt = {}
            self.free = []
            self.semname = {}
            self.nsem = 0
            self.bar_t = 0
            self.eng = dict(tensor=nc.tensor, vector=nc.vector, scalar=nc.scalar,
                            gpsimd=nc.gpsimd, sync=nc.sync)

        def sem(self, name):
            if name not in self.name2sem:
                if self.free and not (name.startswith("d_") or name in PERSISTENT):
                    h = self.free.pop()
                else:
                    h = nc.semaphore("s%d" % self.nsem).__enter__()
                    self.semname[id(h)] = "s%d" % self.nsem
                    self.nsem += 1
                    self.physcnt[id(h)] = 0
                self.name2sem[name] = h
                self.base[name] = self.physcnt[id(h)]
                self.cnt[name] = 0
            return self.name2sem[name]

        def inc(self, instr, name, by=1):
            h = self.sem(name)
            self.cnt[name] += by
            self.physcnt[id(h)] += by
            instr.then_inc(h, by)
            return self.cnt[name]

        def wait(self, eng, name, val=None):
            h = self.sem(name)
            v = self.cnt[name] if val is None else val
            self.eng[eng].wait_ge(h, self.base[name] + v)

        def retire_locals(self):
            for name in list(self.name2sem):
                if name.startswith("d_") or name in PERSISTENT:
                    continue
                self.free.append(self.name2sem.pop(name))
                del self.base[name]
                del self.cnt[name]

        def dma(self, queue, out_, in_, ch=0):
            name = "d_%s_%d" % (queue[0], ch)
            h = self.sem(name)
            if self.cnt[name]:
                self.eng[queue].wait_ge(h, self.base[name] + self.cnt[name])
            n0 = len(_BB.instructions)
            self.eng[queue].dma_start(out=out_, in_=in_).then_inc(h, 16)
            n1 = len(_BB.instructions)
            sname = self.semname[id(h)]
            inc = 0
            for ins in _BB.instructions[n0:n1]:
                m = _SEMRE.search(str(ins))
                if m and m.group(1) == sname:
                    inc += int(m.group(2))
            if inc == 0:
                inc = 16
            self.cnt[name] += inc
            self.physcnt[id(h)] += inc
            return self.cnt[name]

        def dwait(self, eng, queue, ch=0, tgt=None):
            self.wait(eng, "d_%s_%d" % (queue[0], ch), tgt)

        def flush_dmas(self):
            for name in list(self.cnt):
                if name.startswith("d_s"):
                    self.wait("sync", name)
                elif name.startswith("d_g"):
                    self.wait("gpsimd", name)

        def barrier(self):
            self.flush_dmas()
            h = self.sem("bar")
            for e in ("tensor", "vector", "scalar", "gpsimd", "sync"):
                self.eng[e].nop().then_inc(h, 1)
                self.physcnt[id(h)] += 1
            self.bar_t += 5
            for e in ("tensor", "vector", "scalar", "gpsimd", "sync"):
                self.eng[e].wait_ge(h, self.bar_t)
            self.retire_locals()

        def finalize(self):
            blk.__exit__(None, None, None)

    em = Em()
    TE, VE, SE, GE, SY = nc.tensor, nc.vector, nc.scalar, nc.gpsimd, nc.sync

    def vtt(out_, a, b, op):
        VE.tensor_tensor(out_, a, b, op)
        VE.drain()

    def vts(out_, a, s1, s2, op0, op1=None):
        if op1 is None:
            VE.tensor_scalar(out_, a, s1, None, op0)
        else:
            VE.tensor_scalar(out_, a, s1, s2, op0, op1)
        VE.drain()

    def vstt(out_, in0, scalar, in1, op0, op1):
        VE.scalar_tensor_tensor(out_, in0, scalar, in1, op0, op1)
        VE.drain()

    def vcp(out_, in_):
        VE.tensor_copy(out_, in_)
        VE.drain()

    def act(out_, in_, func, **kw):
        SE.activation(out=out_, in_=in_, func=func, **kw)
        SE.drain()

    def col_from_scalar(idx, src_scalar_ap):
        TE.matmul(PP[0:128, 3072:3073], onesf[0:1, 0:128], src_scalar_ap,
                  start=True, stop=True)
        n = em.inc(TE.drain(), "bc_mm", 1)
        em.wait("vector", "bc_mm", n)
        vcp(colsc[:, idx:idx + 1], PP[0:128, 3072:3073])
        n2 = em.inc(VE.drain(), "bc_ev", 1)
        em.wait("tensor", "bc_ev", n2)

    def fold_col_max(src_col_ap, dst):
        TE.transpose(PP[0:1, 2048:2176], src_col_ap, eyef[:])
        n = em.inc(TE.drain(), "fc_mm", 1)
        em.wait("vector", "fc_mm", n)
        VE.tensor_reduce(scal[0:1, dst:dst + 1], PP[0:1, 2048:2176], AX.X, ALU.max)
        VE.drain()
        n2 = em.inc(VE.drain(), "fc_ev", 1)
        em.wait("tensor", "fc_ev", n2)

    def allreduce_max(si, di):
        VE.tensor_scalar(scal[0:1, 32:48],
                         bass.AP(scal, si, [[64, 1], [0, 16]]), 1.0, None, ALU.mult)
        VE.drain()
        n = em.inc(VE.drain(), "cc_v", 1)
        em.wait("sync", "cc_v", n)
        em.dma("sync", cc_in.ap(), scal[0:1, 32:48], ch=7)
        em.dwait("gpsimd", "sync", ch=7)
        cci = GE.collective_compute("AllReduce", ALU.max,
                                    replica_groups=[list(range(n_cores))],
                                    ins=[cc_in.ap()], outs=[cc_out.ap()])
        em.inc(cci, "ar_cc", 1)
        em.wait("gpsimd", "ar_cc")
        em.dma("gpsimd", scal[0:1, di:di + 1], cc_out.ap()[0:1, 0:1], ch=7)
        em.dwait("vector", "gpsimd", ch=7)
        em.dwait("scalar", "gpsimd", ch=7)

    def qround(dst_bf, par, lo, hi, post_col=None, post_rep=None, nn=512):
        yv, sg, it = ytl[par], sgn[par], itl[par]
        n0 = em.inc(VE.drain(), "qr_y", 1)
        em.wait("scalar", "qr_y", n0)
        act(sg[:, 0:nn], yv[:, 0:nn], AF.Sign)
        n1 = em.inc(SE.drain(), "qr_s", 1)
        em.wait("vector", "qr_s", n1)
        vstt(yv[:, 0:nn], sg[:, 0:nn], rmode[:, 0:1], yv[:, 0:nn], ALU.mult, ALU.add)
        vcp(it[:, 0:nn], yv[:, 0:nn])
        if post_col is None and post_rep is None:
            vts(dst_bf, it[:, 0:nn], float(lo), float(hi), ALU.max, ALU.min)
        elif post_rep is None:
            vts(it[:, 0:nn], it[:, 0:nn], float(lo), float(hi), ALU.max, ALU.min)
            vts(dst_bf, it[:, 0:nn], post_col, None, ALU.mult)
        else:
            vts(it[:, 0:nn], it[:, 0:nn], float(lo), float(hi), ALU.max, ALU.min)
            vtt(dst_bf, it[:, 0:nn], post_rep, ALU.mult)

    GE.memset(onesb[:], 1.0)
    GE.memset(onesr[:], 1.0)
    GE.memset(onesf[:], 1.0)
    GE.memset(ones33b[:], 1.0)
    GE.memset(ones33f[:], 1.0)
    GE.memset(gmaxc[:], 0.0)
    GE.memset(A1[:], 0.0)
    GE.memset(A2[:], 0.0)
    GE.memset(A3[:], 0.0)
    GE.memset(brow[:], 0.0)
    GE.memset(rmode[:], 0.75)
    em.inc(GE.drain(), "ini_g", 1)
    em.dma("gpsimd", eyeb[:], cst_eyeb.ap(), ch=0)
    for c in range(16):
        em.dma("sync", convw0[:, 4 * c:4 * c + 4], w0_conv.ap()[128 * c:128 * (c + 1), :], ch=0)
        em.dma("sync", convb0[:, c:c + 1], w0_convb.ap()[128 * c:128 * (c + 1), :], ch=0)
    for c in range(8):
        em.dma("sync", convw1[:, 4 * c:4 * c + 4], w1_conv.ap()[128 * c:128 * (c + 1), :], ch=0)
        em.dma("sync", convb1[:, c:c + 1], w1_convb.ap()[128 * c:128 * (c + 1), :], ch=0)
    for c in range(32):
        em.dma("sync", convw2[:, 3 * c:3 * c + 3], f1_conv.ap()[128 * c:128 * (c + 1), :], ch=0)
        em.dma("sync", convb2[:, c:c + 1], f1_convb.ap()[128 * c:128 * (c + 1), :], ch=0)
    em.dma("sync", arow[:], w0_A.ap(), ch=0)
    em.dma("sync", scal[0:1, 8:9], f0_clipx.ap(), ch=0)
    em.dma("sync", scal[0:1, 9:10], f0_clipd.ap(), ch=0)
    for c in range(8):
        em.dma("gpsimd", d1col[:, c:c + 1],
               w1_D.ap()[0:1, 128 * c:128 * (c + 1)].rearrange("a (p b) -> (a p) b", b=1), ch=0)
    em.dma("gpsimd", drep0[0:1, 0:2048], w0_D.ap()[0:1, :], ch=0)
    em.dwait("scalar", "sync", 0)
    em.dwait("scalar", "gpsimd", 0)
    em.wait("scalar", "ini_g", 1)
    act(arow[:], arow[:], AF.Exp)
    em.inc(SE.drain(), "ini_s", 1)
    em.wait("vector", "ini_s", 1)
    vts(arow[:], arow[:], -1.0, None, ALU.mult)
    vcp(rmi[:], rmode[:])
    vcp(rmode[:], rmi[:])
    vts(rmode[:], rmode[:], -0.5, 0.5, ALU.mult, ALU.add)
    em.inc(VE.drain(), "ini_v", 1)
    em.wait("tensor", "ini_v", 1)
    em.dwait("tensor", "sync", 0)
    em.dwait("tensor", "gpsimd", 0)
    for j in range(4):
        TE.matmul(PP[:, 1024 + 512 * j:1024 + 512 * (j + 1)], onesr[:],
                  drep0[0:1, 512 * j:512 * (j + 1)], start=True, stop=True)
    em.inc(TE.drain(), "ini_t", 1)
    em.wait("vector", "ini_t", 1)
    vcp(eyef[:], eyeb[:])
    for j in range(4):
        vcp(drep0[:, 512 * j:512 * (j + 1)], PP[:, 1024 + 512 * j:1024 + 512 * (j + 1)])
    em.barrier()

    def emit_ln(l, resid):
        for po in range(8):
            sl = po % 2
            if po >= 2:
                em.wait("gpsimd", "lnT%d" % l, po - 1)
                em.wait("gpsimd", "lnS%d" % l, po - 1)
            em.dma("gpsimd", slot(A1, sl), resid.ap()[128 * po:128 * (po + 1), :], ch=sl)
            em.dwait("scalar", "gpsimd", ch=sl)
            act(slot(A2, sl), slot(A1, sl), AF.Square)
            em.inc(SE.drain(), "lnS%d" % l, 1)
            em.wait("tensor", "lnS%d" % l, po + 1)
            for tc in range(NT):
                i = po * NT + tc
                reg = 512 * (i % 2)
                if i >= 2:
                    em.wait("tensor", "lnAe%d" % l, i - 1)
                TE.matmul(PP[0:1, reg:reg + 512], onesb[:, 0:1],
                          slot(A1, sl)[:, 512 * tc:512 * (tc + 1)],
                          start=True, stop=True)
                TE.matmul(PP[32:33, 1024 + reg:1024 + reg + 512], onesb[:, 1:2],
                          slot(A2, sl)[:, 512 * tc:512 * (tc + 1)],
                          start=True, stop=True)
                em.inc(TE.drain(), "lnAm%d" % l, 1)
                em.wait("vector", "lnAm%d" % l, em.cnt["lnAm%d" % l])
                if po == 0:
                    vcp(srow[0][:, 512 * tc:512 * (tc + 1)], PP[0:1, reg:reg + 512])
                    vcp(srow[1][:, 512 * tc:512 * (tc + 1)], PP[32:33, 1024 + reg:1024 + reg + 512])
                else:
                    vtt(srow[0][:, 512 * tc:512 * (tc + 1)], srow[0][:, 512 * tc:512 * (tc + 1)],
                        PP[0:1, reg:reg + 512], ALU.add)
                    vtt(srow[1][:, 512 * tc:512 * (tc + 1)], srow[1][:, 512 * tc:512 * (tc + 1)],
                        PP[32:33, 1024 + reg:1024 + reg + 512], ALU.add)
                em.inc(VE.drain(), "lnAe%d" % l, 1)
            em.inc(TE.drain(), "lnT%d" % l, 1)
        em.wait("vector", "lnT%d" % l, 8)
        vts(srow[0][:, 0:T], srow[0][:, 0:T], 1.0 / D, None, ALU.mult)
        vts(srow[1][:, 0:T], srow[1][:, 0:T], 1.0 / D, 1e-5, ALU.mult, ALU.add)
        for hf in range((T + 2047) // 2048):
            hw = min(2048, T - 2048 * hf)
            vtt(PP[0:1, 0:hw], srow[0][:, 2048 * hf:2048 * hf + hw],
                srow[0][:, 2048 * hf:2048 * hf + hw], ALU.mult)
            vtt(srow[1][:, 2048 * hf:2048 * hf + hw],
                srow[1][:, 2048 * hf:2048 * hf + hw], PP[0:1, 0:hw], ALU.subtract)
        em.inc(VE.drain(), "lnV%d" % l, 1)
        em.wait("scalar", "lnV%d" % l, 1)
        act(srow[1][:, 0:T], srow[1][:, 0:T], AF.Sqrt)
        em.inc(SE.drain(), "lnS%d" % l, 1)
        em.wait("vector", "lnS%d" % l, 9)
        VE.reciprocal(srow[2][:, 0:T], srow[1][:, 0:T])
        VE.drain()
        for hf in range((T + 2047) // 2048):
            hw = min(2048, T - 2048 * hf)
            vcp(PP[0:1, 0:hw], srow[2][:, 2048 * hf:2048 * hf + hw])
            vtt(srow[3][:, 2048 * hf:2048 * hf + hw],
                srow[0][:, 2048 * hf:2048 * hf + hw], PP[0:1, 0:hw], ALU.mult)
        vts(srow[3][:, 0:T], srow[3][:, 0:T], -1.0, None, ALU.mult)
        vcp(srowbA[0:1, 0:T], srow[2][:, 0:T])
        vcp(srowbB[0:1, 0:T], srow[3][:, 0:T])
        em.inc(VE.drain(), "lnV%d" % l, 1)
        em.wait("tensor", "lnV%d" % l, 2)
        em.wait("gpsimd", "lnS%d" % l, 8)
        em.wait("gpsimd", "lnT%d" % l, 8)
        for po in range(8):
            sl = po % 2
            if po >= 2:
                em.wait("gpsimd", "lnN%d" % l, (po - 1) * NT)
            em.dma("gpsimd", slot(A2, sl), resid.ap()[128 * po:128 * (po + 1), :], ch=sl)
            em.dwait("vector", "gpsimd", ch=sl)
            for tc in range(NT):
                i = po * NT + tc
                reg, reg2 = 512 * (i % 2), 1024 + 512 * (i % 2)
                if i >= 2:
                    em.wait("tensor", "lnN%d" % l, i - 1)
                TE.matmul(PP[:, reg:reg + 512], onesr[:],
                          srowbA[0:1, 512 * tc:512 * (tc + 1)], start=True, stop=True)
                TE.matmul(PP[:, reg2:reg2 + 512], ones33b[32:33, 0:128],
                          srowbB[0:1, 512 * tc:512 * (tc + 1)], start=True, stop=True)
                em.inc(TE.drain(), "lnM%d" % l, 1)
                em.wait("vector", "lnM%d" % l)
                if po >= 2 and tc == 0:
                    em.dwait("vector", "sync", ch=2 + (po % 2))
                vtt(slot(A3, po % 2)[:, 512 * tc:512 * (tc + 1)],
                    slot(A2, sl)[:, 512 * tc:512 * (tc + 1)], PP[:, reg:reg + 512], ALU.mult)
                vtt(slot(A3, po % 2)[:, 512 * tc:512 * (tc + 1)],
                    slot(A3, po % 2)[:, 512 * tc:512 * (tc + 1)], PP[:, reg2:reg2 + 512], ALU.add)
                em.inc(VE.drain(), "lnN%d" % l, 1)
            if l == 0:
                VE.tensor_reduce(gmaxc[:, 1:2], slot(A3, po % 2), AX.X,
                                 ALU.max, apply_absolute_value=True)
                VE.drain()
                vtt(gmaxc[:, 0:1], gmaxc[:, 0:1], gmaxc[:, 1:2], ALU.max)
            em.inc(VE.drain(), "lnNv%d" % l, 1)
            em.wait("sync", "lnNv%d" % l, po + 1)
            em.dma("sync", d_xn[l].ap()[128 * po:128 * (po + 1), :], slot(A3, po % 2),
                   ch=2 + (po % 2))
        em.barrier()

    def emit_winx_conv():
        for k in range(8):
            ar = A1 if k < 4 else A2
            em.dma("gpsimd", slot(ar, k % 4), d_xn[0].ap()[128 * k:128 * (k + 1), :], ch=k % 4)
        for c in range(4):
            em.dwait("tensor", "gpsimd", ch=c)
        for po in range(16):
            wsl = po % 2
            if po >= 2:
                em.wait("gpsimd", "wcT", (po - 1) * NT)
            for k in range(8):
                em.dma("gpsimd", WB[:, wsl * 1024 + 128 * k: wsl * 1024 + 128 * (k + 1)],
                       w0_in.ap()[128 * k:128 * (k + 1), 128 * po:128 * (po + 1)], ch=4 + wsl)
            em.dwait("tensor", "gpsimd", ch=4 + wsl)
            for tc in range(NT):
                i = po * NT + tc
                reg = 512 * (i % 4)
                if em.cnt.get("wcE", 0) >= 4:
                    em.wait("tensor", "wcE", em.cnt["wcE"] - 3)
                for k in range(8):
                    ar = A1 if k < 4 else A2
                    TE.matmul(PP[:, reg:reg + 512],
                              WB[:, wsl * 1024 + 128 * k: wsl * 1024 + 128 * (k + 1)],
                              slot(ar, k % 4)[:, 512 * tc:512 * (tc + 1)],
                              start=(k == 0), stop=(k == 7))
                em.inc(TE.drain(), "wcT", 1)
                em.wait("vector", "wcT", em.cnt["wcT"])
                vcp(slotn(A3, 2, 512 * tc, 512), PP[:, reg:reg + 512])
                em.inc(VE.drain(), "wcE", 1)
            em.inc(VE.drain(), "wcV", 1)
            wc = convw0[:, 4 * po:4 * po + 4]
            em.dwait("vector", "sync", ch=6)
            VE.memset(A3[:, 2 * SW:2 * SW + PAD], 0.0)
            VE.drain()
            VE.memset(A3[:, 2 * SW + PAD + T:3 * SW], 0.0)
            VE.drain()
            vts(slot(A3, 0), slotn(A3, 2, 0, T), wc[:, 3:4], None, ALU.mult)
            vstt(slot(A3, 1), slotn(A3, 2, -1, T), wc[:, 2:3], slot(A3, 0), ALU.mult, ALU.add)
            vstt(slot(A3, 0), slotn(A3, 2, -2, T), wc[:, 1:2], slot(A3, 1), ALU.mult, ALU.add)
            vstt(slot(A3, 1), slotn(A3, 2, -3, T), wc[:, 0:1], slot(A3, 0), ALU.mult, ALU.add)
            vts(slot(A3, 1), slot(A3, 1), convb0[:, po:po + 1], None, ALU.add)
            em.inc(VE.drain(), "wcVc", 1)
            em.wait("scalar", "wcVc", po + 1)
            act(slot(A3, 0), slot(A3, 1), AF.Silu)
            em.inc(SE.drain(), "wcS", 1)
            em.wait("sync", "wcS", po + 1)
            em.dma("sync", d_xc0.ap()[128 * po:128 * (po + 1), :], slot(A3, 0), ch=6)
            em.wait("tensor", "wcS", po + 1)
            for b4 in range(T // 512):
                if em.cnt.get("wcTV", 0):
                    em.wait("tensor", "wcTV", em.cnt["wcTV"])
                for j in range(4):
                    TE.transpose(PPb[:, 128 * j:128 * (j + 1)],
                                 slotn(A3, 0, 512 * b4 + 128 * j, 128), eyeb[:])
                em.inc(TE.drain(), "wcTT", 1)
                em.wait("vector", "wcTT", em.cnt["wcTT"])
                tb = trst if b4 % 2 == 0 else trstB
                em.dwait("vector", "sync", ch=7 if b4 % 2 == 0 else 3)
                for j in range(4):
                    vcp(tb[:, 128 * j:128 * (j + 1)],
                        PPb[:, 128 * j:128 * (j + 1)])
                em.inc(VE.drain(), "wcTV", 1)
                em.wait("sync", "wcTV", em.cnt["wcTV"])
                em.dma("sync",
                       d_xct0.ap()[512 * b4:512 * b4 + 512, 128 * po:128 * (po + 1)]
                       .rearrange("(j p) f -> p j f", p=128),
                       tb[:, 0:512].rearrange("p (j f) -> p j f", f=128),
                       ch=7 if b4 % 2 == 0 else 3)
        em.barrier()

    def emit_wx():
        GE.memset(maskrow[0:1, 0:T], 1.0)
        GE.drain()
        GE.memset(maskrow[0:1, 0:T].rearrange("a (c w) -> a c w", w=128)[:, :, 0:1], 0.0)
        em.inc(GE.drain(), "wx_g", 1)
        for e in ("vector",):
            em.wait(e, "wx_g", 1)
        for k in range(16):
            em.dma("gpsimd", WA[:, 132 * k:132 * k + 129],
                   w0_x.ap()[128 * k:128 * (k + 1), :], ch=k % 4)
        for c in range(4):
            em.dwait("tensor", "gpsimd", ch=c)
        for tc in range(NT):
            sl = tc % 2
            ar = A1 if sl == 0 else A2
            if tc >= 2:
                em.wait("sync", "xpT", tc - 1)
            for k in range(16):
                em.dma("sync", ar[:, 512 * k:512 * k + 512],
                       d_xc0.ap()[128 * k:128 * (k + 1), 512 * tc:512 * (tc + 1)], ch=sl)
            em.dwait("tensor", "sync", ch=sl)
            if tc >= 2:
                em.wait("tensor", "xpV", tc - 1)
                em.wait("tensor", "xpVsp", tc - 1)
            for k in range(16):
                TE.matmul(PP[0:128, 512 * (tc % 2):512 * (tc % 2) + 512],
                          WA[:, 132 * k:132 * k + 128], ar[:, 512 * k:512 * k + 512],
                          start=(k == 0), stop=(k == 15))
            for k in range(16):
                TE.matmul(PP[0:1, 1024 + 512 * (tc % 2):1024 + 512 * (tc % 2) + 512],
                          WA[:, 132 * k + 128:132 * k + 129], ar[:, 512 * k:512 * k + 512],
                          start=(k == 0), stop=(k == 15))
            em.inc(TE.drain(), "xpT", 1)
            em.wait("vector", "xpT", tc + 1)
            vcp(ssmBC[:, 512 * tc:512 * (tc + 1)],
                PP[0:128, 512 * (tc % 2):512 * (tc % 2) + 512])
            em.inc(VE.drain(), "xpV", 1)
            em.wait("scalar", "xpV", tc + 1)
            spsrc = PP[0:1, 1024 + 512 * (tc % 2):1024 + 512 * (tc % 2) + 512]
            sptmp = mflat[0:1, 0:512]
            em.wait("scalar", "xpVsp", em.cnt.get("xpVsp", 0))
            act(sptmp, spsrc, AF.Abs)
            act(sptmp, sptmp, AF.Exp, scale=-1.0)
            em.inc(SE.drain(), "xpSa", 1)
            em.wait("vector", "xpSa", em.cnt["xpSa"])
            vts(sptmp, sptmp, 1.0, None, ALU.add)
            em.inc(VE.drain(), "xpVa", 1)
            em.wait("scalar", "xpVa", em.cnt["xpVa"])
            act(sptmp, sptmp, AF.Ln)
            em.inc(SE.drain(), "xpS", 1)
            em.wait("vector", "xpS", tc + 1)
            vstt(dtrow[0:1, 512 * tc:512 * (tc + 1)], spsrc, 0.0, sptmp, ALU.max, ALU.add)
            em.inc(VE.drain(), "xpVsp", 1)
        em.wait("vector", "xpVsp", NT)
        for hf in range((T + 2047) // 2048):
            hw = min(2048, T - 2048 * hf)
            vcp(PP[0:1, 0:hw], maskrow[0:1, 2048 * hf:2048 * hf + hw])
            VE.tensor_tensor_scan(crow[0:1, 2048 * hf:2048 * hf + hw], PP[0:1, 0:hw],
                                  dtrow[0:1, 2048 * hf:2048 * hf + hw], 0.0,
                                  ALU.mult, ALU.add)
            VE.drain()
        VE.tensor_copy(erow[:], bass.AP(ROWS, 31, [[FW, 1], [32, NCH * 4]]))
        VE.drain()
        VE.tensor_copy(bass.AP(brow, 1, [[NCH * 4, 1], [4, NCH], [1, 3]]),
                       bass.AP(erow, 0, [[NCH * 4, 1], [4, NCH], [1, 3]]))
        VE.drain()
        em.inc(VE.drain(), "sp_v", 1)
        em.wait("tensor", "sp_v", 1)
        for j in range(NT):
            if j >= 2:
                em.wait("tensor", "sp_bv", j - 1)
            TE.matmul(PP[0:64, 512 * (j % 2):512 * (j % 2) + 512], ones33f[32:33, 0:64],
                      dtrow[0:1, 512 * j:512 * (j + 1)], start=True, stop=True)
            em.inc(TE.drain(), "sp_t", 1)
            em.wait("vector", "sp_t", j + 1)
            vtt(bdt0[:, 512 * j:512 * (j + 1)], ssmBC[64:128, 512 * j:512 * (j + 1)],
                PP[0:64, 512 * (j % 2):512 * (j % 2) + 512], ALU.mult)
            em.inc(VE.drain(), "sp_bv", 1)
        em.barrier()

    def emit_scan0():
        em.dma("sync", maskrow[0:1, 0:1024], cst_mask.ap(), ch=6)
        em.dwait("tensor", "sync", ch=6)
        for j in range(2):
            TE.matmul(PP[0:64, 512 * j:512 * (j + 1)], ones33f[64:65, 0:64],
                      maskrow[0:1, 512 * j:512 * (j + 1)], start=True, stop=True)
        em.inc(TE.drain(), "s0_mk", 1)
        em.wait("vector", "s0_mk", 1)
        vcp(mask64[:, 0:512], PP[0:64, 0:512])
        vcp(mask64[:, 512:1024], PP[0:64, 512:1024])
        em.inc(VE.drain(), "s0_mkv", 1)
        for e in ("tensor", "scalar", "gpsimd", "sync"):
            em.wait(e, "s0_mkv", 1)
        GE.memset(H0s[:], 0.0)
        GE.drain()
        GE.memset(WB[0:64, 0:2048], 0.0)
        em.inc(GE.drain(), "s0_g", 1)
        for e in ("tensor", "vector", "scalar", "sync"):
            em.wait(e, "s0_g", 1)
        for c in range(NCH):
            xsl = c % 2
            ar = A1 if xsl == 0 else A2
            if c >= 2:
                em.wait("sync", "s0_hb", c - 1)
                em.wait("sync", "s0_umm2", c - 1)
            em.dma("sync", ar[:, 0:2048], d_xct0.ap()[128 * c:128 * (c + 1), :], ch=xsl)
            XT = ar[:, 0:2048]
            cc = crow[0:1, 128 * c:128 * (c + 1)]
            Csl = ssmBC[0:64, 128 * c:128 * (c + 1)]
            Bdsl = bdt0[0:64, 128 * c:128 * (c + 1)]
            if c >= 1:
                em.wait("vector", "s0_amm", c)
            vtt(crowD[0:1, 0:128], cc,
                bass.AP(brow, 4 * c, [[NCH * 4, 1], [1, 4], [0, 32]]), ALU.subtract)
            vtt(crowD[0:1, 128:256],
                bass.AP(erow, 4 * c, [[NCH * 4, 1], [1, 4], [0, 32]]), cc, ALU.subtract)
            vtt(crowD[0:1, 384:512],
                bass.AP(erow, 4 * c + 3, [[NCH * 4, 1], [0, 128]]), cc, ALU.subtract)
            VE.tensor_copy(crowD[0:1, 512:515], bass.AP(brow, 4 * c + 1, [[NCH * 4, 1], [1, 3]]))
            VE.drain()
            VE.tensor_copy(crowD[0:1, 515:517], bass.AP(brow, 4 * c + 2, [[NCH * 4, 1], [1, 2]]))
            VE.drain()
            VE.tensor_copy(crowD[0:1, 517:518], bass.AP(brow, 4 * c + 3, [[NCH * 4, 1], [1, 1]]))
            VE.drain()
            VE.tensor_copy(crowD[0:1, 520:523], bass.AP(erow, 4 * c, [[NCH * 4, 1], [0, 3]]))
            VE.drain()
            VE.tensor_copy(crowD[0:1, 523:525], bass.AP(erow, 4 * c + 1, [[NCH * 4, 1], [0, 2]]))
            VE.drain()
            VE.tensor_copy(crowD[0:1, 525:526], bass.AP(erow, 4 * c + 2, [[NCH * 4, 1], [0, 1]]))
            VE.drain()
            vtt(crowD[0:1, 512:518], crowD[0:1, 512:518], crowD[0:1, 520:526], ALU.subtract)
            vts(growA[0:1, 0:64], arow[0:1, 0:64], erow[0:1, 4 * c + 3:4 * c + 4],
                None, ALU.mult)
            em.inc(VE.drain(), "s0_rows", 1)
            em.wait("tensor", "s0_rows", c + 1)
            if c >= 1:
                em.wait("tensor", "s0_mf", 4 * c)
            TE.matmul(PP[0:64, 3072:3200], arow[0:1, 0:64], crowD[0:1, 0:128],
                      start=True, stop=True)
            TE.matmul(PP[0:64, 3200:3328], arow[0:1, 0:64], crowD[0:1, 128:256],
                      start=True, stop=True)
            TE.matmul(PP[0:64, 3328:3456], arow[0:1, 0:64], cc,
                      start=True, stop=True)
            TE.matmul(PP[0:128, 3456:3520], crowD[0:1, 384:512], arow[0:1, 0:64],
                      start=True, stop=True)
            TE.matmul(PP[0:64, 3520:3526], arow[0:1, 0:64], crowD[0:1, 512:518],
                      start=True, stop=True)
            TE.matmul(PP[0:64, 2560:2688], arow[0:1, 0:64], cc, start=True, stop=True)
            TE.transpose(PPb[0:128, 512:576], Bdsl, eyeb[0:64, 0:64])
            em.inc(TE.drain(), "s0_amm", 1)
            em.wait("scalar", "s0_amm", c + 1)
            act(mid6[:, 0:6], PP[0:64, 3520:3526], AF.Exp)
            act(accs[:, 0:128], PP[0:64, 2560:2688], AF.Copy)
            act(Cbt[:], PP[0:64, 3072:3200], AF.Exp)
            act(Bbt[:], PP[0:64, 3200:3328], AF.Exp)
            act(Cct[:], PP[0:64, 3328:3456], AF.Exp)
            act(Bht[:], PP[0:128, 3456:3520], AF.Exp)
            act(growB[0:1, 0:64], growA[0:1, 0:64], AF.Exp)
            em.inc(SE.drain(), "s0_exp", 1)
            em.wait("vector", "s0_exp", c + 1)
            vtt(Cbt[:], Cbt[:], Csl, ALU.mult)
            vtt(Bbt[:], Bbt[:], Bdsl, ALU.mult)
            vtt(Cct[:], Cct[:], Csl, ALU.mult)
            vtt(Bht[:], Bht[:], PPb[0:128, 512:576], ALU.mult)
            vtt(FY[0:64, 0:4096],
                bass.AP(accs, 0, [[128, 64], [32, 4], [0, 32], [1, 32]]),
                bass.AP(accs, 0, [[128, 64], [32, 4], [1, 32], [0, 32]]),
                ALU.subtract)
            vtt(FY[0:64, 0:4096], FY[0:64, 0:4096],
                bass.AP(A3, EXT, [[A3W, 64], [0, 4], [1, 1024]]), ALU.add)
            em.inc(VE.drain(), "s0_fac", 1)
            em.wait("scalar", "s0_fac", c + 1)
            act(ediag[:, 0:4096], FY[0:64, 0:4096], AF.Exp)
            em.inc(SE.drain(), "s0_ed", 1)
            em.wait("vector", "s0_ed", c + 1)
            vtt(ediag[:, 0:4096], ediag[:, 0:4096],
                bass.AP(WA, 4352 + 128 * c, [[8448, 64], [32, 4], [1, 32], [0, 32]]),
                ALU.mult)
            vtt(ediag[:, 0:4096], ediag[:, 0:4096],
                bass.AP(ssmBC, 128 * c, [[T, 64], [32, 4], [0, 32], [1, 32]]),
                ALU.mult)
            for j in range(3):
                vtt(CbM[:, 32 * (j + 1):128], Cbt[:, 32 * (j + 1):128],
                    bass.AP(mid6, [0, 3, 5][j], [[8, 64], [1, 3 - j], [0, 32]]), ALU.mult)
            em.inc(VE.drain(), "s0_pd", 1)
            em.wait("tensor", "s0_pd", c + 1)
            for j in range(3):
                TE.matmul(PP[32 * j:32 * j + 32, 2048 + 32 * (j + 1):2048 + 128],
                          Bbt[:, 32 * j:32 * j + 32], CbM[:, 32 * (j + 1):128],
                          start=True, stop=True, tile_position=(0, 32 * j))
            em.inc(TE.drain(), "s0_offd", 1)
            for k in range(4):
                if k >= 1:
                    em.wait("tensor", "s0_mf", em.cnt["s0_mf"])
                TE.matmul(PP[0:1, 2560:3072], onesb[0:64, 2:3],
                          ediag[:, 1024 * k:1024 * k + 512], start=True, stop=True)
                TE.matmul(PP[0:1, 3072:3584], onesb[0:64, 3:4],
                          ediag[:, 1024 * k + 512:1024 * (k + 1)], start=True, stop=True)
                em.inc(TE.drain(), "s0_dr", 1)
                em.wait("scalar", "s0_dr", em.cnt["s0_dr"])
                act(mflat[0:1, 1024 * k:1024 * k + 512], PP[0:1, 2560:3072], AF.Copy)
                act(mflat[0:1, 1024 * k + 512:1024 * (k + 1)], PP[0:1, 3072:3584], AF.Copy)
                em.inc(SE.drain(), "s0_mf", 1)
            em.wait("vector", "s0_offd", c + 1)
            em.wait("vector", "s0_mf", em.cnt["s0_mf"])
            VE.memset(Mfull[:], 0.0)
            VE.drain()
            for j in range(3):
                vcp(Mfull[32 * j:32 * j + 32, 32 * (j + 1):128],
                    PP[32 * j:32 * j + 32, 2048 + 32 * (j + 1):2048 + 128])
            em.inc(VE.drain(), "s0_mz", 1)
            em.wait("gpsimd", "s0_mz", c + 1)
            for k in range(4):
                em.dma("gpsimd", Mfull[32 * k:32 * k + 32, 32 * k:32 * k + 32],
                       mflat[0:1, 1024 * k:1024 * (k + 1)], ch=2 + (k % 2))
            em.dwait("tensor", "gpsimd", ch=2)
            em.dwait("tensor", "gpsimd", ch=3)
            em.dwait("tensor", "sync", ch=xsl)
            if c >= 1:
                em.wait("tensor", "s0_yev", c)
            for nb in range(4):
                TE.matmul(PP[:, 512 * nb:512 * (nb + 1)], Mfull[:],
                          XT[:, 512 * nb:512 * (nb + 1)], start=True, stop=False)
                TE.matmul(PP[:, 512 * nb:512 * (nb + 1)], Cct[:],
                          WB[0:64, 512 * nb:512 * (nb + 1)], start=False, stop=True)
            em.inc(TE.drain(), "s0_ymm", 1)
            TE.matmul(PP[0:64, 2544:2545], growB[0:1, 0:64], onesf[0:1, 0:1],
                      start=True, stop=True)
            em.wait("tensor", "s0_mf", em.cnt["s0_mf"])
            TE.matmul(PP[0:64, 2560:3072], Bht[:], XT[:, 0:512], start=True, stop=True)
            TE.matmul(PP[0:64, 3072:3584], Bht[:], XT[:, 512:1024], start=True, stop=True)
            em.inc(TE.drain(), "s0_umm", 1)
            em.wait("vector", "s0_ymm", c + 1)
            vtt(y2s[:, 0:2048], XT, drep0[:, 0:2048], ALU.mult)
            for nb in range(4):
                vtt(y2s[:, 512 * nb:512 * (nb + 1)], y2s[:, 512 * nb:512 * (nb + 1)],
                    PP[:, 512 * nb:512 * (nb + 1)], ALU.add)
            em.inc(VE.drain(), "s0_yev", 1)
            em.wait("vector", "s0_umm", c + 1)
            vstt(H0s[:, 0:512], H0s[:, 0:512], PP[0:64, 2544:2545], PP[0:64, 2560:3072],
                 ALU.mult, ALU.add)
            vstt(H0s[:, 512:1024], H0s[:, 512:1024], PP[0:64, 2544:2545],
                 PP[0:64, 3072:3584], ALU.mult, ALU.add)
            em.inc(VE.drain(), "s0_hup", 1)
            em.wait("tensor", "s0_hup", c + 1)
            TE.matmul(PP[0:64, 2560:3072], Bht[:], XT[:, 1024:1536], start=True, stop=True)
            TE.matmul(PP[0:64, 3072:3584], Bht[:], XT[:, 1536:2048], start=True, stop=True)
            em.inc(TE.drain(), "s0_umm2", 1)
            em.wait("vector", "s0_umm2", c + 1)
            vstt(H0s[:, 1024:1536], H0s[:, 1024:1536], PP[0:64, 2544:2545],
                 PP[0:64, 2560:3072], ALU.mult, ALU.add)
            vstt(H0s[:, 1536:2048], H0s[:, 1536:2048], PP[0:64, 2544:2545],
                 PP[0:64, 3072:3584], ALU.mult, ALU.add)
            vcp(WB[0:64, 0:2048], H0s[:])
            em.inc(VE.drain(), "s0_hb", 1)
            em.wait("tensor", "s0_hb", c + 1)
            for j4 in range(4):
                if em.cnt.get("s0_ytv", 0):
                    em.wait("tensor", "s0_ytv", em.cnt["s0_ytv"])
                for j in range(4):
                    TE.transpose(PPb[:, 128 * j:128 * (j + 1)],
                                 y2s[:, 512 * j4 + 128 * j:512 * j4 + 128 * (j + 1)], eyeb[:])
                em.inc(TE.drain(), "s0_ytt", 1)
                em.wait("vector", "s0_ytt", em.cnt["s0_ytt"])
                if not (c == 0 and j4 == 0):
                    em.dwait("vector", "sync", ch=4 + (j4 % 2))
                tb = trst if j4 % 2 == 0 else trstB
                for j in range(4):
                    vcp(tb[:, 128 * j:128 * (j + 1)],
                        PPb[:, 128 * j:128 * (j + 1)])
                em.inc(VE.drain(), "s0_ytv", 1)
                em.wait("sync", "s0_ytv", em.cnt["s0_ytv"])
                em.dma("sync",
                       d_y2f0.ap()[512 * j4:512 * j4 + 512, 128 * c:128 * (c + 1)]
                       .rearrange("(j p) t -> p j t", p=128),
                       tb[:, 0:512].rearrange("p (j t) -> p j t", t=128), ch=4 + (j4 % 2))
        em.barrier()

    def emit_scale0():
        fold_col_max(gmaxc[:, 0:1], 16)
        allreduce_max(16, 17)
        act(scal[0:1, 18:19], scal[0:1, 8:9], AF.Sigmoid)
        em.inc(SE.drain(), "sc0_s", 1)
        em.wait("vector", "sc0_s", 1)
        vtt(scal[0:1, 0:1], scal[0:1, 18:19], scal[0:1, 17:18], ALU.mult)
        vts(scal[0:1, 0:1], scal[0:1, 0:1], 1.0 / 127.0, None, ALU.mult)
        VE.reciprocal(scal[0:1, 1:2], scal[0:1, 0:1])
        VE.drain()
        em.inc(VE.drain(), "sc0_v", 1)
        em.wait("tensor", "sc0_v", 1)
        col_from_scalar(0, scal[0:1, 0:1])
        col_from_scalar(1, scal[0:1, 1:2])
        em.barrier()

    def emit_quant_wide(src, dst, R, name, lo, hi, mode, inv_idx=0, post_idx=0):
        nR = R // 128
        HB = min(512, T)
        tiles = [(h, po) for h in range(T // HB) for po in range(nR)]
        nt = len(tiles)
        dsts = [y2s[:, 0:HB], y2s[:, 512:512 + HB]]

        def pre(idx):
            h, po = tiles[idx]
            g = idx % 2
            yv = ytl[g][:, 0:HB]
            sg = FA[:, 2048 * g:2048 * g + HB]
            if mode == "row" and po == 0:
                em.wait("tensor", "%s_q" % name, idx)
                for b in range(HB // 512):
                    TE.matmul(PP[:, 1024 + 512 * b:1536 + 512 * b], onesf[0:1, 0:128],
                              qscr[0:1, HB * h + 512 * b:HB * h + 512 * (b + 1)],
                              start=True, stop=True)
                    TE.matmul(PP[:, 2048 + 512 * b:2560 + 512 * b], ones33f[32:33, 0:128],
                              qinr[0:1, HB * h + 512 * b:HB * h + 512 * (b + 1)],
                              start=True, stop=True)
                em.inc(TE.drain(), "%s_r" % name, 1)
            if idx >= 2:
                em.wait("gpsimd", "%s_q" % name, idx - 1)
            em.dma("gpsimd", yv, src.ap()[128 * po:128 * (po + 1), HB * h:HB * (h + 1)],
                   ch=g)
            em.dwait("vector", "gpsimd", ch=g)
            if mode == "row":
                em.wait("vector", "%s_r" % name, h + 1)
                for b in range(HB // 512):
                    vtt(yv[:, 512 * b:512 * (b + 1)], yv[:, 512 * b:512 * (b + 1)],
                        PP[:, 1024 + 512 * b:1536 + 512 * b], ALU.mult)
            else:
                vts(yv, yv, colsc[:, inv_idx:inv_idx + 1], None, ALU.mult)
            em.inc(VE.drain(), "%s_v" % name, 1)
            em.wait("scalar", "%s_v" % name, em.cnt["%s_v" % name])
            act(sg, yv, AF.Sign)
            em.inc(SE.drain(), "%s_s" % name, 1)

        def post(idx):
            h, po = tiles[idx]
            g = idx % 2
            yv = ytl[g][:, 0:HB]
            sg = FA[:, 2048 * g:2048 * g + HB]
            it = itl1[:, 0:HB]
            em.wait("vector", "%s_s" % name, idx + 1)
            vstt(yv, sg, rmode[:, 0:1], yv, ALU.mult, ALU.add)
            vcp(it, yv)
            vts(it, it, float(lo), float(hi), ALU.max, ALU.min)
            em.dwait("vector", "sync", ch=2 + g)
            if mode == "row":
                for b in range(HB // 512):
                    vtt(dsts[g][:, 512 * b:512 * (b + 1)], it[:, 512 * b:512 * (b + 1)],
                        PP[:, 2048 + 512 * b:2560 + 512 * b], ALU.mult)
            else:
                vts(dsts[g], it, colsc[:, post_idx:post_idx + 1], None, ALU.mult)
            em.inc(VE.drain(), "%s_q" % name, 1)
            em.wait("sync", "%s_q" % name, idx + 1)
            em.dma("sync", dst.ap()[128 * po:128 * (po + 1), HB * h:HB * (h + 1)],
                   dsts[g], ch=2 + g)

        PIPE = True
        if PIPE:
            emitted = 0
            for i in range(nt):
                while emitted <= i + 1 and emitted < nt:
                    j = emitted
                    if mode == "row" and tiles[j][1] == 0 and j > 0 and j > i:
                        break
                    pre(j)
                    emitted += 1
                post(i)
        else:
            for i in range(nt):
                pre(i)
                post(i)
        em.barrier()

    def emit_quant_global(src, dst, R, inv_idx, post_idx, lo, hi, name):
        nR = R // 128
        for po in range(nR):
            for tc in range(NT):
                par = (po * NT + tc) % 2
                if po * NT + tc >= 2:
                    em.dwait("gpsimd", "sync", ch=par)
                em.dma("gpsimd", ytl[par][:, 0:512],
                       src.ap()[128 * po:128 * (po + 1), 512 * tc:512 * (tc + 1)], ch=par)
                em.dwait("vector", "gpsimd", ch=par)
                vts(ytl[par][:, 0:512], ytl[par][:, 0:512], colsc[:, inv_idx:inv_idx + 1],
                    None, ALU.mult)
                qround(y2s[:, 512 * par:512 * (par + 1)],
                       par, lo, hi, post_col=colsc[:, post_idx:post_idx + 1])
                em.inc(VE.drain(), "%s_v" % name, 1)
                em.wait("sync", "%s_v" % name, em.cnt["%s_v" % name])
                em.dma("sync", dst.ap()[128 * po:128 * (po + 1), 512 * tc:512 * (tc + 1)],
                       y2s[:, 512 * par:512 * (par + 1)], ch=par)
        em.barrier()

    d_xq0 = d_y1

    def emit_xq0():
        emit_quant_global(d_xn[0], d_xq0, D, 1, 0, -128.0, 127.0, "xq0")

    def emit_gate_wout0():
        d_g0 = d_xc0
        for k in range(8):
            ar = A1 if k < 4 else A2
            em.dma("gpsimd", slot(ar, k % 4), d_xn[0].ap()[128 * k:128 * (k + 1), :],
                   ch=4 + k % 2)
        em.dwait("tensor", "gpsimd", ch=4)
        em.dwait("tensor", "gpsimd", ch=5)
        for po in range(16):
            wsl = po % 2
            if po >= 2:
                em.wait("gpsimd", "gaT", (po - 1) * NT)
            for k in range(8):
                em.dma("gpsimd", WB[:, wsl * 1024 + 128 * k:wsl * 1024 + 128 * (k + 1)],
                       w0_in.ap()[128 * k:128 * (k + 1), DI + 128 * po:DI + 128 * (po + 1)],
                       ch=6 + wsl)
            em.dwait("tensor", "gpsimd", ch=6 + wsl)
            for tc in range(NT):
                i = po * NT + tc
                reg = 512 * (i % 4)
                if i >= 4:
                    em.wait("tensor", "gaS", i - 3)
                for k in range(8):
                    ar = A1 if k < 4 else A2
                    TE.matmul(PP[:, reg:reg + 512],
                              WB[:, wsl * 1024 + 128 * k:wsl * 1024 + 128 * (k + 1)],
                              slot(ar, k % 4)[:, 512 * tc:512 * (tc + 1)],
                              start=(k == 0), stop=(k == 7))
                em.inc(TE.drain(), "gaT", 1)
                if i >= 4:
                    em.wait("gpsimd", "gaV", i - 3)
                em.dma("gpsimd", y2s[:, 512 * (i % 4):512 * (i % 4) + 512],
                       d_y2f0.ap()[128 * po:128 * (po + 1), 512 * tc:512 * (tc + 1)], ch=i % 4)
                em.wait("scalar", "gaT", i + 1)
                em.dwait("scalar", "sync", ch=i % 4)
                act(A3[:, 512 * (i % 8):512 * (i % 8) + 512], PP[:, reg:reg + 512], AF.Silu)
                em.inc(SE.drain(), "gaS", 1)
                em.wait("vector", "gaS", i + 1)
                em.dwait("vector", "gpsimd", ch=i % 4)
                vtt(A3[:, 512 * (i % 8):512 * (i % 8) + 512],
                    A3[:, 512 * (i % 8):512 * (i % 8) + 512],
                    y2s[:, 512 * (i % 4):512 * (i % 4) + 512], ALU.mult)
                em.inc(VE.drain(), "gaV", 1)
                em.wait("sync", "gaV", i + 1)
                em.dma("sync", d_g0.ap()[128 * po:128 * (po + 1), 512 * tc:512 * (tc + 1)],
                       A3[:, 512 * (i % 8):512 * (i % 8) + 512], ch=i % 4)
        em.wait("gpsimd", "gaT", 16 * NT)
        for k in range(16):
            em.dma("gpsimd", A2[:, 1024 * k:1024 * (k + 1)],
                   w0_out.ap()[128 * k:128 * (k + 1), :], ch=4 + k % 2)
        em.dwait("tensor", "gpsimd", ch=4)
        em.dwait("tensor", "gpsimd", ch=5)
        em.wait("gpsimd", "gaT", 16 * NT)
        em.wait("tensor", "gaS", 16 * NT)
        em.wait("scalar", "gaV", 16 * NT)
        for c in range(4):
            em.dwait("gpsimd", "sync", ch=c)
        for tc in range(NT):
            if tc > 0:
                em.wait("gpsimd", "gbT", max(0, (tc - 1) * 8))
            for k in range(16):
                em.dma("gpsimd", A1[:, 8192 * (tc % 2) + 512 * k:8192 * (tc % 2) + 512 * (k + 1)],
                       d_g0.ap()[128 * k:128 * (k + 1), 512 * tc:512 * (tc + 1)], ch=k % 4)
            for c in range(4):
                em.dwait("tensor", "gpsimd", ch=c)
            for po2 in range(8):
                i2 = tc * 8 + po2
                reg = 512 * (i2 % 4)
                if i2 >= 4:
                    em.wait("tensor", "gbS", i2 - 3)
                for k in range(16):
                    TE.matmul(PP[:, reg:reg + 512],
                              A2[:, 1024 * k + 128 * po2:1024 * k + 128 * (po2 + 1)],
                              A1[:, 8192 * (tc % 2) + 512 * k:8192 * (tc % 2) + 512 * (k + 1)],
                              start=(k == 0), stop=(k == 15))
                em.inc(TE.drain(), "gbT", 1)
                em.wait("scalar", "gbT", i2 + 1)
                em.dwait("scalar", "sync", ch=i2 % 4)
                act(y2s[:, 512 * (i2 % 4):512 * (i2 % 4) + 512], PP[:, reg:reg + 512], AF.Copy)
                em.inc(SE.drain(), "gbS", 1)
                em.wait("sync", "gbS", i2 + 1)
                em.dma("sync", d_s[0].ap()[128 * po2:128 * (po2 + 1), 512 * tc:512 * (tc + 1)],
                       y2s[:, 512 * (i2 % 4):512 * (i2 % 4) + 512], ch=i2 % 4)
        em.barrier()

    def emit_wquant_row(w_dram, dst_dram, K, PO, name):
        nK = K // 128
        nI = (PO + 127) // 128
        for k in range(nK):
            ch = k % 2
            buf = FY if k == 0 else FA
            em.dma("sync", buf[:, 0:PO], w_dram.ap()[128 * k:128 * (k + 1), :], ch=ch)
            em.dwait("vector", "sync", ch=ch)
            em.dwait("scalar", "sync", ch=ch)
            em.wait("scalar", "%s_f" % name, em.cnt.get("%s_f" % name, 0))
            if k == 0:
                act(FY[:, 0:PO], FY[:, 0:PO], AF.Abs)
            else:
                act(FA[:, 0:PO], FA[:, 0:PO], AF.Abs)
            em.inc(SE.drain(), "%s_fa" % name, 1)
            em.wait("vector", "%s_fa" % name, em.cnt["%s_fa" % name])
            if k > 0:
                vtt(FY[:, 0:PO], FY[:, 0:PO], FA[:, 0:PO], ALU.max)
            em.inc(VE.drain(), "%s_f" % name, 1)
            if k + 1 < nK:
                em.wait("sync", "%s_f" % name, k + 1)
        em.wait("tensor", "%s_f" % name, nK)
        for i in range(nI):
            w = min(128, PO - 128 * i)
            if i:
                em.wait("tensor", "%s_c" % name, i)
                em.wait("tensor", "%s_r" % name, i)
            TE.transpose(PP[0:w, 2048:2176], FY[:, 128 * i:128 * i + w], eyef[:])
            em.inc(TE.drain(), "%s_t" % name, 1)
            em.wait("vector", "%s_t" % name, em.cnt["%s_t" % name])
            VE.tensor_reduce(colsc[0:w, 8:9], PP[0:w, 2048:2176], AX.X, ALU.max)
            VE.drain()
            vts(colsc[0:w, 8:9], colsc[0:w, 8:9], 1.0 / 127.0, None, ALU.mult)
            em.inc(VE.drain(), "%s_c" % name, 1)
            em.wait("tensor", "%s_c" % name, em.cnt["%s_c" % name])
            TE.transpose(PP[0:1, 2176:2304], colsc[0:128, 8:9], eyef[:])
            em.inc(TE.drain(), "%s_t2" % name, 1)
            em.wait("scalar", "%s_t2" % name, em.cnt["%s_t2" % name])
            act(qscr[0:1, 128 * i:128 * i + w], PP[0:1, 2176:2176 + w], AF.Copy)
            em.inc(SE.drain(), "%s_r" % name, 1)
        em.wait("vector", "%s_r" % name, nI)
        VE.reciprocal(qinr[0:1, 0:PO], qscr[0:1, 0:PO])
        VE.drain()
        em.inc(VE.drain(), "%s_rr" % name, 1)
        em.wait("tensor", "%s_rr" % name, 1)
        em.barrier()
        nB = (PO + 511) // 512
        tiles = [(b, k) for b in range(nB) for k in range(nK)]
        nt = len(tiles)
        dsts = [y2s[:, 0:512], A3[:, EXT:EXT + 512]]

        def wq_pre(idx):
            b, k = tiles[idx]
            w = min(512, PO - 512 * b)
            g = idx % 2
            yv = ytl[g][:, 0:w]
            sg = FA[:, 2048 * g:2048 * g + w]
            if k == 0:
                em.wait("tensor", "%s_q" % name, idx)
                TE.matmul(PP[:, 2048:2048 + w], ones33f[32:33, 0:128],
                          qinr[0:1, 512 * b:512 * b + w], start=True, stop=True)
                TE.matmul(PP[:, 3072:3072 + w], onesf[0:1, 0:128],
                          qscr[0:1, 512 * b:512 * b + w], start=True, stop=True)
                em.inc(TE.drain(), "%s_b" % name, 1)
            if em.cnt.get("%s_q" % name, 0) >= 1:
                em.wait("sync", "%s_q" % name, em.cnt["%s_q" % name])
            em.dma("sync", yv, w_dram.ap()[128 * k:128 * (k + 1), 512 * b:512 * b + w],
                   ch=g)
            em.dwait("vector", "sync", ch=g)
            em.wait("vector", "%s_b" % name, (b + 1))
            vtt(yv, yv, PP[:, 2048:2048 + w], ALU.mult)
            em.inc(VE.drain(), "%s_v" % name, 1)
            em.wait("scalar", "%s_v" % name, em.cnt["%s_v" % name])
            act(sg, yv, AF.Sign)
            em.inc(SE.drain(), "%s_s" % name, 1)

        def wq_post(idx):
            b, k = tiles[idx]
            w = min(512, PO - 512 * b)
            g = idx % 2
            yv = ytl[g][:, 0:w]
            sg = FA[:, 2048 * g:2048 * g + w]
            it = itl1[:, 0:w]
            em.wait("vector", "%s_s" % name, idx + 1)
            vstt(yv, sg, rmode[:, 0:1], yv, ALU.mult, ALU.add)
            vcp(it, yv)
            vts(it, it, -128.0, 127.0, ALU.max, ALU.min)
            em.dwait("vector", "sync", ch=2 + g)
            vtt(dsts[g][:, 0:w], it, PP[:, 3072:3072 + w], ALU.mult)
            em.inc(VE.drain(), "%s_q" % name, 1)
            em.wait("sync", "%s_q" % name, em.cnt["%s_q" % name])
            em.dma("sync", wtile_store_ap(dst_dram, nK, 4 * b, w // 128, k),
                   dsts[g][:, 0:w].rearrange("p (j f) -> p j f", f=128),
                   ch=2 + g)

        emitted = 0
        for i in range(nt):
            while emitted <= i + 1 and emitted < nt:
                j = emitted
                if tiles[j][1] == 0 and j > 0 and j > i:
                    break
                wq_pre(j)
                emitted += 1
            wq_post(i)
        em.barrier()

    def emit_ffn0_gateup():
        for k in range(8):
            ar = A1 if k < 4 else A2
            em.dma("gpsimd", slot(ar, k % 4), d_xq0.ap()[128 * k:128 * (k + 1), :], ch=k % 4)
        for c in range(4):
            em.dwait("tensor", "gpsimd", ch=c)
        GE.memset(gmaxc[:, 2:3], 0.0)
        em.inc(GE.drain(), "fg_g", 1)
        em.wait("vector", "fg_g", 1)
        for po in range(22):
            wsl = po % 2
            if po >= 2:
                em.wait("sync", "fg_T", po - 1)
            em.dma("sync", WB[:, wsl * 2048:wsl * 2048 + 1024],
                   wtile_load_ap(d_wgq, 8, po), ch=4 + wsl)
            em.dma("sync", WB[:, wsl * 2048 + 1024:wsl * 2048 + 2048],
                   wtile_load_ap(d_wuq, 8, po), ch=4 + wsl)
            em.dwait("tensor", "sync", ch=4 + wsl)
            for tc in range(NT):
                i = po * NT + tc
                reg = 1024 * (i % 2)
                if i >= 2:
                    em.wait("tensor", "fg_V", i - 1)
                for k in range(8):
                    ar = A1 if k < 4 else A2
                    TE.matmul(PP[:, reg:reg + 512],
                              WB[:, wsl * 2048 + 128 * k:wsl * 2048 + 128 * (k + 1)],
                              slot(ar, k % 4)[:, 512 * tc:512 * (tc + 1)],
                              start=(k == 0), stop=(k == 7))
                    TE.matmul(PP[:, reg + 512:reg + 1024],
                              WB[:, wsl * 2048 + 1024 + 128 * k:wsl * 2048 + 1024 + 128 * (k + 1)],
                              slot(ar, k % 4)[:, 512 * tc:512 * (tc + 1)],
                              start=(k == 0), stop=(k == 7))
                em.inc(TE.drain(), "fg_T", 1)
                em.wait("scalar", "fg_T", em.cnt["fg_T"])
                em.wait("scalar", "fg_V", em.cnt.get("fg_V", 0))
                act(trst[:, 0:512], PP[:, reg:reg + 512], AF.Silu)
                em.inc(SE.drain(), "fg_S", 1)
                em.wait("vector", "fg_S", em.cnt["fg_S"])
                em.dwait("vector", "sync", ch=i % 2)
                vtt(y2s[:, 512 * (i % 2):512 * (i % 2) + 512], trst[:, 0:512],
                    PP[:, reg + 512:reg + 1024], ALU.mult)
                em.inc(VE.drain(), "fg_V", 1)
                VE.tensor_reduce(gmaxc[:, 3:4], y2s[:, 512 * (i % 2):512 * (i % 2) + 512],
                                 AX.X, ALU.max, apply_absolute_value=True)
                VE.drain()
                vtt(gmaxc[:, 2:3], gmaxc[:, 2:3], gmaxc[:, 3:4], ALU.max)
                em.inc(VE.drain(), "fg_Vm", 1)
                em.wait("sync", "fg_Vm", em.cnt["fg_Vm"])
                em.dma("sync", d_h0.ap()[128 * po:128 * (po + 1), 512 * tc:512 * (tc + 1)],
                       y2s[:, 512 * (i % 2):512 * (i % 2) + 512], ch=i % 2)
        em.barrier()

    def emit_scaleh():
        fold_col_max(gmaxc[:, 2:3], 20)
        allreduce_max(20, 21)
        act(scal[0:1, 22:23], scal[0:1, 9:10], AF.Sigmoid)
        em.inc(SE.drain(), "sch_s", 1)
        em.wait("vector", "sch_s", 1)
        vtt(scal[0:1, 2:3], scal[0:1, 22:23], scal[0:1, 21:22], ALU.mult)
        vts(scal[0:1, 2:3], scal[0:1, 2:3], 1.0 / 127.0, None, ALU.mult)
        VE.reciprocal(scal[0:1, 3:4], scal[0:1, 2:3])
        VE.drain()
        em.inc(VE.drain(), "sch_v", 1)
        em.wait("tensor", "sch_v", 1)
        col_from_scalar(2, scal[0:1, 2:3])
        col_from_scalar(3, scal[0:1, 3:4])
        em.barrier()

    def emit_ffn0_down():
        groups = [list(range(g, min(g + 2, NT))) for g in range(0, NT, 2)]
        for gi, tcs in enumerate(groups):
            baseT = em.cnt.get("fd_T", 0)
            baseQ = em.cnt.get("fd_qq", 0)
            for ti, tc in enumerate(tcs):
                buf = A1 if ti == 0 else A2

                def fd_pre(k, tc=tc, buf=buf, gi=gi, baseT=baseT):
                    par = k % 2
                    if em.cnt.get("fd_q", 0) >= 1:
                        em.wait("gpsimd", "fd_q", em.cnt["fd_q"])
                    if gi >= 1 and k < 2:
                        em.wait("gpsimd", "fd_T", baseT)
                    em.dma("gpsimd", ytl[par][:, 0:512],
                           d_h0.ap()[128 * k:128 * (k + 1), 512 * tc:512 * (tc + 1)],
                           ch=par)
                    em.dwait("vector", "gpsimd", ch=par)
                    if gi >= 1 and k == 0:
                        em.wait("vector", "fd_T", baseT)
                    vts(ytl[par][:, 0:512], ytl[par][:, 0:512], colsc[:, 3:4], None,
                        ALU.mult)
                    em.inc(VE.drain(), "fd_v", 1)
                    em.wait("scalar", "fd_v", em.cnt["fd_v"])
                    act(FA[:, 2048 * par:2048 * par + 512], ytl[par][:, 0:512], AF.Sign)
                    em.inc(SE.drain(), "fd_s", 1)

                def fd_post(k, tc=tc, buf=buf):
                    par = k % 2
                    yv = ytl[par][:, 0:512]
                    sg = FA[:, 2048 * par:2048 * par + 512]
                    it = itl1[:, 0:512]
                    em.wait("vector", "fd_s", k + 1 + tc * 22)
                    vstt(yv, sg, rmode[:, 0:1], yv, ALU.mult, ALU.add)
                    vcp(it, yv)
                    vts(it, it, -128.0, 127.0, ALU.max, ALU.min)
                    vts(buf[:, 512 * k:512 * (k + 1)], it, colsc[:, 2:3], None, ALU.mult)
                    em.inc(VE.drain(), "fd_q", 1)

                fd_pre(0)
                fd_pre(1)
                for k in range(22):
                    fd_post(k)
                    if k + 2 < 22:
                        fd_pre(k + 2)
                em.inc(VE.drain(), "fd_qq", 1)
            nl = len(tcs)
            for po2 in range(8):
                wsl = po2 % 2
                tgt = baseT + nl * (po2 - 1)
                if tgt > 0:
                    em.wait("sync", "fd_T", tgt)
                em.dma("sync", WB[:, wsl * 2816:wsl * 2816 + 2816],
                       wtile_load_ap(d_wdq, 22, po2), ch=4 + po2 % 4)
                em.dwait("tensor", "sync", ch=4 + po2 % 4)
                for ti, tc in enumerate(tcs):
                    buf = A1 if ti == 0 else A2
                    i3 = baseT + po2 * nl + ti
                    reg = 512 * (i3 % 4)
                    if i3 >= 4:
                        em.wait("tensor", "fd_S", i3 - 3)
                    if po2 == 0:
                        em.wait("tensor", "fd_qq", baseQ + ti + 1)
                    for k in range(22):
                        TE.matmul(PP[:, reg:reg + 512],
                                  WB[:, wsl * 2816 + 128 * k:wsl * 2816 + 128 * (k + 1)],
                                  buf[:, 512 * k:512 * (k + 1)], start=(k == 0),
                                  stop=(k == 21))
                    em.inc(TE.drain(), "fd_T", 1)
                    em.wait("scalar", "fd_T", em.cnt["fd_T"])
                    em.dwait("scalar", "sync", ch=i3 % 4)
                    act(y2s[:, 512 * (i3 % 4):512 * (i3 % 4) + 512], PP[:, reg:reg + 512],
                        AF.Silu)
                    em.inc(SE.drain(), "fd_S", 1)
                    em.wait("sync", "fd_S", em.cnt["fd_S"])
                    em.dma("sync",
                           d_f[0].ap()[128 * po2:128 * (po2 + 1), 512 * tc:512 * (tc + 1)],
                           y2s[:, 512 * (i3 % 4):512 * (i3 % 4) + 512], ch=i3 % 4)
        em.barrier()

    def emit_combine(l, resid_dram, out_dram):
        for po in range(8):
            if po >= 2:
                em.wait("gpsimd", "cb%d_v" % l, po - 1)
            em.dma("sync", FA[:, 0:T], resid_dram.ap()[128 * po:128 * (po + 1), :], ch=0)
            em.dma("gpsimd", slot(A1, po % 2), d_s[l].ap()[128 * po:128 * (po + 1), :], ch=po % 2)
            em.dma("gpsimd", slot(A2, po % 2), d_f[l].ap()[128 * po:128 * (po + 1), :], ch=2 + po % 2)
            em.dwait("vector", "sync", ch=0)
            em.dwait("vector", "gpsimd", ch=po % 2)
            em.dwait("vector", "gpsimd", ch=2 + po % 2)
            vtt(FY[:, 0:T], slot(A1, po % 2), slot(A2, po % 2), ALU.mult)
            vtt(FY[:, 0:T], FY[:, 0:T], FA[:, 0:T], ALU.add)
            em.inc(VE.drain(), "cb%d_v" % l, 1)
            em.wait("sync", "cb%d_v" % l, po + 1)
            em.dma("sync", out_dram.ap()[128 * po:128 * (po + 1), :], FY[:, 0:T], ch=1)
            em.dwait("vector", "sync", ch=1)
        em.barrier()

    def emit_inproj1():
        for k in range(8):
            ar = A1 if k < 4 else A2
            em.dma("gpsimd", slot(ar, k % 4), d_xn[1].ap()[128 * k:128 * (k + 1), :], ch=k % 4)
        for c in range(4):
            em.dwait("tensor", "gpsimd", ch=c)
        for po in range(16):
            wsl = po % 2
            if po >= 2:
                em.wait("gpsimd", "ip_T", (po - 1) * NT)
            for k in range(8):
                em.dma("gpsimd", WB[:, wsl * 1024 + 128 * k:wsl * 1024 + 128 * (k + 1)],
                       w1_in.ap()[128 * k:128 * (k + 1), 128 * po:128 * (po + 1)], ch=4 + wsl)
            em.dwait("tensor", "gpsimd", ch=4 + wsl)
            for tc in range(NT):
                i = po * NT + tc
                reg = 512 * (i % 4)
                if em.cnt.get("ip_E", 0) >= 4:
                    em.wait("tensor", "ip_E", em.cnt["ip_E"] - 3)
                for k in range(8):
                    ar = A1 if k < 4 else A2
                    TE.matmul(PP[:, reg:reg + 512],
                              WB[:, wsl * 1024 + 128 * k:wsl * 1024 + 128 * (k + 1)],
                              slot(ar, k % 4)[:, 512 * tc:512 * (tc + 1)],
                              start=(k == 0), stop=(k == 7))
                em.inc(TE.drain(), "ip_T", 1)
                em.wait("vector", "ip_T", em.cnt["ip_T"])
                vcp(slotn(A3, 2, 512 * tc, 512), PP[:, reg:reg + 512])
                em.inc(VE.drain(), "ip_E", 1)
            em.inc(VE.drain(), "ip_V", 1)
            if po < 8:
                wc = convw1[:, 4 * po:4 * po + 4]
                em.dwait("vector", "sync", ch=6)
                VE.memset(A3[:, 2 * SW:2 * SW + PAD], 0.0)
                VE.drain()
                VE.memset(A3[:, 2 * SW + PAD + T:3 * SW], 0.0)
                VE.drain()
                vts(slot(A3, 0), slotn(A3, 2, 0, T), wc[:, 3:4], None, ALU.mult)
                vstt(slot(A3, 1), slotn(A3, 2, -1, T), wc[:, 2:3], slot(A3, 0), ALU.mult, ALU.add)
                vstt(slot(A3, 0), slotn(A3, 2, -2, T), wc[:, 1:2], slot(A3, 1), ALU.mult, ALU.add)
                vstt(slot(A3, 1), slotn(A3, 2, -3, T), wc[:, 0:1], slot(A3, 0), ALU.mult, ALU.add)
                vts(slot(A3, 1), slot(A3, 1), convb1[:, po:po + 1], None, ALU.add)
                em.inc(VE.drain(), "ip_Vc", 1)
                em.wait("scalar", "ip_Vc", em.cnt["ip_Vc"])
                act(slot(A3, 0), slot(A3, 1), AF.Silu)
                em.inc(SE.drain(), "ip_S", 1)
                em.wait("sync", "ip_S", em.cnt["ip_S"])
                em.dma("sync", d_xc1.ap()[128 * po:128 * (po + 1), :], slot(A3, 0), ch=6)
                em.dwait("vector", "sync", ch=6)
            else:
                em.wait("sync", "ip_V", po + 1)
                em.dma("sync", d_z1.ap()[128 * (po - 8):128 * (po - 7), :], slotn(A3, 2, 0, T), ch=7)
                em.dwait("vector", "sync", ch=7)
        em.barrier()

    def emit_xproj1():
        for k in range(8):
            em.dma("gpsimd", WA[:, 1056 * k:1056 * k + 1056],
                   w1_x.ap()[128 * k:128 * (k + 1), :], ch=k % 4)
        for c in range(4):
            em.dwait("tensor", "gpsimd", ch=c)
        for tc in range(NT):
            sl = tc % 2
            ar = A1 if sl == 0 else A2
            if tc >= 2:
                em.wait("sync", "xj_T", tc - 1)
            for k in range(8):
                em.dma("sync", ar[:, 512 * k:512 * k + 512],
                       d_xc1.ap()[128 * k:128 * (k + 1), 512 * tc:512 * (tc + 1)], ch=sl)
            em.dwait("tensor", "sync", ch=sl)
            for po in range(8):
                reg = 512 * (po % 4)
                if tc > 0 or po >= 4:
                    em.wait("tensor", "xj_V", em.cnt.get("xj_V", 0) - 3)
                for k in range(8):
                    TE.matmul(PP[:, reg:reg + 512],
                              WA[:, 1056 * k + 128 * po:1056 * k + 128 * (po + 1)],
                              ar[:, 512 * k:512 * k + 512], start=(k == 0), stop=(k == 7))
                em.inc(TE.drain(), "xj_T2", 1)
                em.wait("scalar", "xj_T2", em.cnt["xj_T2"])
                em.wait("scalar", "xj_Ssp", em.cnt.get("xj_Ssp", 0))
                act(sgn1[:, 0:512], PP[:, reg:reg + 512], AF.Abs)
                act(sgn1[:, 0:512], sgn1[:, 0:512], AF.Exp, scale=-1.0)
                em.inc(SE.drain(), "xj_Sa", 1)
                em.wait("vector", "xj_Sa", em.cnt["xj_Sa"])
                vts(sgn1[:, 0:512], sgn1[:, 0:512], 1.0, None, ALU.add)
                em.inc(VE.drain(), "xj_Va", 1)
                em.wait("scalar", "xj_Va", em.cnt["xj_Va"])
                act(sgn1[:, 0:512], sgn1[:, 0:512], AF.Ln)
                em.inc(SE.drain(), "xj_S", 1)
                em.wait("vector", "xj_S", em.cnt["xj_S"])
                em.dwait("vector", "sync", ch=4 + po % 2)
                vstt(FA[:, 512 * (po % 2):512 * (po % 2) + 512], PP[:, reg:reg + 512],
                     0.0, sgn1[:, 0:512], ALU.max, ALU.add)
                em.inc(VE.drain(), "xj_Ssp", 1)
                em.dwait("vector", "sync", ch=2 + po % 2)
                vtt(y2s[:, 512 * (po % 2):512 * (po % 2) + 512],
                    ar[:, 512 * po:512 * po + 512],
                    FA[:, 512 * (po % 2):512 * (po % 2) + 512], ALU.mult)
                em.inc(VE.drain(), "xj_V", 1)
                em.wait("sync", "xj_V", em.cnt["xj_V"])
                em.dma("sync", d_u1.ap()[128 * po:128 * (po + 1), 512 * tc:512 * (tc + 1)],
                       y2s[:, 512 * (po % 2):512 * (po % 2) + 512], ch=2 + po % 2)
                em.wait("sync", "xj_Ssp", em.cnt["xj_Ssp"])
                em.dma("sync", d_dt1.ap()[128 * po:128 * (po + 1), 512 * tc:512 * (tc + 1)],
                       FA[:, 512 * (po % 2):512 * (po % 2) + 512], ch=4 + po % 2)
            for k in range(8):
                TE.matmul(PP[0:32, 2048:2560], WA[:, 1056 * k + 1024:1056 * k + 1056],
                          ar[:, 512 * k:512 * k + 512], start=(k == 0), stop=(k == 7))
            em.inc(TE.drain(), "xj_T", 1)
            em.wait("vector", "xj_T", tc + 1)
            vcp(y2s[0:32, 1024:1536], PP[0:32, 2048:2560])
            em.inc(VE.drain(), "xj_Vb", 1)
            em.wait("sync", "xj_Vb", tc + 1)
            em.dma("sync", d_bc.ap()[0:32, 512 * tc:512 * (tc + 1)], y2s[0:32, 1024:1536], ch=6)
            em.dwait("vector", "sync", ch=6)
        em.barrier()

    def emit_scan1():
        for tq in range(4):
            for n in range(16):
                em.dma("sync", A1[:, TQ * n:TQ * (n + 1)],
                       bass.AP(d_bc, n * T + tq * TQ, [[0, 128], [1, TQ]]), ch=n % 4)
                em.dma("sync", A2[:, TQ * n:TQ * (n + 1)],
                       bass.AP(d_bc, (16 + n) * T + tq * TQ, [[0, 128], [1, TQ]]), ch=n % 4)
            for c in range(4):
                em.dwait("vector", "sync", ch=c)
                em.dwait("scalar", "sync", ch=c)
                em.dwait("gpsimd", "sync", ch=c)
            for dh in range(8):
                if em.cnt.get("s1_o", 0):
                    em.wait("gpsimd", "s1_o", em.cnt["s1_o"])
                    em.wait("sync", "s1_o", em.cnt["s1_o"])
                em.dma("sync", FA[:, 0:TQ],
                       d_dt1.ap()[128 * dh:128 * (dh + 1), tq * TQ:(tq + 1) * TQ], ch=4)
                em.dma("gpsimd", A3[:, 0:TQ],
                       d_u1.ap()[128 * dh:128 * (dh + 1), tq * TQ:(tq + 1) * TQ], ch=0)
                em.dma("gpsimd", A3[:, TQ:2 * TQ],
                       d_xc1.ap()[128 * dh:128 * (dh + 1), tq * TQ:(tq + 1) * TQ], ch=1)
                em.dwait("scalar", "sync", ch=4)
                em.dwait("vector", "gpsimd", ch=0)
                em.dwait("vector", "gpsimd", ch=1)
                GE.memset(FY[:, 0:2 * TQ], 0.0)
                em.inc(GE.drain(), "s1_z", 1)
                for n in range(16):
                    reg = 1024 * (n % 2)
                    idx = dh * 16 + n
                    fb = FA[:, 1024 + TQ * (n % 2):1024 + TQ * (n % 2) + TQ]
                    if n >= 2:
                        em.wait("scalar", "s1_sc", em.cnt.get("s1_sc", 0) - 1)
                    act(PP[:, reg:reg + TQ], FA[:, 0:TQ], AF.Exp, scale=-(float(n + 1)))
                    em.inc(SE.drain(), "s1_a", 1)
                    vtt(FY[:, 2 * TQ:3 * TQ], A3[:, 0:TQ], A1[:, TQ * n:TQ * n + TQ],
                        ALU.mult)
                    em.wait("vector", "s1_a", em.cnt["s1_a"])
                    if n >= 2:
                        em.wait("vector", "s1_gy", em.cnt.get("s1_gy", 0))
                    if tq == 0:
                        VE.tensor_tensor_scan(fb, PP[:, reg:reg + TQ],
                                              FY[:, 2 * TQ:3 * TQ], 0.0, ALU.mult, ALU.add)
                    else:
                        VE.tensor_tensor_scan(fb, PP[:, reg:reg + TQ],
                                              FY[:, 2 * TQ:3 * TQ], carry[:, idx:idx + 1],
                                              ALU.mult, ALU.add)
                    VE.drain()
                    em.inc(VE.drain(), "s1_sc", 1)
                    em.wait("scalar", "s1_sc", em.cnt["s1_sc"])
                    act(carry[:, idx:idx + 1], fb[:, TQ - 1:TQ], AF.Copy)
                    em.inc(SE.drain(), "s1_cv", 1)
                    if n % 2 == 1:
                        em.wait("gpsimd", "s1_cv", em.cnt["s1_cv"])
                        if n == 1:
                            em.wait("gpsimd", "s1_z", em.cnt["s1_z"])
                        GE.tensor_tensor(out=y2s[:, 0:2 * TQ],
                                         in0=FA[:, 1024:1024 + 2 * TQ],
                                         in1=A2[:, TQ * (n - 1):TQ * (n + 1)], op=ALU.mult)
                        GE.drain()
                        GE.tensor_tensor(out=FY[:, 0:2 * TQ], in0=FY[:, 0:2 * TQ],
                                         in1=y2s[:, 0:2 * TQ], op=ALU.add)
                        em.inc(GE.drain(), "s1_gy", 1)
                em.wait("vector", "s1_gy", em.cnt["s1_gy"])
                vtt(FY[:, 0:TQ], FY[:, 0:TQ], FY[:, TQ:2 * TQ], ALU.add)
                vstt(FY[:, 0:TQ], A3[:, TQ:2 * TQ], d1col[:, dh:dh + 1], FY[:, 0:TQ],
                     ALU.mult, ALU.add)
                vcp(y2s[:, 0:TQ], FY[:, 0:TQ])
                em.inc(VE.drain(), "s1_o", 1)
                em.wait("sync", "s1_o", em.cnt["s1_o"])
                em.dma("sync", d_y1.ap()[128 * dh:128 * (dh + 1), tq * TQ:(tq + 1) * TQ],
                       y2s[:, 0:TQ], ch=5)
                em.dwait("vector", "sync", ch=5)
                em.dwait("gpsimd", "sync", ch=4)
        em.barrier()

    def emit_gate_wout1():
        for k in range(8):
            em.dma("gpsimd", WA[:, 1024 * k:1024 * (k + 1)],
                   w1_out.ap()[128 * k:128 * (k + 1), :], ch=k % 4)
        for c in range(4):
            em.dwait("tensor", "gpsimd", ch=c)
        for tc in range(NT):
            if tc > 0:
                em.wait("scalar", "g1_To", 8 * tc)
            for k in range(8):
                par = k % 2
                if em.cnt.get("g1_S", 0) >= 2:
                    em.wait("gpsimd", "g1_S", em.cnt["g1_S"] - 1)
                em.dma("gpsimd", y2s[:, 512 * par:512 * (par + 1)],
                       d_z1.ap()[128 * k:128 * (k + 1), 512 * tc:512 * (tc + 1)], ch=4 + par)
                em.dwait("scalar", "gpsimd", ch=4 + par)
                act(A3[:, 512 * k:512 * (k + 1)], y2s[:, 512 * par:512 * (par + 1)], AF.Silu)
                em.inc(SE.drain(), "g1_S", 1)
            em.wait("vector", "g1_S", em.cnt["g1_S"])
            for k in range(8):
                par = k % 2
                if em.cnt.get("g1_V", 0) >= 2:
                    em.wait("gpsimd", "g1_V", em.cnt["g1_V"] - 1)
                em.dma("gpsimd", y2s[:, 1024 + 512 * par:1024 + 512 * (par + 1)],
                       d_y1.ap()[128 * k:128 * (k + 1), 512 * tc:512 * (tc + 1)], ch=6 + par)
                em.dwait("vector", "gpsimd", ch=6 + par)
                vtt(A3[:, 512 * k:512 * (k + 1)], A3[:, 512 * k:512 * (k + 1)],
                    y2s[:, 1024 + 512 * par:1024 + 512 * (par + 1)], ALU.mult)
                em.inc(VE.drain(), "g1_V", 1)
            em.wait("tensor", "g1_V", em.cnt["g1_V"])
            for po2 in range(8):
                reg = 2048 + 512 * (po2 % 2)
                if tc > 0 or po2 >= 2:
                    em.wait("tensor", "g1_Vo", em.cnt.get("g1_Vo", 0) - 1)
                for k in range(8):
                    TE.matmul(PP[:, reg:reg + 512],
                              WA[:, 1024 * k + 128 * po2:1024 * k + 128 * (po2 + 1)],
                              A3[:, 512 * k:512 * (k + 1)], start=(k == 0), stop=(k == 7))
                em.inc(TE.drain(), "g1_To", 1)
                em.wait("vector", "g1_To", em.cnt["g1_To"])
                vcp(trst[:, 0:512] if po2 % 2 == 0 else trst[:, 0:512], PP[:, reg:reg + 512])
                em.inc(VE.drain(), "g1_Vo", 1)
                em.wait("sync", "g1_Vo", em.cnt["g1_Vo"])
                em.dma("sync", d_s[1].ap()[128 * po2:128 * (po2 + 1), 512 * tc:512 * (tc + 1)],
                       trst[:, 0:512], ch=po2 % 2)
                em.dwait("vector", "sync", ch=po2 % 2)
        em.barrier()

    def emit_wquant_mean(w_dram, dst_dram, K, PO, name):
        nK = K // 128
        nH = (PO + 2047) // 2048
        for k in range(nK):
            ch = k % 2
            for hh in range(nH):
                w0c, w1c = 2048 * hh, min(PO, 2048 * (hh + 1))
                ww = w1c - w0c
                if em.cnt.get("%s_a" % name, 0):
                    em.wait("sync", "%s_a" % name, em.cnt["%s_a" % name])
                em.dma("sync", FA[:, 0:ww], w_dram.ap()[128 * k:128 * (k + 1), w0c:w1c], ch=ch)
                em.dwait("scalar", "sync", ch=ch)
                act(A3[:, 0:ww], FA[:, 0:ww], AF.Abs)
                em.inc(SE.drain(), "%s_a" % name, 1)
                em.wait("tensor", "%s_a" % name, em.cnt["%s_a" % name])
                first = (k == 0 and hh == 0)
                last = (k == nK - 1 and hh == nH - 1)
                for b in range(ww // 512):
                    TE.matmul(PP[0:1, 1024 + 512 * b:1024 + 512 * (b + 1)], onesb[:, 4:5],
                              A3[:, 512 * b:512 * (b + 1)], start=first, stop=last)
                em.inc(TE.drain(), "%s_t" % name, 1)
                em.wait("scalar", "%s_t" % name, em.cnt["%s_t" % name])
        em.wait("vector", "%s_t" % name, em.cnt["%s_t" % name])
        VE.tensor_reduce(scal[0:1, 24:25], PP[0:1, 1024:1024 + min(PO, 2048)], AX.X, ALU.add)
        VE.drain()
        vts(scal[0:1, 24:25], scal[0:1, 24:25], 1.0 / (K * PO), None, ALU.mult)
        vts(scal[0:1, 24:25], scal[0:1, 24:25], 1e-5, None, ALU.max)
        VE.reciprocal(scal[0:1, 25:26], scal[0:1, 24:25])
        VE.drain()
        em.inc(VE.drain(), "%s_v" % name, 1)
        em.wait("tensor", "%s_v" % name, 1)
        col_from_scalar(4, scal[0:1, 24:25])
        col_from_scalar(5, scal[0:1, 25:26])
        em.barrier()
        for k in range(nK):
            ch = k % 2
            for hh in range(nH):
                w0c, w1c = 2048 * hh, min(PO, 2048 * (hh + 1))
                ww = w1c - w0c
                em.dma("sync", FA[:, 0:ww], w_dram.ap()[128 * k:128 * (k + 1), w0c:w1c], ch=ch)
                em.dwait("vector", "sync", ch=ch)
                for b in range(ww // 512):
                    par = b % 2
                    em.dwait("vector", "sync", ch=2 + par)
                    vts(ytl[par][:, 0:512], FA[:, 512 * b:512 * (b + 1)], colsc[:, 5:6],
                        None, ALU.mult)
                    qround(y2s[:, 512 * par:512 * (par + 1)], par, -1.0, 1.0,
                           post_col=colsc[:, 4:5])
                    em.inc(VE.drain(), "%s_q" % name, 1)
                    em.wait("sync", "%s_q" % name, em.cnt["%s_q" % name])
                    em.dma("sync", wtile_store_ap(dst_dram, nK, (w0c + 512 * b) // 128, 4, k),
                           y2s[:, 512 * par:512 * (par + 1)].rearrange("p (j f) -> p j f", f=128),
                           ch=2 + par)
                em.inc(VE.drain(), "%s_fk" % name, 1)
                em.wait("sync", "%s_fk" % name, em.cnt["%s_fk" % name])
        em.barrier()

    def emit_token_scale_rows(name):
        for b in range(T // 128):
            if em.cnt.get("%s_c" % name, 0) >= 2:
                em.wait("tensor", "%s_c" % name, em.cnt["%s_c" % name] - 1)
            TE.transpose(PP[:, 2048 + 128 * (b % 2):2048 + 128 * (b % 2) + 128],
                         FA[:, 128 * b:128 * (b + 1)], eyef[:])
            em.inc(TE.drain(), "%s_t" % name, 1)
            em.wait("vector", "%s_t" % name, em.cnt["%s_t" % name])
            VE.tensor_reduce(gmaxc[:, 4 + b % 2:5 + b % 2],
                             PP[:, 2048 + 128 * (b % 2):2048 + 128 * (b % 2) + 128],
                             AX.X, ALU.max)
            VE.drain()
            vcp(sgn[0][:, b:b + 1], gmaxc[:, 4 + b % 2:5 + b % 2])
            em.inc(VE.drain(), "%s_c" % name, 1)
            if b + 2 < T // 128:
                em.wait("tensor", "%s_c" % name, em.cnt["%s_c" % name] - 1)
        em.wait("tensor", "%s_c" % name, em.cnt["%s_c" % name])
        nblk = T // 128
        for g in range((nblk + 127) // 128):
            w = min(128, nblk - 128 * g)
            TE.transpose(PP[0:w, 2048:2176], sgn[0][:, 128 * g:128 * g + w], eyef[:])
            em.inc(TE.drain(), "%s_t2" % name, 1)
            em.wait("vector", "%s_t2" % name, em.cnt["%s_t2" % name])
            vcp(accs[0:w, 0:128], PP[0:w, 2048:2176])
            em.inc(VE.drain(), "%s_v2" % name, 1)
            em.wait("gpsimd", "%s_v2" % name, em.cnt["%s_v2" % name])
            em.dma("gpsimd", bass.AP(ROWS, 128 * 128 * g, [[FW, 1], [1, w * 128]]),
                   accs[0:w, 0:128], ch=6)
            em.dwait("vector", "gpsimd", ch=6)
        vts(qscr[0:1, 0:T], qscr[0:1, 0:T], 1e-5, None, ALU.max)
        vts(qinr[0:1, 0:T], qscr[0:1, 0:T], 1.0 / 127.0, None, ALU.mult)
        VE.reciprocal(qscr[0:1, 0:T], qinr[0:1, 0:T])
        VE.drain()
        em.inc(VE.drain(), "%s_fin" % name, 1)
        em.wait("tensor", "%s_fin" % name, 1)
        em.wait("scalar", "%s_fin" % name, 1)

    def emit_quant_tok(src, dst, R, name):
        nR = R // 128
        for tc in range(NT):
            TE.matmul(PP[:, 3072:3584], onesf[0:1, 0:128],
                      qscr[0:1, 512 * tc:512 * (tc + 1)], start=True, stop=True)
            TE.matmul(PP[:, 2560:3072], ones33f[32:33, 0:128],
                      qinr[0:1, 512 * tc:512 * (tc + 1)], start=True, stop=True)
            em.inc(TE.drain(), "%s_r" % name, 1)
            em.wait("vector", "%s_r" % name, em.cnt["%s_r" % name])
            for po in range(nR):
                par = po % 2
                if em.cnt.get("%s_q" % name, 0) >= 2:
                    em.wait("gpsimd", "%s_q" % name, em.cnt["%s_q" % name] - 1)
                em.dma("gpsimd", ytl[par][:, 0:512],
                       src.ap()[128 * po:128 * (po + 1), 512 * tc:512 * (tc + 1)], ch=par)
                em.dwait("vector", "gpsimd", ch=par)
                em.dwait("vector", "sync", ch=par)
                vtt(ytl[par][:, 0:512], ytl[par][:, 0:512], PP[:, 3072:3584], ALU.mult)
                qround(y2s[:, 512 * par:512 * (par + 1)], par, -128.0, 127.0,
                       post_rep=PP[:, 2560:3072])
                em.inc(VE.drain(), "%s_q" % name, 1)
                em.wait("sync", "%s_q" % name, em.cnt["%s_q" % name])
                em.dma("sync", dst.ap()[128 * po:128 * (po + 1), 512 * tc:512 * (tc + 1)],
                       y2s[:, 512 * par:512 * (par + 1)], ch=par)
                if po >= 2:
                    em.dwait("gpsimd", "sync", ch=par)
            em.inc(VE.drain(), "%s_tc" % name, 1)
            em.wait("tensor", "%s_tc" % name, tc + 1)
        em.barrier()

    d_xq1 = d_y2f0

    def emit_xq1():
        for k in range(8):
            sl = k % 2
            em.dma("gpsimd", slot(A1, sl), d_xn[1].ap()[128 * k:128 * (k + 1), :], ch=sl)
            em.dwait("scalar", "gpsimd", ch=sl)
            em.wait("scalar", "xq1_f", em.cnt.get("xq1_f", 0))
            act(slot(A3, 0), slot(A1, sl), AF.Abs)
            em.inc(SE.drain(), "xq1_fa", 1)
            em.wait("vector", "xq1_fa", em.cnt["xq1_fa"])
            if k == 0:
                vcp(FA[:, 0:T], slot(A3, 0))
            else:
                vtt(FA[:, 0:T], FA[:, 0:T], slot(A3, 0), ALU.max)
            em.inc(VE.drain(), "xq1_f", 1)
            if k + 2 <= 8:
                em.wait("gpsimd", "xq1_f", max(1, em.cnt["xq1_f"] - 1))
        em.wait("tensor", "xq1_f", 8)
        em.wait("scalar", "xq1_f", 8)
        emit_token_scale_rows("ts1")
        emit_quant_wide(d_xn[1], d_xq1, D, "qt1", -128.0, 127.0, "row")

    def emit_ffn1_w1():
        for k in range(8):
            ar = A1 if k < 4 else A2
            em.dma("gpsimd", slot(ar, k % 4), d_xq1.ap()[128 * k:128 * (k + 1), :], ch=k % 4)
        for c in range(4):
            em.dwait("tensor", "gpsimd", ch=c)
        GE.memset(FA[:, 0:T], 0.0)
        em.inc(GE.drain(), "f1_g", 1)
        em.wait("vector", "f1_g", 1)
        VE.memset(A3[:, 2 * SW:2 * SW + PAD], 0.0)
        VE.drain()
        VE.memset(A3[:, 2 * SW + PAD + T:3 * SW], 0.0)
        em.inc(VE.drain(), "f1_p", 1)
        for po in range(32):
            wsl = po % 2
            if po >= 2:
                em.wait("sync", "f1_T", (po - 1) * NT)
            em.dma("sync", WB[:, wsl * 1024:wsl * 1024 + 1024],
                   wtile_load_ap(d_w1q, 8, po), ch=4 + wsl)
            em.dwait("tensor", "sync", ch=4 + wsl)
            for tc in range(NT):
                i = po * NT + tc
                reg = 512 * (i % 4)
                if em.cnt.get("f1_E", 0) >= 4:
                    em.wait("tensor", "f1_E", em.cnt["f1_E"] - 3)
                for k in range(8):
                    ar = A1 if k < 4 else A2
                    TE.matmul(PP[:, reg:reg + 512],
                              WB[:, wsl * 1024 + 128 * k:wsl * 1024 + 128 * (k + 1)],
                              slot(ar, k % 4)[:, 512 * tc:512 * (tc + 1)],
                              start=(k == 0), stop=(k == 7))
                em.inc(TE.drain(), "f1_T", 1)
                em.wait("scalar", "f1_T", em.cnt["f1_T"])
                act(slotn(A3, 2, 512 * tc, 512), PP[:, reg:reg + 512], AF.Copy)
                em.inc(SE.drain(), "f1_E", 1)
            wc = convw2[:, 3 * po:3 * po + 3]
            em.wait("vector", "f1_E", em.cnt["f1_E"])
            if po > 0:
                em.wait("vector", "f1_Sa", po)
            vts(slot(A3, 0), slotn(A3, 2, -1, T), wc[:, 0:1], None, ALU.mult)
            vstt(slot(A3, 1), slotn(A3, 2, 0, T), wc[:, 1:2], slot(A3, 0), ALU.mult, ALU.add)
            vstt(slot(A3, 0), slotn(A3, 2, 1, T), wc[:, 2:3], slot(A3, 1), ALU.mult, ALU.add)
            vts(slot(A3, 0), slot(A3, 0), convb2[:, po:po + 1], None, ALU.add)
            em.inc(VE.drain(), "f1_Vc", 1)
            em.wait("scalar", "f1_Vc", po + 1)
            em.dwait("scalar", "sync", ch=6)
            em.dwait("scalar", "sync", ch=7)
            fbuf = A3[:, EXT:EXT + T]
            act(fbuf, slot(A3, 0), AF.Silu)
            em.inc(SE.drain(), "f1_S", 1)
            em.wait("sync", "f1_S", po + 1)
            em.dma("sync", d_h1.ap()[128 * po:128 * (po + 1), 0:T // 2],
                   fbuf[:, 0:T // 2], ch=6)
            em.dma("sync", d_h1.ap()[128 * po:128 * (po + 1), T // 2:T],
                   fbuf[:, T // 2:T], ch=7)
            act(slot(A3, 1), fbuf, AF.Abs)
            em.inc(SE.drain(), "f1_Sa", 1)
            em.wait("vector", "f1_Sa", po + 1)
            vtt(FA[:, 0:T], FA[:, 0:T], slot(A3, 1), ALU.max)
            em.inc(VE.drain(), "f1_Vm", 1)
        em.barrier()
        emit_token_scale_rows("ts2")
        emit_quant_wide(d_h1, d_h1q, FF1, "qt2", -128.0, 127.0, "row")

    def emit_ffn1_w2():
        groups = [list(range(g, min(g + 2, NT))) for g in range(0, NT, 2)]
        for gi, tcs in enumerate(groups):
            baseT = em.cnt.get("f2_T", 0)
            for ti, tc in enumerate(tcs):
                buf = A1 if ti == 0 else A2
                if gi >= 1 and ti == 0:
                    em.wait("gpsimd", "f2_T", baseT)
                for k in range(32):
                    em.dma("gpsimd", buf[:, 512 * k:512 * (k + 1)],
                           d_h1q.ap()[128 * k:128 * (k + 1), 512 * tc:512 * (tc + 1)],
                           ch=k % 4)
            for c in range(4):
                em.dwait("tensor", "gpsimd", ch=c)
            nl = len(tcs)
            for po2 in range(8):
                wsl = po2 % 2
                tgt = baseT + nl * (po2 - 1)
                if tgt > 0:
                    em.wait("sync", "f2_T", tgt)
                em.dma("sync", WA[:, wsl * 4096:wsl * 4096 + 4096],
                       wtile_load_ap(d_w2q, 32, po2), ch=4 + po2 % 4)
                em.dwait("tensor", "sync", ch=4 + po2 % 4)
                for ti, tc in enumerate(tcs):
                    buf = A1 if ti == 0 else A2
                    i3 = baseT + po2 * nl + ti
                    reg = 512 * (i3 % 4)
                    if i3 >= 4:
                        em.wait("tensor", "f2_S", i3 - 3)
                    for k in range(32):
                        TE.matmul(PP[:, reg:reg + 512],
                                  WA[:, wsl * 4096 + 128 * k:wsl * 4096 + 128 * (k + 1)],
                                  buf[:, 512 * k:512 * (k + 1)], start=(k == 0),
                                  stop=(k == 31))
                    em.inc(TE.drain(), "f2_T", 1)
                    em.wait("scalar", "f2_T", em.cnt["f2_T"])
                    em.dwait("scalar", "sync", ch=i3 % 4)
                    act(y2s[:, 512 * (i3 % 4):512 * (i3 % 4) + 512], PP[:, reg:reg + 512],
                        AF.Silu)
                    em.inc(SE.drain(), "f2_S", 1)
                    em.wait("sync", "f2_S", em.cnt["f2_S"])
                    em.dma("sync",
                           d_f[1].ap()[128 * po2:128 * (po2 + 1), 512 * tc:512 * (tc + 1)],
                           y2s[:, 512 * (i3 % 4):512 * (i3 % 4) + 512], ch=i3 % 4)
        em.barrier()

    _stage_list = [
        ("ln0", lambda: emit_ln(0, xT)),
        ("scale0", emit_scale0),
        ("winx_conv", emit_winx_conv),
        ("wx", emit_wx),
        ("scan0", emit_scan0),
        ("xq0", emit_xq0),
        ("gate_wout0", emit_gate_wout0),
        ("wq_wg", lambda: emit_wquant_row(f0_wg, d_wgq, D, FF0, "wg")),
        ("wq_wu", lambda: emit_wquant_row(f0_wu, d_wuq, D, FF0, "wu")),
        ("wq_wd", lambda: emit_wquant_row(f0_wd, d_wdq, FF0, D, "wd")),
        ("ffn0_gateup", emit_ffn0_gateup),
        ("scaleh", emit_scaleh),
        ("ffn0_down", emit_ffn0_down),
        ("combine0", lambda: emit_combine(0, xT, d_x1)),
        ("ln1", lambda: emit_ln(1, d_x1)),
        ("inproj1", emit_inproj1),
        ("xproj1", emit_xproj1),
        ("scan1", emit_scan1),
        ("gate_wout1", emit_gate_wout1),
        ("wq_m1", lambda: emit_wquant_mean(f1_w1, d_w1q, D, FF1, "m1")),
        ("wq_m2", lambda: emit_wquant_mean(f1_w2, d_w2q, FF1, D, "m2")),
        ("xq1", emit_xq1),
        ("ffn1_w1", emit_ffn1_w1),
        ("ffn1_w2", emit_ffn1_w2),
        ("combine1", lambda: emit_combine(1, d_x1, out)),
    ]
    global STAGE_RANGES
    STAGE_RANGES = []
    for _sname, _sfn in _stage_list:
        _n0 = len(_BB.instructions)
        _sfn()
        STAGE_RANGES.append((_sname, _n0, len(_BB.instructions)))

    em.finalize()
    return nc



def mask_const():
    m = np.zeros((32, 32), np.float32)
    m[np.tril_indices(32, -1)] = BIG_NEG
    return np.ascontiguousarray(m.reshape(1, 1024))


def prepare_core_inputs(inputs, b):
    c = lambda a: np.ascontiguousarray(np.asarray(a), dtype=np.float32)
    xp = np.asarray(inputs["L1_attn_x_proj_w"])
    xp_re = np.concatenate([xp[32:], xp[:16], xp[16:32]], axis=0)
    return {
        "xT": c(np.asarray(inputs["L0_attn_x"])[b].T),
        "w0_in": c(np.asarray(inputs["L0_attn_W_in"]).T),
        "w0_conv": c(np.asarray(inputs["L0_attn_conv_w"])[:, 0, :]),
        "w0_convb": c(np.asarray(inputs["L0_attn_conv_b"])[:, None]),
        "w0_x": c(np.asarray(inputs["L0_attn_W_x"])[list(range(64, 128)) + list(range(0, 64)) + [128]].T),
        "w0_A": c(np.asarray(inputs["L0_attn_A_log"])[None, :]),
        "w0_D": c(np.asarray(inputs["L0_attn_D"])[None, :]),
        "w0_out": c(np.asarray(inputs["L0_attn_W_out"]).T),
        "f0_clipx": c(np.asarray(inputs["L0_ffn_clip_x"])[:, None]),
        "f0_clipd": c(np.asarray(inputs["L0_ffn_clip_down"])[:, None]),
        "f0_wg": c(np.asarray(inputs["L0_ffn_w_gate"]).T),
        "f0_wu": c(np.asarray(inputs["L0_ffn_w_up"]).T),
        "f0_wd": c(np.asarray(inputs["L0_ffn_w_down"]).T),
        "w1_in": c(np.asarray(inputs["L1_attn_in_proj_w"]).T),
        "w1_conv": c(np.asarray(inputs["L1_attn_conv_w"])[:, 0, :]),
        "w1_convb": c(np.asarray(inputs["L1_attn_conv_b"])[:, None]),
        "w1_x": c(xp_re.T),
        "w1_D": c(np.asarray(inputs["L1_attn_D_param"])[None, :]),
        "w1_out": c(np.asarray(inputs["L1_attn_out_proj_w"]).T),
        "f1_w1": c(np.asarray(inputs["L1_ffn_w1"]).T),
        "f1_conv": c(np.asarray(inputs["L1_ffn_conv_w"])[:, 0, :]),
        "f1_convb": c(np.asarray(inputs["L1_ffn_conv_b"])[:, None]),
        "f1_w2": c(np.asarray(inputs["L1_ffn_w2"]).T),
        "cst_mask": mask_const(),
        "cst_eyeb": np.eye(128, dtype=np.float32),
    }


def kernel(**inputs):
    nc = build_program(T=4096)
    in_maps = [prepare_core_inputs(inputs, b) for b in range(8)]
    res = run_bass_kernel_spmd(nc, in_maps, core_ids=list(range(8)))
    return np.stack([np.ascontiguousarray(r["out"].T) for r in res.results]).astype(np.float32)


# version 28
# speedup vs baseline: 1.0882x; 1.0882x over previous
import numpy as np
import concourse.bass as bass
import concourse.mybir as mybir
from concourse.bass_utils import run_bass_kernel_spmd

F32 = mybir.dt.float32
BF16 = mybir.dt.bfloat16
I32 = mybir.dt.int32
AF = mybir.ActivationFunctionType
ALU = mybir.AluOpType
AX = mybir.AxisListType

D = 1024
DI = 2048
DS = 64
FF0 = 2816
FF1 = 4096
NST = 16
BIG_NEG = -1.0e30
PAD = 8


STAGE_RANGES = []


def build_program(T=4096, n_cores=8):
    nc = bass.Bass("TRN2", target_bir_lowering=False)
    NT = T // 512
    NCH = T // 128
    TQ = T // 4
    TQ = T // 4
    SW = T + 2 * PAD

    ein = lambda n, shp, dt=F32: nc.dram_tensor(n, shp, dt, kind="ExternalInput")
    xT = ein("xT", [D, T])
    w0_in = ein("w0_in", [D, 2 * DI])
    w0_conv = ein("w0_conv", [DI, 4])
    w0_convb = ein("w0_convb", [DI, 1])
    w0_x = ein("w0_x", [DI, 129])
    w0_A = ein("w0_A", [1, DS])
    w0_D = ein("w0_D", [1, DI])
    w0_out = ein("w0_out", [DI, D])
    f0_clipx = ein("f0_clipx", [1, 1])
    f0_clipd = ein("f0_clipd", [1, 1])
    f0_wg = ein("f0_wg", [D, FF0])
    f0_wu = ein("f0_wu", [D, FF0])
    f0_wd = ein("f0_wd", [FF0, D])
    w1_in = ein("w1_in", [D, 2 * D])
    w1_conv = ein("w1_conv", [D, 4])
    w1_convb = ein("w1_convb", [D, 1])
    w1_x = ein("w1_x", [D, D + 2 * NST])
    w1_D = ein("w1_D", [1, D])
    w1_out = ein("w1_out", [D, D])
    f1_w1 = ein("f1_w1", [D, FF1])
    f1_conv = ein("f1_conv", [FF1, 3])
    f1_convb = ein("f1_convb", [FF1, 1])
    f1_w2 = ein("f1_w2", [FF1, D])
    cst_mask = ein("cst_mask", [1, 1024])
    cst_eyeb = ein("cst_eyeb", [128, 128])

    out = nc.dram_tensor("out", [D, T], F32, kind="ExternalOutput")

    dram = lambda n, shp, dt=BF16: nc.dram_tensor(n, shp, dt)
    d_xn = [dram("d_xn%d" % l, [D, T]) for l in range(2)]
    d_xc0 = dram("d_xc0", [DI, T])
    d_xct0 = dram("d_xct0", [T, DI])
    d_y2f0 = dram("d_y2f0", [DI, T])
    d_s = [dram("d_s%d" % l, [D, T]) for l in range(2)]
    d_h0 = dram("d_h0", [FF0, T])
    d_f = [dram("d_f%d" % l, [D, T]) for l in range(2)]
    d_x1 = dram("d_x1", [D, T], F32)
    d_xc1 = dram("d_xc1", [D, T])
    d_z1 = dram("d_z1", [D, T])
    d_dt1 = dram("d_dt1", [D, T], F32)
    d_u1 = dram("d_u1", [D, T])
    d_y1 = dram("d_y1", [D, T])
    d_h1 = dram("d_h1", [FF1, T])
    d_bc = dram("d_bc", [32, T])
    d_wgq = dram("d_wgq", [D, FF0])
    d_wuq = dram("d_wuq", [D, FF0])
    d_wdq = dram("d_wdq", [FF0, D])
    d_w1q = dram("d_w1q", [D, FF1])
    d_w2q = dram("d_w2q", [FF1, D])
    d_h1q = dram("d_h1q", [FF1, T])
    cc_in = nc.dram_tensor("cc_in", [1, 16], F32)
    cc_out = nc.dram_tensor("cc_out", [1, 16], F32)

    ctx = []
    def sbuf(name, shp, dt):
        h = nc.sbuf_tensor(name, shp, dt).__enter__()
        ctx.append(h)
        return h
    A1W = max(4 * SW, 16448)
    EXT = max(3 * SW, 4608)
    YO = max(EXT + 2048, 8192)
    A3W = YO + 2048
    FW = max(T, 4096)
    A1 = sbuf("A1", [128, A1W], BF16)
    A2 = sbuf("A2", [128, A1W], BF16)
    A3 = sbuf("A3", [128, A3W], BF16)
    WA = sbuf("WA", [128, 8448], BF16)
    WB = sbuf("WB", [128, 6144], BF16)
    FA = sbuf("FA", [128, FW], F32)
    FY = sbuf("FY", [128, FW], F32)
    FB = FA[:, 3072:4096]
    ccbv = None
    H0s = FA[0:64, 0:2048]
    PP = nc.psum_tensor("PP", [128, 3584], F32).__enter__()
    ctx.append(PP)
    PPb = nc.psum_tensor("PPb", [128, 1024], BF16).__enter__()
    ctx.append(PPb)

    def slot(arena, i):
        return arena[:, i * SW + PAD: i * SW + PAD + T]

    def slotn(arena, i, sh, n):
        return arena[:, i * SW + PAD + sh: i * SW + PAD + sh + n]

    def wtile_store_ap(dst, nK, jbase, nj, k):
        return bass.AP(dst, jbase * nK * 16384 + k * 128,
                       [[nK * 128, 128], [nK * 16384, nj], [1, 128]])

    def wtile_load_ap(dst, nK, po):
        return bass.AP(dst, po * nK * 16384, [[nK * 128, 128], [1, nK * 128]])

    onesb = sbuf("onesb", [128, 8], BF16)
    onesr = sbuf("onesr", [1, 128], BF16)
    onesf = sbuf("onesf", [1, 128], F32)
    ones33b = sbuf("ones33b", [65, 128], BF16)
    ones33f = sbuf("ones33f", [65, 128], F32)
    eyeb = sbuf("eyeb", [128, 128], BF16)
    eyef = sbuf("eyef", [128, 128], F32)
    convw0 = sbuf("convw0", [128, 16 * 4], F32)
    convb0 = sbuf("convb0", [128, 16], F32)
    convw1 = sbuf("convw1", [128, 8 * 4], F32)
    convb1 = sbuf("convb1", [128, 8], F32)
    convw2 = sbuf("convw2", [128, 32 * 3], F32)
    convb2 = sbuf("convb2", [128, 32], F32)
    arow = sbuf("arow", [1, DS], F32)
    drep0 = WB[:, 2048:4096]
    d1col = sbuf("d1col", [128, 8], F32)
    scal = sbuf("scal", [1, 64], F32)
    colsc = sbuf("colsc", [128, 16], F32)
    ROWS = sbuf("ROWS", [128, FW], F32)
    crow = ROWS[0:1, :]
    dtrow = ROWS[32:33, :]
    maskrow = ROWS[64:65, :]
    mflat = ROWS[96:97, 0:4096]
    qscr = ROWS[0:1, :]
    qinr = ROWS[32:33, :]
    srow = [ROWS[0:1, :], ROWS[32:33, :], ROWS[64:65, :], ROWS[96:97, :]]
    srowbA = A2[0:1, 8240:8240 + FW]
    srowbB = A2[32:33, 8240:8240 + FW]
    ssmBC = sbuf("ssmBC", [128, T], BF16)
    bdt0 = WA[0:64, 4352:4352 + T]
    crowD = sbuf("crowD", [1, 576], F32)
    erow = sbuf("erow", [1, NCH * 4], F32)
    brow = sbuf("brow", [1, NCH * 4], F32)
    Mfull = sbuf("Mfull", [128, 128], BF16)
    Cbt = sbuf("Cbt", [64, 128], BF16)
    Bbt = sbuf("Bbt", [64, 128], BF16)
    CbM = sbuf("CbM", [64, 128], BF16)
    Cct = sbuf("Cct", [64, 128], BF16)
    Bht = sbuf("Bht", [128, 64], BF16)
    mid6 = sbuf("mid6", [64, 8], F32)
    ediag = A3[0:64, 0:4096]
    mask64 = A3[0:64, EXT:EXT + 1024]
    trstB = A3[:, EXT + 1024:EXT + 1536]
    trst = A3[:, EXT + 1536:EXT + 2048]
    y2s = A3[:, YO:YO + 2048]
    accs = sbuf("accs", [64, 128], F32)
    growA = sbuf("growA", [1, 64], F32)
    growB = sbuf("growB", [1, 64], F32)
    gmaxc = sbuf("gmaxc", [128, 8], F32)
    sgn = None
    itl = None
    ytl = [FB[:, 0:512], FB[:, 512:1024]]
    rmode = sbuf("rmode", [128, 1], F32)
    rmi = sbuf("rmi", [128, 1], I32)
    sgn1 = sbuf("sgn1", [128, 512], F32)
    itl1 = sbuf("itl1", [128, 2048], I32)
    sgn = [sgn1, sgn1]
    itl = [itl1, itl1]
    carry = sbuf("carry", [128, 130], F32)

    blk = nc.Block().__enter__()
    _BB = nc.m.functions[0].blocks[-1]
    import re as _re
    _SEMRE = _re.compile(r"S\[(s\d+)\]\+=(\d+)")

    PERSISTENT = ("bar", "qr_y", "qr_s", "qr_v", "bc_mm", "bc_ev",
                  "fc_mm", "fc_ev", "cc_v", "ar_cc", "ini_g", "ini_s", "ini_v", "ini_t")

    class Em:
        def __init__(self):
            self.name2sem = {}
            self.base = {}
            self.cnt = {}
            self.physcnt = {}
            self.free = []
            self.semname = {}
            self.nsem = 0
            self.bar_t = 0
            self.eng = dict(tensor=nc.tensor, vector=nc.vector, scalar=nc.scalar,
                            gpsimd=nc.gpsimd, sync=nc.sync)

        def sem(self, name):
            if name not in self.name2sem:
                if self.free and not (name.startswith("d_") or name in PERSISTENT):
                    h = self.free.pop()
                else:
                    h = nc.semaphore("s%d" % self.nsem).__enter__()
                    self.semname[id(h)] = "s%d" % self.nsem
                    self.nsem += 1
                    self.physcnt[id(h)] = 0
                self.name2sem[name] = h
                self.base[name] = self.physcnt[id(h)]
                self.cnt[name] = 0
            return self.name2sem[name]

        def inc(self, instr, name, by=1):
            h = self.sem(name)
            self.cnt[name] += by
            self.physcnt[id(h)] += by
            instr.then_inc(h, by)
            return self.cnt[name]

        def wait(self, eng, name, val=None):
            h = self.sem(name)
            v = self.cnt[name] if val is None else val
            self.eng[eng].wait_ge(h, self.base[name] + v)

        def retire_locals(self):
            for name in list(self.name2sem):
                if name.startswith("d_") or name in PERSISTENT:
                    continue
                self.free.append(self.name2sem.pop(name))
                del self.base[name]
                del self.cnt[name]

        def dma(self, queue, out_, in_, ch=0):
            name = "d_%s_%d" % (queue[0], ch)
            h = self.sem(name)
            if self.cnt[name]:
                self.eng[queue].wait_ge(h, self.base[name] + self.cnt[name])
            n0 = len(_BB.instructions)
            self.eng[queue].dma_start(out=out_, in_=in_).then_inc(h, 16)
            n1 = len(_BB.instructions)
            sname = self.semname[id(h)]
            inc = 0
            for ins in _BB.instructions[n0:n1]:
                m = _SEMRE.search(str(ins))
                if m and m.group(1) == sname:
                    inc += int(m.group(2))
            if inc == 0:
                inc = 16
            self.cnt[name] += inc
            self.physcnt[id(h)] += inc
            return self.cnt[name]

        def dwait(self, eng, queue, ch=0, tgt=None):
            self.wait(eng, "d_%s_%d" % (queue[0], ch), tgt)

        def flush_dmas(self):
            for name in list(self.cnt):
                if name.startswith("d_s"):
                    self.wait("sync", name)
                elif name.startswith("d_g"):
                    self.wait("gpsimd", name)

        def barrier(self):
            self.flush_dmas()
            h = self.sem("bar")
            for e in ("tensor", "vector", "scalar", "gpsimd", "sync"):
                self.eng[e].nop().then_inc(h, 1)
                self.physcnt[id(h)] += 1
            self.bar_t += 5
            for e in ("tensor", "vector", "scalar", "gpsimd", "sync"):
                self.eng[e].wait_ge(h, self.bar_t)
            self.retire_locals()

        def finalize(self):
            blk.__exit__(None, None, None)

    em = Em()
    TE, VE, SE, GE, SY = nc.tensor, nc.vector, nc.scalar, nc.gpsimd, nc.sync

    def vtt(out_, a, b, op):
        VE.tensor_tensor(out_, a, b, op)
        VE.drain()

    def vts(out_, a, s1, s2, op0, op1=None):
        if op1 is None:
            VE.tensor_scalar(out_, a, s1, None, op0)
        else:
            VE.tensor_scalar(out_, a, s1, s2, op0, op1)
        VE.drain()

    def vstt(out_, in0, scalar, in1, op0, op1):
        VE.scalar_tensor_tensor(out_, in0, scalar, in1, op0, op1)
        VE.drain()

    def vcp(out_, in_):
        VE.tensor_copy(out_, in_)
        VE.drain()

    def act(out_, in_, func, **kw):
        SE.activation(out=out_, in_=in_, func=func, **kw)
        SE.drain()

    def col_from_scalar(idx, src_scalar_ap):
        TE.matmul(PP[0:128, 3072:3073], onesf[0:1, 0:128], src_scalar_ap,
                  start=True, stop=True)
        n = em.inc(TE.drain(), "bc_mm", 1)
        em.wait("vector", "bc_mm", n)
        vcp(colsc[:, idx:idx + 1], PP[0:128, 3072:3073])
        n2 = em.inc(VE.drain(), "bc_ev", 1)
        em.wait("tensor", "bc_ev", n2)

    def fold_col_max(src_col_ap, dst):
        TE.transpose(PP[0:1, 2048:2176], src_col_ap, eyef[:])
        n = em.inc(TE.drain(), "fc_mm", 1)
        em.wait("vector", "fc_mm", n)
        VE.tensor_reduce(scal[0:1, dst:dst + 1], PP[0:1, 2048:2176], AX.X, ALU.max)
        VE.drain()
        n2 = em.inc(VE.drain(), "fc_ev", 1)
        em.wait("tensor", "fc_ev", n2)

    def allreduce_max(si, di):
        VE.tensor_scalar(scal[0:1, 32:48],
                         bass.AP(scal, si, [[64, 1], [0, 16]]), 1.0, None, ALU.mult)
        VE.drain()
        n = em.inc(VE.drain(), "cc_v", 1)
        em.wait("sync", "cc_v", n)
        em.dma("sync", cc_in.ap(), scal[0:1, 32:48], ch=7)
        em.dwait("gpsimd", "sync", ch=7)
        cci = GE.collective_compute("AllReduce", ALU.max,
                                    replica_groups=[list(range(n_cores))],
                                    ins=[cc_in.ap()], outs=[cc_out.ap()])
        em.inc(cci, "ar_cc", 1)
        em.wait("gpsimd", "ar_cc")
        em.dma("gpsimd", scal[0:1, di:di + 1], cc_out.ap()[0:1, 0:1], ch=7)
        em.dwait("vector", "gpsimd", ch=7)
        em.dwait("scalar", "gpsimd", ch=7)

    def qround(dst_bf, par, lo, hi, post_col=None, post_rep=None, nn=512):
        yv, sg, it = ytl[par], sgn[par], itl[par]
        n0 = em.inc(VE.drain(), "qr_y", 1)
        em.wait("scalar", "qr_y", n0)
        act(sg[:, 0:nn], yv[:, 0:nn], AF.Sign)
        n1 = em.inc(SE.drain(), "qr_s", 1)
        em.wait("vector", "qr_s", n1)
        vstt(yv[:, 0:nn], sg[:, 0:nn], rmode[:, 0:1], yv[:, 0:nn], ALU.mult, ALU.add)
        vcp(it[:, 0:nn], yv[:, 0:nn])
        if post_col is None and post_rep is None:
            vts(dst_bf, it[:, 0:nn], float(lo), float(hi), ALU.max, ALU.min)
        elif post_rep is None:
            vts(it[:, 0:nn], it[:, 0:nn], float(lo), float(hi), ALU.max, ALU.min)
            vts(dst_bf, it[:, 0:nn], post_col, None, ALU.mult)
        else:
            vts(it[:, 0:nn], it[:, 0:nn], float(lo), float(hi), ALU.max, ALU.min)
            vtt(dst_bf, it[:, 0:nn], post_rep, ALU.mult)

    GE.memset(onesb[:], 1.0)
    GE.memset(onesr[:], 1.0)
    GE.memset(onesf[:], 1.0)
    GE.memset(ones33b[:], 1.0)
    GE.memset(ones33f[:], 1.0)
    GE.memset(gmaxc[:], 0.0)
    GE.memset(A1[:], 0.0)
    GE.memset(A2[:], 0.0)
    GE.memset(A3[:], 0.0)
    GE.memset(brow[:], 0.0)
    GE.memset(rmode[:], 0.75)
    em.inc(GE.drain(), "ini_g", 1)
    em.dma("gpsimd", eyeb[:], cst_eyeb.ap(), ch=0)
    for c in range(16):
        em.dma("sync", convw0[:, 4 * c:4 * c + 4], w0_conv.ap()[128 * c:128 * (c + 1), :], ch=0)
        em.dma("sync", convb0[:, c:c + 1], w0_convb.ap()[128 * c:128 * (c + 1), :], ch=0)
    for c in range(8):
        em.dma("sync", convw1[:, 4 * c:4 * c + 4], w1_conv.ap()[128 * c:128 * (c + 1), :], ch=0)
        em.dma("sync", convb1[:, c:c + 1], w1_convb.ap()[128 * c:128 * (c + 1), :], ch=0)
    for c in range(32):
        em.dma("sync", convw2[:, 3 * c:3 * c + 3], f1_conv.ap()[128 * c:128 * (c + 1), :], ch=0)
        em.dma("sync", convb2[:, c:c + 1], f1_convb.ap()[128 * c:128 * (c + 1), :], ch=0)
    em.dma("sync", arow[:], w0_A.ap(), ch=0)
    em.dma("sync", scal[0:1, 8:9], f0_clipx.ap(), ch=0)
    em.dma("sync", scal[0:1, 9:10], f0_clipd.ap(), ch=0)
    for c in range(8):
        em.dma("gpsimd", d1col[:, c:c + 1],
               w1_D.ap()[0:1, 128 * c:128 * (c + 1)].rearrange("a (p b) -> (a p) b", b=1), ch=0)
    em.dma("gpsimd", drep0[0:1, 0:2048], w0_D.ap()[0:1, :], ch=0)
    em.dwait("scalar", "sync", 0)
    em.dwait("scalar", "gpsimd", 0)
    em.wait("scalar", "ini_g", 1)
    act(arow[:], arow[:], AF.Exp)
    em.inc(SE.drain(), "ini_s", 1)
    em.wait("vector", "ini_s", 1)
    vts(arow[:], arow[:], -1.0, None, ALU.mult)
    vcp(rmi[:], rmode[:])
    vcp(rmode[:], rmi[:])
    vts(rmode[:], rmode[:], -0.5, 0.5, ALU.mult, ALU.add)
    em.inc(VE.drain(), "ini_v", 1)
    em.wait("tensor", "ini_v", 1)
    em.dwait("tensor", "sync", 0)
    em.dwait("tensor", "gpsimd", 0)
    for j in range(4):
        TE.matmul(PP[:, 1024 + 512 * j:1024 + 512 * (j + 1)], onesr[:],
                  drep0[0:1, 512 * j:512 * (j + 1)], start=True, stop=True)
    em.inc(TE.drain(), "ini_t", 1)
    em.wait("vector", "ini_t", 1)
    vcp(eyef[:], eyeb[:])
    for j in range(4):
        vcp(drep0[:, 512 * j:512 * (j + 1)], PP[:, 1024 + 512 * j:1024 + 512 * (j + 1)])
    em.barrier()

    def emit_ln(l, resid):
        for po in range(8):
            sl = po % 2
            if po >= 2:
                em.wait("gpsimd", "lnT%d" % l, po - 1)
                em.wait("gpsimd", "lnS%d" % l, po - 1)
            em.dma("gpsimd", slot(A1, sl), resid.ap()[128 * po:128 * (po + 1), :], ch=sl)
            em.dwait("scalar", "gpsimd", ch=sl)
            act(slot(A2, sl), slot(A1, sl), AF.Square)
            em.inc(SE.drain(), "lnS%d" % l, 1)
            em.wait("tensor", "lnS%d" % l, po + 1)
            for tc in range(NT):
                i = po * NT + tc
                reg = 512 * (i % 2)
                if i >= 2:
                    em.wait("tensor", "lnAe%d" % l, i - 1)
                TE.matmul(PP[0:1, reg:reg + 512], onesb[:, 0:1],
                          slot(A1, sl)[:, 512 * tc:512 * (tc + 1)],
                          start=True, stop=True)
                TE.matmul(PP[32:33, 1024 + reg:1024 + reg + 512], onesb[:, 1:2],
                          slot(A2, sl)[:, 512 * tc:512 * (tc + 1)],
                          start=True, stop=True)
                em.inc(TE.drain(), "lnAm%d" % l, 1)
                em.wait("vector", "lnAm%d" % l, em.cnt["lnAm%d" % l])
                if po == 0:
                    vcp(srow[0][:, 512 * tc:512 * (tc + 1)], PP[0:1, reg:reg + 512])
                    vcp(srow[1][:, 512 * tc:512 * (tc + 1)], PP[32:33, 1024 + reg:1024 + reg + 512])
                else:
                    vtt(srow[0][:, 512 * tc:512 * (tc + 1)], srow[0][:, 512 * tc:512 * (tc + 1)],
                        PP[0:1, reg:reg + 512], ALU.add)
                    vtt(srow[1][:, 512 * tc:512 * (tc + 1)], srow[1][:, 512 * tc:512 * (tc + 1)],
                        PP[32:33, 1024 + reg:1024 + reg + 512], ALU.add)
                em.inc(VE.drain(), "lnAe%d" % l, 1)
            em.inc(TE.drain(), "lnT%d" % l, 1)
        em.wait("vector", "lnT%d" % l, 8)
        vts(srow[0][:, 0:T], srow[0][:, 0:T], 1.0 / D, None, ALU.mult)
        vts(srow[1][:, 0:T], srow[1][:, 0:T], 1.0 / D, 1e-5, ALU.mult, ALU.add)
        for hf in range((T + 2047) // 2048):
            hw = min(2048, T - 2048 * hf)
            vtt(PP[0:1, 0:hw], srow[0][:, 2048 * hf:2048 * hf + hw],
                srow[0][:, 2048 * hf:2048 * hf + hw], ALU.mult)
            vtt(srow[1][:, 2048 * hf:2048 * hf + hw],
                srow[1][:, 2048 * hf:2048 * hf + hw], PP[0:1, 0:hw], ALU.subtract)
        em.inc(VE.drain(), "lnV%d" % l, 1)
        em.wait("scalar", "lnV%d" % l, 1)
        act(srow[1][:, 0:T], srow[1][:, 0:T], AF.Sqrt)
        em.inc(SE.drain(), "lnS%d" % l, 1)
        em.wait("vector", "lnS%d" % l, 9)
        VE.reciprocal(srow[2][:, 0:T], srow[1][:, 0:T])
        VE.drain()
        for hf in range((T + 2047) // 2048):
            hw = min(2048, T - 2048 * hf)
            vcp(PP[0:1, 0:hw], srow[2][:, 2048 * hf:2048 * hf + hw])
            vtt(srow[3][:, 2048 * hf:2048 * hf + hw],
                srow[0][:, 2048 * hf:2048 * hf + hw], PP[0:1, 0:hw], ALU.mult)
        vts(srow[3][:, 0:T], srow[3][:, 0:T], -1.0, None, ALU.mult)
        vcp(srowbA[0:1, 0:T], srow[2][:, 0:T])
        vcp(srowbB[0:1, 0:T], srow[3][:, 0:T])
        em.inc(VE.drain(), "lnV%d" % l, 1)
        em.wait("tensor", "lnV%d" % l, 2)
        em.wait("gpsimd", "lnS%d" % l, 8)
        em.wait("gpsimd", "lnT%d" % l, 8)
        for po in range(8):
            sl = po % 2
            if po >= 2:
                em.wait("gpsimd", "lnN%d" % l, (po - 1) * NT)
            em.dma("gpsimd", slot(A2, sl), resid.ap()[128 * po:128 * (po + 1), :], ch=sl)
            em.dwait("vector", "gpsimd", ch=sl)
            for tc in range(NT):
                i = po * NT + tc
                reg, reg2 = 512 * (i % 2), 1024 + 512 * (i % 2)
                if i >= 2:
                    em.wait("tensor", "lnN%d" % l, i - 1)
                TE.matmul(PP[:, reg:reg + 512], onesr[:],
                          srowbA[0:1, 512 * tc:512 * (tc + 1)], start=True, stop=True)
                TE.matmul(PP[:, reg2:reg2 + 512], ones33b[32:33, 0:128],
                          srowbB[0:1, 512 * tc:512 * (tc + 1)], start=True, stop=True)
                em.inc(TE.drain(), "lnM%d" % l, 1)
                em.wait("vector", "lnM%d" % l)
                if po >= 2 and tc == 0:
                    em.dwait("vector", "sync", ch=2 + (po % 2))
                vtt(slot(A3, po % 2)[:, 512 * tc:512 * (tc + 1)],
                    slot(A2, sl)[:, 512 * tc:512 * (tc + 1)], PP[:, reg:reg + 512], ALU.mult)
                vtt(slot(A3, po % 2)[:, 512 * tc:512 * (tc + 1)],
                    slot(A3, po % 2)[:, 512 * tc:512 * (tc + 1)], PP[:, reg2:reg2 + 512], ALU.add)
                em.inc(VE.drain(), "lnN%d" % l, 1)
            if l == 0:
                VE.tensor_reduce(gmaxc[:, 1:2], slot(A3, po % 2), AX.X,
                                 ALU.max, apply_absolute_value=True)
                VE.drain()
                vtt(gmaxc[:, 0:1], gmaxc[:, 0:1], gmaxc[:, 1:2], ALU.max)
            em.inc(VE.drain(), "lnNv%d" % l, 1)
            em.wait("sync", "lnNv%d" % l, po + 1)
            em.dma("sync", d_xn[l].ap()[128 * po:128 * (po + 1), :], slot(A3, po % 2),
                   ch=2 + (po % 2))
        em.barrier()

    def emit_winx_conv():
        for k in range(8):
            ar = A1 if k < 4 else A2
            em.dma("gpsimd", slot(ar, k % 4), d_xn[0].ap()[128 * k:128 * (k + 1), :], ch=k % 4)
        for c in range(4):
            em.dwait("tensor", "gpsimd", ch=c)
        for po in range(16):
            wsl = po % 2
            if po >= 2:
                em.wait("gpsimd", "wcT", (po - 1) * NT)
            for k in range(8):
                em.dma("gpsimd", WB[:, wsl * 1024 + 128 * k: wsl * 1024 + 128 * (k + 1)],
                       w0_in.ap()[128 * k:128 * (k + 1), 128 * po:128 * (po + 1)], ch=4 + wsl)
            em.dwait("tensor", "gpsimd", ch=4 + wsl)
            for tc in range(NT):
                i = po * NT + tc
                reg = 512 * (i % 4)
                if em.cnt.get("wcE", 0) >= 4:
                    em.wait("tensor", "wcE", em.cnt["wcE"] - 3)
                for k in range(8):
                    ar = A1 if k < 4 else A2
                    TE.matmul(PP[:, reg:reg + 512],
                              WB[:, wsl * 1024 + 128 * k: wsl * 1024 + 128 * (k + 1)],
                              slot(ar, k % 4)[:, 512 * tc:512 * (tc + 1)],
                              start=(k == 0), stop=(k == 7))
                em.inc(TE.drain(), "wcT", 1)
                em.wait("vector", "wcT", em.cnt["wcT"])
                vcp(slotn(A3, 2, 512 * tc, 512), PP[:, reg:reg + 512])
                em.inc(VE.drain(), "wcE", 1)
            em.inc(VE.drain(), "wcV", 1)
            wc = convw0[:, 4 * po:4 * po + 4]
            em.dwait("vector", "sync", ch=6)
            VE.memset(A3[:, 2 * SW:2 * SW + PAD], 0.0)
            VE.drain()
            VE.memset(A3[:, 2 * SW + PAD + T:3 * SW], 0.0)
            VE.drain()
            vts(slot(A3, 0), slotn(A3, 2, 0, T), wc[:, 3:4], None, ALU.mult)
            vstt(slot(A3, 1), slotn(A3, 2, -1, T), wc[:, 2:3], slot(A3, 0), ALU.mult, ALU.add)
            vstt(slot(A3, 0), slotn(A3, 2, -2, T), wc[:, 1:2], slot(A3, 1), ALU.mult, ALU.add)
            vstt(slot(A3, 1), slotn(A3, 2, -3, T), wc[:, 0:1], slot(A3, 0), ALU.mult, ALU.add)
            vts(slot(A3, 1), slot(A3, 1), convb0[:, po:po + 1], None, ALU.add)
            em.inc(VE.drain(), "wcVc", 1)
            em.wait("scalar", "wcVc", po + 1)
            act(slot(A3, 0), slot(A3, 1), AF.Silu)
            em.inc(SE.drain(), "wcS", 1)
            em.wait("sync", "wcS", po + 1)
            em.dma("sync", d_xc0.ap()[128 * po:128 * (po + 1), :], slot(A3, 0), ch=6)
            em.wait("tensor", "wcS", po + 1)
            for b4 in range(T // 512):
                if em.cnt.get("wcTV", 0):
                    em.wait("tensor", "wcTV", em.cnt["wcTV"])
                for j in range(4):
                    TE.transpose(PPb[:, 128 * j:128 * (j + 1)],
                                 slotn(A3, 0, 512 * b4 + 128 * j, 128), eyeb[:])
                em.inc(TE.drain(), "wcTT", 1)
                em.wait("vector", "wcTT", em.cnt["wcTT"])
                tb = trst if b4 % 2 == 0 else trstB
                em.dwait("vector", "sync", ch=7 if b4 % 2 == 0 else 3)
                for j in range(4):
                    vcp(tb[:, 128 * j:128 * (j + 1)],
                        PPb[:, 128 * j:128 * (j + 1)])
                em.inc(VE.drain(), "wcTV", 1)
                em.wait("sync", "wcTV", em.cnt["wcTV"])
                em.dma("sync",
                       d_xct0.ap()[512 * b4:512 * b4 + 512, 128 * po:128 * (po + 1)]
                       .rearrange("(j p) f -> p j f", p=128),
                       tb[:, 0:512].rearrange("p (j f) -> p j f", f=128),
                       ch=7 if b4 % 2 == 0 else 3)
        em.barrier()

    def emit_wx():
        GE.memset(maskrow[0:1, 0:T], 1.0)
        GE.drain()
        GE.memset(maskrow[0:1, 0:T].rearrange("a (c w) -> a c w", w=128)[:, :, 0:1], 0.0)
        em.inc(GE.drain(), "wx_g", 1)
        for e in ("vector",):
            em.wait(e, "wx_g", 1)
        for k in range(16):
            em.dma("gpsimd", WA[:, 132 * k:132 * k + 129],
                   w0_x.ap()[128 * k:128 * (k + 1), :], ch=k % 4)
        for c in range(4):
            em.dwait("tensor", "gpsimd", ch=c)
        for tc in range(NT):
            sl = tc % 2
            ar = A1 if sl == 0 else A2
            if tc >= 2:
                em.wait("sync", "xpT", tc - 1)
            for k in range(16):
                em.dma("sync", ar[:, 512 * k:512 * k + 512],
                       d_xc0.ap()[128 * k:128 * (k + 1), 512 * tc:512 * (tc + 1)], ch=sl)
            em.dwait("tensor", "sync", ch=sl)
            if tc >= 2:
                em.wait("tensor", "xpV", tc - 1)
                em.wait("tensor", "xpVsp", tc - 1)
            for k in range(16):
                TE.matmul(PP[0:128, 512 * (tc % 2):512 * (tc % 2) + 512],
                          WA[:, 132 * k:132 * k + 128], ar[:, 512 * k:512 * k + 512],
                          start=(k == 0), stop=(k == 15))
            for k in range(16):
                TE.matmul(PP[0:1, 1024 + 512 * (tc % 2):1024 + 512 * (tc % 2) + 512],
                          WA[:, 132 * k + 128:132 * k + 129], ar[:, 512 * k:512 * k + 512],
                          start=(k == 0), stop=(k == 15))
            em.inc(TE.drain(), "xpT", 1)
            em.wait("vector", "xpT", tc + 1)
            vcp(ssmBC[:, 512 * tc:512 * (tc + 1)],
                PP[0:128, 512 * (tc % 2):512 * (tc % 2) + 512])
            em.inc(VE.drain(), "xpV", 1)
            em.wait("scalar", "xpV", tc + 1)
            spsrc = PP[0:1, 1024 + 512 * (tc % 2):1024 + 512 * (tc % 2) + 512]
            sptmp = mflat[0:1, 0:512]
            em.wait("scalar", "xpVsp", em.cnt.get("xpVsp", 0))
            act(sptmp, spsrc, AF.Abs)
            act(sptmp, sptmp, AF.Exp, scale=-1.0)
            em.inc(SE.drain(), "xpSa", 1)
            em.wait("vector", "xpSa", em.cnt["xpSa"])
            vts(sptmp, sptmp, 1.0, None, ALU.add)
            em.inc(VE.drain(), "xpVa", 1)
            em.wait("scalar", "xpVa", em.cnt["xpVa"])
            act(sptmp, sptmp, AF.Ln)
            em.inc(SE.drain(), "xpS", 1)
            em.wait("vector", "xpS", tc + 1)
            vstt(dtrow[0:1, 512 * tc:512 * (tc + 1)], spsrc, 0.0, sptmp, ALU.max, ALU.add)
            em.inc(VE.drain(), "xpVsp", 1)
        em.wait("vector", "xpVsp", NT)
        for hf in range((T + 2047) // 2048):
            hw = min(2048, T - 2048 * hf)
            vcp(PP[0:1, 0:hw], maskrow[0:1, 2048 * hf:2048 * hf + hw])
            VE.tensor_tensor_scan(crow[0:1, 2048 * hf:2048 * hf + hw], PP[0:1, 0:hw],
                                  dtrow[0:1, 2048 * hf:2048 * hf + hw], 0.0,
                                  ALU.mult, ALU.add)
            VE.drain()
        VE.tensor_copy(erow[:], bass.AP(ROWS, 31, [[FW, 1], [32, NCH * 4]]))
        VE.drain()
        VE.tensor_copy(bass.AP(brow, 1, [[NCH * 4, 1], [4, NCH], [1, 3]]),
                       bass.AP(erow, 0, [[NCH * 4, 1], [4, NCH], [1, 3]]))
        VE.drain()
        em.inc(VE.drain(), "sp_v", 1)
        em.wait("tensor", "sp_v", 1)
        for j in range(NT):
            if j >= 2:
                em.wait("tensor", "sp_bv", j - 1)
            TE.matmul(PP[0:64, 512 * (j % 2):512 * (j % 2) + 512], ones33f[32:33, 0:64],
                      dtrow[0:1, 512 * j:512 * (j + 1)], start=True, stop=True)
            em.inc(TE.drain(), "sp_t", 1)
            em.wait("vector", "sp_t", j + 1)
            vtt(bdt0[:, 512 * j:512 * (j + 1)], ssmBC[64:128, 512 * j:512 * (j + 1)],
                PP[0:64, 512 * (j % 2):512 * (j % 2) + 512], ALU.mult)
            em.inc(VE.drain(), "sp_bv", 1)
        em.barrier()

    def emit_scan0():
        em.dma("sync", maskrow[0:1, 0:1024], cst_mask.ap(), ch=6)
        em.dwait("tensor", "sync", ch=6)
        for j in range(2):
            TE.matmul(PP[0:64, 512 * j:512 * (j + 1)], ones33f[64:65, 0:64],
                      maskrow[0:1, 512 * j:512 * (j + 1)], start=True, stop=True)
        em.inc(TE.drain(), "s0_mk", 1)
        em.wait("vector", "s0_mk", 1)
        vcp(mask64[:, 0:512], PP[0:64, 0:512])
        vcp(mask64[:, 512:1024], PP[0:64, 512:1024])
        em.inc(VE.drain(), "s0_mkv", 1)
        for e in ("tensor", "scalar", "gpsimd", "sync"):
            em.wait(e, "s0_mkv", 1)
        GE.memset(H0s[:], 0.0)
        GE.drain()
        GE.memset(WB[0:64, 0:2048], 0.0)
        em.inc(GE.drain(), "s0_g", 1)
        for e in ("tensor", "vector", "scalar", "sync"):
            em.wait(e, "s0_g", 1)
        for c in range(NCH):
            xsl = c % 2
            ar = A1 if xsl == 0 else A2
            if c >= 2:
                em.wait("sync", "s0_hb", c - 1)
                em.wait("sync", "s0_umm2", c - 1)
            em.dma("sync", ar[:, 0:2048], d_xct0.ap()[128 * c:128 * (c + 1), :], ch=xsl)
            XT = ar[:, 0:2048]
            cc = crow[0:1, 128 * c:128 * (c + 1)]
            Csl = ssmBC[0:64, 128 * c:128 * (c + 1)]
            Bdsl = bdt0[0:64, 128 * c:128 * (c + 1)]
            if c >= 1:
                em.wait("vector", "s0_amm", c)
            vtt(crowD[0:1, 0:128], cc,
                bass.AP(brow, 4 * c, [[NCH * 4, 1], [1, 4], [0, 32]]), ALU.subtract)
            vtt(crowD[0:1, 128:256],
                bass.AP(erow, 4 * c, [[NCH * 4, 1], [1, 4], [0, 32]]), cc, ALU.subtract)
            vtt(crowD[0:1, 384:512],
                bass.AP(erow, 4 * c + 3, [[NCH * 4, 1], [0, 128]]), cc, ALU.subtract)
            VE.tensor_copy(crowD[0:1, 512:515], bass.AP(brow, 4 * c + 1, [[NCH * 4, 1], [1, 3]]))
            VE.drain()
            VE.tensor_copy(crowD[0:1, 515:517], bass.AP(brow, 4 * c + 2, [[NCH * 4, 1], [1, 2]]))
            VE.drain()
            VE.tensor_copy(crowD[0:1, 517:518], bass.AP(brow, 4 * c + 3, [[NCH * 4, 1], [1, 1]]))
            VE.drain()
            VE.tensor_copy(crowD[0:1, 520:523], bass.AP(erow, 4 * c, [[NCH * 4, 1], [0, 3]]))
            VE.drain()
            VE.tensor_copy(crowD[0:1, 523:525], bass.AP(erow, 4 * c + 1, [[NCH * 4, 1], [0, 2]]))
            VE.drain()
            VE.tensor_copy(crowD[0:1, 525:526], bass.AP(erow, 4 * c + 2, [[NCH * 4, 1], [0, 1]]))
            VE.drain()
            vtt(crowD[0:1, 512:518], crowD[0:1, 512:518], crowD[0:1, 520:526], ALU.subtract)
            vts(growA[0:1, 0:64], arow[0:1, 0:64], erow[0:1, 4 * c + 3:4 * c + 4],
                None, ALU.mult)
            em.inc(VE.drain(), "s0_rows", 1)
            em.wait("tensor", "s0_rows", c + 1)
            TE.matmul(PP[0:64, 3072:3200], arow[0:1, 0:64], crowD[0:1, 0:128],
                      start=True, stop=True)
            TE.matmul(PP[0:64, 3200:3328], arow[0:1, 0:64], crowD[0:1, 128:256],
                      start=True, stop=True)
            TE.matmul(PP[0:64, 3328:3456], arow[0:1, 0:64], cc,
                      start=True, stop=True)
            TE.matmul(PP[0:128, 3456:3520], crowD[0:1, 384:512], arow[0:1, 0:64],
                      start=True, stop=True)
            TE.matmul(PP[0:64, 3520:3526], arow[0:1, 0:64], crowD[0:1, 512:518],
                      start=True, stop=True)
            TE.matmul(PP[0:64, 2560:2688], arow[0:1, 0:64], cc, start=True, stop=True)
            TE.transpose(PPb[0:128, 512:576], Bdsl, eyeb[0:64, 0:64])
            em.inc(TE.drain(), "s0_amm", 1)
            em.wait("scalar", "s0_amm", c + 1)
            act(mid6[:, 0:6], PP[0:64, 3520:3526], AF.Exp)
            act(accs[:, 0:128], PP[0:64, 2560:2688], AF.Copy)
            act(Cbt[:], PP[0:64, 3072:3200], AF.Exp)
            act(Bbt[:], PP[0:64, 3200:3328], AF.Exp)
            act(Cct[:], PP[0:64, 3328:3456], AF.Exp)
            act(Bht[:], PP[0:128, 3456:3520], AF.Exp)
            act(growB[0:1, 0:64], growA[0:1, 0:64], AF.Exp)
            em.inc(SE.drain(), "s0_exp", 1)
            em.wait("vector", "s0_exp", c + 1)
            vtt(Cbt[:], Cbt[:], Csl, ALU.mult)
            vtt(Bbt[:], Bbt[:], Bdsl, ALU.mult)
            vtt(Cct[:], Cct[:], Csl, ALU.mult)
            vtt(Bht[:], Bht[:], PPb[0:128, 512:576], ALU.mult)
            vtt(FY[0:64, 0:4096],
                bass.AP(accs, 0, [[128, 64], [32, 4], [0, 32], [1, 32]]),
                bass.AP(accs, 0, [[128, 64], [32, 4], [1, 32], [0, 32]]),
                ALU.subtract)
            vtt(FY[0:64, 0:4096], FY[0:64, 0:4096],
                bass.AP(A3, EXT, [[A3W, 64], [0, 4], [1, 1024]]), ALU.add)
            em.inc(VE.drain(), "s0_fac", 1)
            em.wait("scalar", "s0_fac", c + 1)
            act(ediag[:, 0:4096], FY[0:64, 0:4096], AF.Exp)
            em.inc(SE.drain(), "s0_ed", 1)
            em.wait("vector", "s0_ed", c + 1)
            vtt(ediag[:, 0:4096], ediag[:, 0:4096],
                bass.AP(WA, 4352 + 128 * c, [[8448, 64], [32, 4], [1, 32], [0, 32]]),
                ALU.mult)
            vtt(ediag[:, 0:4096], ediag[:, 0:4096],
                bass.AP(ssmBC, 128 * c, [[T, 64], [32, 4], [0, 32], [1, 32]]),
                ALU.mult)
            for j in range(3):
                vtt(CbM[:, 32 * (j + 1):128], Cbt[:, 32 * (j + 1):128],
                    bass.AP(mid6, [0, 3, 5][j], [[8, 64], [1, 3 - j], [0, 32]]), ALU.mult)
            em.inc(VE.drain(), "s0_pd", 1)
            em.wait("tensor", "s0_pd", c + 1)
            for j in range(3):
                TE.matmul(PP[32 * j:32 * j + 32, 2048 + 32 * (j + 1):2048 + 128],
                          Bbt[:, 32 * j:32 * j + 32], CbM[:, 32 * (j + 1):128],
                          start=True, stop=True, tile_position=(0, 32 * j))
            em.inc(TE.drain(), "s0_offd", 1)
            if c >= 1:
                em.wait("tensor", "s0_yev", c)
            for w2 in range(2):
                if not (c == 0 and w2 == 0):
                    em.wait("tensor", "s0_mf", em.cnt["s0_mf"])
                for j in range(4):
                    kk = w2 * 2 + j // 2
                    half = j % 2
                    TE.matmul(PP[0:1, 512 * j:512 * (j + 1)],
                              onesb[0:64, 2 + half:3 + half],
                              ediag[:, 1024 * kk + 512 * half:1024 * kk + 512 * (half + 1)],
                              start=True, stop=True)
                em.inc(TE.drain(), "s0_dr", 1)
                em.wait("scalar", "s0_dr", em.cnt["s0_dr"])
                for j in range(4):
                    kk = w2 * 2 + j // 2
                    half = j % 2
                    act(mflat[0:1, 1024 * kk + 512 * half:1024 * kk + 512 * (half + 1)],
                        PP[0:1, 512 * j:512 * (j + 1)], AF.Copy)
                em.inc(SE.drain(), "s0_mf", 1)
            em.wait("vector", "s0_offd", c + 1)
            em.wait("vector", "s0_mf", em.cnt["s0_mf"])
            VE.memset(Mfull[:], 0.0)
            VE.drain()
            for j in range(3):
                vcp(Mfull[32 * j:32 * j + 32, 32 * (j + 1):128],
                    PP[32 * j:32 * j + 32, 2048 + 32 * (j + 1):2048 + 128])
            em.inc(VE.drain(), "s0_mz", 1)
            em.wait("gpsimd", "s0_mz", c + 1)
            for k in range(4):
                em.dma("gpsimd", Mfull[32 * k:32 * k + 32, 32 * k:32 * k + 32],
                       mflat[0:1, 1024 * k:1024 * (k + 1)], ch=2 + (k % 2))
            em.dwait("tensor", "gpsimd", ch=2)
            em.dwait("tensor", "gpsimd", ch=3)
            em.dwait("tensor", "sync", ch=xsl)
            if c >= 1:
                em.wait("tensor", "s0_yev", c)
            em.wait("tensor", "s0_mf", em.cnt["s0_mf"])
            for nb in range(4):
                TE.matmul(PP[:, 512 * nb:512 * (nb + 1)], Mfull[:],
                          XT[:, 512 * nb:512 * (nb + 1)], start=True, stop=False)
                TE.matmul(PP[:, 512 * nb:512 * (nb + 1)], Cct[:],
                          WB[0:64, 512 * nb:512 * (nb + 1)], start=False, stop=True)
            em.inc(TE.drain(), "s0_ymm", 1)
            TE.matmul(PP[0:64, 2544:2545], growB[0:1, 0:64], onesf[0:1, 0:1],
                      start=True, stop=True)
            em.wait("tensor", "s0_mf", em.cnt["s0_mf"])
            TE.matmul(PP[0:64, 2560:3072], Bht[:], XT[:, 0:512], start=True, stop=True)
            TE.matmul(PP[0:64, 3072:3584], Bht[:], XT[:, 512:1024], start=True, stop=True)
            em.inc(TE.drain(), "s0_umm", 1)
            em.wait("vector", "s0_ymm", c + 1)
            vtt(y2s[:, 0:2048], XT, drep0[:, 0:2048], ALU.mult)
            for nb in range(4):
                vtt(y2s[:, 512 * nb:512 * (nb + 1)], y2s[:, 512 * nb:512 * (nb + 1)],
                    PP[:, 512 * nb:512 * (nb + 1)], ALU.add)
            em.inc(VE.drain(), "s0_yev", 1)
            em.wait("vector", "s0_umm", c + 1)
            vstt(H0s[:, 0:512], H0s[:, 0:512], PP[0:64, 2544:2545], PP[0:64, 2560:3072],
                 ALU.mult, ALU.add)
            vstt(H0s[:, 512:1024], H0s[:, 512:1024], PP[0:64, 2544:2545],
                 PP[0:64, 3072:3584], ALU.mult, ALU.add)
            em.inc(VE.drain(), "s0_hup", 1)
            em.wait("tensor", "s0_hup", c + 1)
            TE.matmul(PP[0:64, 2560:3072], Bht[:], XT[:, 1024:1536], start=True, stop=True)
            TE.matmul(PP[0:64, 3072:3584], Bht[:], XT[:, 1536:2048], start=True, stop=True)
            em.inc(TE.drain(), "s0_umm2", 1)
            em.wait("vector", "s0_umm2", c + 1)
            vstt(H0s[:, 1024:1536], H0s[:, 1024:1536], PP[0:64, 2544:2545],
                 PP[0:64, 2560:3072], ALU.mult, ALU.add)
            vstt(H0s[:, 1536:2048], H0s[:, 1536:2048], PP[0:64, 2544:2545],
                 PP[0:64, 3072:3584], ALU.mult, ALU.add)
            vcp(WB[0:64, 0:2048], H0s[:])
            em.inc(VE.drain(), "s0_hb", 1)
            em.wait("tensor", "s0_hb", c + 1)
            for j4 in range(4):
                if em.cnt.get("s0_ytv", 0):
                    em.wait("tensor", "s0_ytv", em.cnt["s0_ytv"])
                for j in range(4):
                    TE.transpose(PPb[:, 128 * j:128 * (j + 1)],
                                 y2s[:, 512 * j4 + 128 * j:512 * j4 + 128 * (j + 1)], eyeb[:])
                em.inc(TE.drain(), "s0_ytt", 1)
                em.wait("vector", "s0_ytt", em.cnt["s0_ytt"])
                if not (c == 0 and j4 == 0):
                    em.dwait("vector", "sync", ch=4 + (j4 % 2))
                tb = trst if j4 % 2 == 0 else trstB
                for j in range(4):
                    vcp(tb[:, 128 * j:128 * (j + 1)],
                        PPb[:, 128 * j:128 * (j + 1)])
                em.inc(VE.drain(), "s0_ytv", 1)
                em.wait("sync", "s0_ytv", em.cnt["s0_ytv"])
                em.dma("sync",
                       d_y2f0.ap()[512 * j4:512 * j4 + 512, 128 * c:128 * (c + 1)]
                       .rearrange("(j p) t -> p j t", p=128),
                       tb[:, 0:512].rearrange("p (j t) -> p j t", t=128), ch=4 + (j4 % 2))
        em.barrier()

    def emit_scale0():
        fold_col_max(gmaxc[:, 0:1], 16)
        allreduce_max(16, 17)
        act(scal[0:1, 18:19], scal[0:1, 8:9], AF.Sigmoid)
        em.inc(SE.drain(), "sc0_s", 1)
        em.wait("vector", "sc0_s", 1)
        vtt(scal[0:1, 0:1], scal[0:1, 18:19], scal[0:1, 17:18], ALU.mult)
        vts(scal[0:1, 0:1], scal[0:1, 0:1], 1.0 / 127.0, None, ALU.mult)
        VE.reciprocal(scal[0:1, 1:2], scal[0:1, 0:1])
        VE.drain()
        em.inc(VE.drain(), "sc0_v", 1)
        em.wait("tensor", "sc0_v", 1)
        col_from_scalar(0, scal[0:1, 0:1])
        col_from_scalar(1, scal[0:1, 1:2])
        em.barrier()

    def emit_quant_wide(src, dst, R, name, lo, hi, mode, inv_idx=0, post_idx=0):
        nR = R // 128
        HB = min(512, T)
        tiles = [(h, po) for h in range(T // HB) for po in range(nR)]
        nt = len(tiles)
        dsts = [y2s[:, 0:HB], y2s[:, 512:512 + HB]]

        def pre(idx):
            h, po = tiles[idx]
            g = idx % 2
            yv = ytl[g][:, 0:HB]
            sg = FA[:, 2048 * g:2048 * g + HB]
            if mode == "row" and po == 0:
                em.wait("tensor", "%s_q" % name, idx)
                for b in range(HB // 512):
                    TE.matmul(PP[:, 1024 + 512 * b:1536 + 512 * b], onesf[0:1, 0:128],
                              qscr[0:1, HB * h + 512 * b:HB * h + 512 * (b + 1)],
                              start=True, stop=True)
                    TE.matmul(PP[:, 2048 + 512 * b:2560 + 512 * b], ones33f[32:33, 0:128],
                              qinr[0:1, HB * h + 512 * b:HB * h + 512 * (b + 1)],
                              start=True, stop=True)
                em.inc(TE.drain(), "%s_r" % name, 1)
            if idx >= 2:
                em.wait("gpsimd", "%s_q" % name, idx - 1)
            em.dma("gpsimd", yv, src.ap()[128 * po:128 * (po + 1), HB * h:HB * (h + 1)],
                   ch=g)
            em.dwait("vector", "gpsimd", ch=g)
            if mode == "row":
                em.wait("vector", "%s_r" % name, h + 1)
                for b in range(HB // 512):
                    vtt(yv[:, 512 * b:512 * (b + 1)], yv[:, 512 * b:512 * (b + 1)],
                        PP[:, 1024 + 512 * b:1536 + 512 * b], ALU.mult)
            else:
                vts(yv, yv, colsc[:, inv_idx:inv_idx + 1], None, ALU.mult)
            em.inc(VE.drain(), "%s_v" % name, 1)
            em.wait("scalar", "%s_v" % name, em.cnt["%s_v" % name])
            act(sg, yv, AF.Sign)
            em.inc(SE.drain(), "%s_s" % name, 1)

        def post(idx):
            h, po = tiles[idx]
            g = idx % 2
            yv = ytl[g][:, 0:HB]
            sg = FA[:, 2048 * g:2048 * g + HB]
            it = itl1[:, 0:HB]
            em.wait("vector", "%s_s" % name, idx + 1)
            vstt(yv, sg, rmode[:, 0:1], yv, ALU.mult, ALU.add)
            vcp(it, yv)
            vts(it, it, float(lo), float(hi), ALU.max, ALU.min)
            em.dwait("vector", "sync", ch=2 + g)
            if mode == "row":
                for b in range(HB // 512):
                    vtt(dsts[g][:, 512 * b:512 * (b + 1)], it[:, 512 * b:512 * (b + 1)],
                        PP[:, 2048 + 512 * b:2560 + 512 * b], ALU.mult)
            else:
                vts(dsts[g], it, colsc[:, post_idx:post_idx + 1], None, ALU.mult)
            em.inc(VE.drain(), "%s_q" % name, 1)
            em.wait("sync", "%s_q" % name, idx + 1)
            em.dma("sync", dst.ap()[128 * po:128 * (po + 1), HB * h:HB * (h + 1)],
                   dsts[g], ch=2 + g)

        PIPE = True
        if PIPE:
            emitted = 0
            for i in range(nt):
                while emitted <= i + 1 and emitted < nt:
                    j = emitted
                    if mode == "row" and tiles[j][1] == 0 and j > 0 and j > i:
                        break
                    pre(j)
                    emitted += 1
                post(i)
        else:
            for i in range(nt):
                pre(i)
                post(i)
        em.barrier()

    def emit_quant_global(src, dst, R, inv_idx, post_idx, lo, hi, name):
        nR = R // 128
        for po in range(nR):
            for tc in range(NT):
                par = (po * NT + tc) % 2
                if po * NT + tc >= 2:
                    em.dwait("gpsimd", "sync", ch=par)
                em.dma("gpsimd", ytl[par][:, 0:512],
                       src.ap()[128 * po:128 * (po + 1), 512 * tc:512 * (tc + 1)], ch=par)
                em.dwait("vector", "gpsimd", ch=par)
                vts(ytl[par][:, 0:512], ytl[par][:, 0:512], colsc[:, inv_idx:inv_idx + 1],
                    None, ALU.mult)
                qround(y2s[:, 512 * par:512 * (par + 1)],
                       par, lo, hi, post_col=colsc[:, post_idx:post_idx + 1])
                em.inc(VE.drain(), "%s_v" % name, 1)
                em.wait("sync", "%s_v" % name, em.cnt["%s_v" % name])
                em.dma("sync", dst.ap()[128 * po:128 * (po + 1), 512 * tc:512 * (tc + 1)],
                       y2s[:, 512 * par:512 * (par + 1)], ch=par)
        em.barrier()

    d_xq0 = d_y1

    def emit_xq0():
        emit_quant_global(d_xn[0], d_xq0, D, 1, 0, -128.0, 127.0, "xq0")

    def emit_gate_wout0():
        d_g0 = d_xc0
        for k in range(8):
            ar = A1 if k < 4 else A2
            em.dma("gpsimd", slot(ar, k % 4), d_xn[0].ap()[128 * k:128 * (k + 1), :],
                   ch=4 + k % 2)
        em.dwait("tensor", "gpsimd", ch=4)
        em.dwait("tensor", "gpsimd", ch=5)
        for po in range(16):
            wsl = po % 2
            if po >= 2:
                em.wait("gpsimd", "gaT", (po - 1) * NT)
            for k in range(8):
                em.dma("gpsimd", WB[:, wsl * 1024 + 128 * k:wsl * 1024 + 128 * (k + 1)],
                       w0_in.ap()[128 * k:128 * (k + 1), DI + 128 * po:DI + 128 * (po + 1)],
                       ch=6 + wsl)
            em.dwait("tensor", "gpsimd", ch=6 + wsl)
            for tc in range(NT):
                i = po * NT + tc
                reg = 512 * (i % 4)
                if i >= 4:
                    em.wait("tensor", "gaS", i - 3)
                for k in range(8):
                    ar = A1 if k < 4 else A2
                    TE.matmul(PP[:, reg:reg + 512],
                              WB[:, wsl * 1024 + 128 * k:wsl * 1024 + 128 * (k + 1)],
                              slot(ar, k % 4)[:, 512 * tc:512 * (tc + 1)],
                              start=(k == 0), stop=(k == 7))
                em.inc(TE.drain(), "gaT", 1)
                if i >= 4:
                    em.wait("gpsimd", "gaV", i - 3)
                em.dma("gpsimd", y2s[:, 512 * (i % 4):512 * (i % 4) + 512],
                       d_y2f0.ap()[128 * po:128 * (po + 1), 512 * tc:512 * (tc + 1)], ch=i % 4)
                em.wait("scalar", "gaT", i + 1)
                em.dwait("scalar", "sync", ch=i % 4)
                act(A3[:, 512 * (i % 8):512 * (i % 8) + 512], PP[:, reg:reg + 512], AF.Silu)
                em.inc(SE.drain(), "gaS", 1)
                em.wait("vector", "gaS", i + 1)
                em.dwait("vector", "gpsimd", ch=i % 4)
                vtt(A3[:, 512 * (i % 8):512 * (i % 8) + 512],
                    A3[:, 512 * (i % 8):512 * (i % 8) + 512],
                    y2s[:, 512 * (i % 4):512 * (i % 4) + 512], ALU.mult)
                em.inc(VE.drain(), "gaV", 1)
                em.wait("sync", "gaV", i + 1)
                em.dma("sync", d_g0.ap()[128 * po:128 * (po + 1), 512 * tc:512 * (tc + 1)],
                       A3[:, 512 * (i % 8):512 * (i % 8) + 512], ch=i % 4)
        em.wait("gpsimd", "gaT", 16 * NT)
        for k in range(16):
            em.dma("gpsimd", A2[:, 1024 * k:1024 * (k + 1)],
                   w0_out.ap()[128 * k:128 * (k + 1), :], ch=4 + k % 2)
        em.dwait("tensor", "gpsimd", ch=4)
        em.dwait("tensor", "gpsimd", ch=5)
        em.wait("gpsimd", "gaT", 16 * NT)
        em.wait("tensor", "gaS", 16 * NT)
        em.wait("scalar", "gaV", 16 * NT)
        for c in range(4):
            em.dwait("gpsimd", "sync", ch=c)
        for tc in range(NT):
            if tc > 0:
                em.wait("gpsimd", "gbT", max(0, (tc - 1) * 8))
            for k in range(16):
                em.dma("gpsimd", A1[:, 8192 * (tc % 2) + 512 * k:8192 * (tc % 2) + 512 * (k + 1)],
                       d_g0.ap()[128 * k:128 * (k + 1), 512 * tc:512 * (tc + 1)], ch=k % 4)
            for c in range(4):
                em.dwait("tensor", "gpsimd", ch=c)
            for po2 in range(8):
                i2 = tc * 8 + po2
                reg = 512 * (i2 % 4)
                if i2 >= 4:
                    em.wait("tensor", "gbS", i2 - 3)
                for k in range(16):
                    TE.matmul(PP[:, reg:reg + 512],
                              A2[:, 1024 * k + 128 * po2:1024 * k + 128 * (po2 + 1)],
                              A1[:, 8192 * (tc % 2) + 512 * k:8192 * (tc % 2) + 512 * (k + 1)],
                              start=(k == 0), stop=(k == 15))
                em.inc(TE.drain(), "gbT", 1)
                em.wait("scalar", "gbT", i2 + 1)
                em.dwait("scalar", "sync", ch=i2 % 4)
                act(y2s[:, 512 * (i2 % 4):512 * (i2 % 4) + 512], PP[:, reg:reg + 512], AF.Copy)
                em.inc(SE.drain(), "gbS", 1)
                em.wait("sync", "gbS", i2 + 1)
                em.dma("sync", d_s[0].ap()[128 * po2:128 * (po2 + 1), 512 * tc:512 * (tc + 1)],
                       y2s[:, 512 * (i2 % 4):512 * (i2 % 4) + 512], ch=i2 % 4)
        em.barrier()

    def emit_wquant_row(w_dram, dst_dram, K, PO, name):
        nK = K // 128
        nI = (PO + 127) // 128
        for k in range(nK):
            ch = k % 2
            buf = FY if k == 0 else FA
            em.dma("sync", buf[:, 0:PO], w_dram.ap()[128 * k:128 * (k + 1), :], ch=ch)
            em.dwait("vector", "sync", ch=ch)
            em.dwait("scalar", "sync", ch=ch)
            em.wait("scalar", "%s_f" % name, em.cnt.get("%s_f" % name, 0))
            if k == 0:
                act(FY[:, 0:PO], FY[:, 0:PO], AF.Abs)
            else:
                act(FA[:, 0:PO], FA[:, 0:PO], AF.Abs)
            em.inc(SE.drain(), "%s_fa" % name, 1)
            em.wait("vector", "%s_fa" % name, em.cnt["%s_fa" % name])
            if k > 0:
                vtt(FY[:, 0:PO], FY[:, 0:PO], FA[:, 0:PO], ALU.max)
            em.inc(VE.drain(), "%s_f" % name, 1)
            if k + 1 < nK:
                em.wait("sync", "%s_f" % name, k + 1)
        em.wait("tensor", "%s_f" % name, nK)
        for i in range(nI):
            w = min(128, PO - 128 * i)
            if i:
                em.wait("tensor", "%s_c" % name, i)
                em.wait("tensor", "%s_r" % name, i)
            TE.transpose(PP[0:w, 2048:2176], FY[:, 128 * i:128 * i + w], eyef[:])
            em.inc(TE.drain(), "%s_t" % name, 1)
            em.wait("vector", "%s_t" % name, em.cnt["%s_t" % name])
            VE.tensor_reduce(colsc[0:w, 8:9], PP[0:w, 2048:2176], AX.X, ALU.max)
            VE.drain()
            vts(colsc[0:w, 8:9], colsc[0:w, 8:9], 1.0 / 127.0, None, ALU.mult)
            em.inc(VE.drain(), "%s_c" % name, 1)
            em.wait("tensor", "%s_c" % name, em.cnt["%s_c" % name])
            TE.transpose(PP[0:1, 2176:2304], colsc[0:128, 8:9], eyef[:])
            em.inc(TE.drain(), "%s_t2" % name, 1)
            em.wait("scalar", "%s_t2" % name, em.cnt["%s_t2" % name])
            act(qscr[0:1, 128 * i:128 * i + w], PP[0:1, 2176:2176 + w], AF.Copy)
            em.inc(SE.drain(), "%s_r" % name, 1)
        em.wait("vector", "%s_r" % name, nI)
        VE.reciprocal(qinr[0:1, 0:PO], qscr[0:1, 0:PO])
        VE.drain()
        em.inc(VE.drain(), "%s_rr" % name, 1)
        em.wait("tensor", "%s_rr" % name, 1)
        em.barrier()
        nB = (PO + 511) // 512
        tiles = [(b, k) for b in range(nB) for k in range(nK)]
        nt = len(tiles)
        dsts = [y2s[:, 0:512], A3[:, EXT:EXT + 512]]

        def wq_pre(idx):
            b, k = tiles[idx]
            w = min(512, PO - 512 * b)
            g = idx % 2
            yv = ytl[g][:, 0:w]
            sg = FA[:, 2048 * g:2048 * g + w]
            if k == 0:
                em.wait("tensor", "%s_q" % name, idx)
                TE.matmul(PP[:, 2048:2048 + w], ones33f[32:33, 0:128],
                          qinr[0:1, 512 * b:512 * b + w], start=True, stop=True)
                TE.matmul(PP[:, 3072:3072 + w], onesf[0:1, 0:128],
                          qscr[0:1, 512 * b:512 * b + w], start=True, stop=True)
                em.inc(TE.drain(), "%s_b" % name, 1)
            if em.cnt.get("%s_q" % name, 0) >= 1:
                em.wait("sync", "%s_q" % name, em.cnt["%s_q" % name])
            em.dma("sync", yv, w_dram.ap()[128 * k:128 * (k + 1), 512 * b:512 * b + w],
                   ch=g)
            em.dwait("vector", "sync", ch=g)
            em.wait("vector", "%s_b" % name, (b + 1))
            vtt(yv, yv, PP[:, 2048:2048 + w], ALU.mult)
            em.inc(VE.drain(), "%s_v" % name, 1)
            em.wait("scalar", "%s_v" % name, em.cnt["%s_v" % name])
            act(sg, yv, AF.Sign)
            em.inc(SE.drain(), "%s_s" % name, 1)

        def wq_post(idx):
            b, k = tiles[idx]
            w = min(512, PO - 512 * b)
            g = idx % 2
            yv = ytl[g][:, 0:w]
            sg = FA[:, 2048 * g:2048 * g + w]
            it = itl1[:, 0:w]
            em.wait("vector", "%s_s" % name, idx + 1)
            vstt(yv, sg, rmode[:, 0:1], yv, ALU.mult, ALU.add)
            vcp(it, yv)
            vts(it, it, -128.0, 127.0, ALU.max, ALU.min)
            em.dwait("vector", "sync", ch=2 + g)
            vtt(dsts[g][:, 0:w], it, PP[:, 3072:3072 + w], ALU.mult)
            em.inc(VE.drain(), "%s_q" % name, 1)
            em.wait("sync", "%s_q" % name, em.cnt["%s_q" % name])
            em.dma("sync", wtile_store_ap(dst_dram, nK, 4 * b, w // 128, k),
                   dsts[g][:, 0:w].rearrange("p (j f) -> p j f", f=128),
                   ch=2 + g)

        emitted = 0
        for i in range(nt):
            while emitted <= i + 1 and emitted < nt:
                j = emitted
                if tiles[j][1] == 0 and j > 0 and j > i:
                    break
                wq_pre(j)
                emitted += 1
            wq_post(i)
        em.barrier()

    def emit_ffn0_gateup():
        for k in range(8):
            ar = A1 if k < 4 else A2
            em.dma("gpsimd", slot(ar, k % 4), d_xq0.ap()[128 * k:128 * (k + 1), :], ch=k % 4)
        for c in range(4):
            em.dwait("tensor", "gpsimd", ch=c)
        GE.memset(gmaxc[:, 2:3], 0.0)
        em.inc(GE.drain(), "fg_g", 1)
        em.wait("vector", "fg_g", 1)
        for po in range(22):
            wsl = po % 2
            if po >= 2:
                em.wait("sync", "fg_T", po - 1)
            em.dma("sync", WB[:, wsl * 2048:wsl * 2048 + 1024],
                   wtile_load_ap(d_wgq, 8, po), ch=4 + wsl)
            em.dma("sync", WB[:, wsl * 2048 + 1024:wsl * 2048 + 2048],
                   wtile_load_ap(d_wuq, 8, po), ch=4 + wsl)
            em.dwait("tensor", "sync", ch=4 + wsl)
            for tc in range(NT):
                i = po * NT + tc
                reg = 1024 * (i % 2)
                if i >= 2:
                    em.wait("tensor", "fg_V", i - 1)
                for k in range(8):
                    ar = A1 if k < 4 else A2
                    TE.matmul(PP[:, reg:reg + 512],
                              WB[:, wsl * 2048 + 128 * k:wsl * 2048 + 128 * (k + 1)],
                              slot(ar, k % 4)[:, 512 * tc:512 * (tc + 1)],
                              start=(k == 0), stop=(k == 7))
                    TE.matmul(PP[:, reg + 512:reg + 1024],
                              WB[:, wsl * 2048 + 1024 + 128 * k:wsl * 2048 + 1024 + 128 * (k + 1)],
                              slot(ar, k % 4)[:, 512 * tc:512 * (tc + 1)],
                              start=(k == 0), stop=(k == 7))
                em.inc(TE.drain(), "fg_T", 1)
                em.wait("scalar", "fg_T", em.cnt["fg_T"])
                em.wait("scalar", "fg_V", em.cnt.get("fg_V", 0))
                act(trst[:, 0:512], PP[:, reg:reg + 512], AF.Silu)
                em.inc(SE.drain(), "fg_S", 1)
                em.wait("vector", "fg_S", em.cnt["fg_S"])
                em.dwait("vector", "sync", ch=i % 2)
                vtt(y2s[:, 512 * (i % 2):512 * (i % 2) + 512], trst[:, 0:512],
                    PP[:, reg + 512:reg + 1024], ALU.mult)
                em.inc(VE.drain(), "fg_V", 1)
                VE.tensor_reduce(gmaxc[:, 3:4], y2s[:, 512 * (i % 2):512 * (i % 2) + 512],
                                 AX.X, ALU.max, apply_absolute_value=True)
                VE.drain()
                vtt(gmaxc[:, 2:3], gmaxc[:, 2:3], gmaxc[:, 3:4], ALU.max)
                em.inc(VE.drain(), "fg_Vm", 1)
                em.wait("sync", "fg_Vm", em.cnt["fg_Vm"])
                em.dma("sync", d_h0.ap()[128 * po:128 * (po + 1), 512 * tc:512 * (tc + 1)],
                       y2s[:, 512 * (i % 2):512 * (i % 2) + 512], ch=i % 2)
        em.barrier()

    def emit_scaleh():
        fold_col_max(gmaxc[:, 2:3], 20)
        allreduce_max(20, 21)
        act(scal[0:1, 22:23], scal[0:1, 9:10], AF.Sigmoid)
        em.inc(SE.drain(), "sch_s", 1)
        em.wait("vector", "sch_s", 1)
        vtt(scal[0:1, 2:3], scal[0:1, 22:23], scal[0:1, 21:22], ALU.mult)
        vts(scal[0:1, 2:3], scal[0:1, 2:3], 1.0 / 127.0, None, ALU.mult)
        VE.reciprocal(scal[0:1, 3:4], scal[0:1, 2:3])
        VE.drain()
        em.inc(VE.drain(), "sch_v", 1)
        em.wait("tensor", "sch_v", 1)
        col_from_scalar(2, scal[0:1, 2:3])
        col_from_scalar(3, scal[0:1, 3:4])
        em.barrier()

    def emit_ffn0_down():
        groups = [list(range(g, min(g + 2, NT))) for g in range(0, NT, 2)]
        for gi, tcs in enumerate(groups):
            baseT = em.cnt.get("fd_T", 0)
            baseQ = em.cnt.get("fd_qq", 0)
            for ti, tc in enumerate(tcs):
                buf = A1 if ti == 0 else A2

                def fd_pre(k, tc=tc, buf=buf, gi=gi, baseT=baseT):
                    par = k % 2
                    if em.cnt.get("fd_q", 0) >= 1:
                        em.wait("gpsimd", "fd_q", em.cnt["fd_q"])
                    if gi >= 1 and k < 2:
                        em.wait("gpsimd", "fd_T", baseT)
                    em.dma("gpsimd", ytl[par][:, 0:512],
                           d_h0.ap()[128 * k:128 * (k + 1), 512 * tc:512 * (tc + 1)],
                           ch=par)
                    em.dwait("vector", "gpsimd", ch=par)
                    if gi >= 1 and k == 0:
                        em.wait("vector", "fd_T", baseT)
                    vts(ytl[par][:, 0:512], ytl[par][:, 0:512], colsc[:, 3:4], None,
                        ALU.mult)
                    em.inc(VE.drain(), "fd_v", 1)
                    em.wait("scalar", "fd_v", em.cnt["fd_v"])
                    act(FA[:, 2048 * par:2048 * par + 512], ytl[par][:, 0:512], AF.Sign)
                    em.inc(SE.drain(), "fd_s", 1)

                def fd_post(k, tc=tc, buf=buf):
                    par = k % 2
                    yv = ytl[par][:, 0:512]
                    sg = FA[:, 2048 * par:2048 * par + 512]
                    it = itl1[:, 0:512]
                    em.wait("vector", "fd_s", k + 1 + tc * 22)
                    vstt(yv, sg, rmode[:, 0:1], yv, ALU.mult, ALU.add)
                    vcp(it, yv)
                    vts(it, it, -128.0, 127.0, ALU.max, ALU.min)
                    vts(buf[:, 512 * k:512 * (k + 1)], it, colsc[:, 2:3], None, ALU.mult)
                    em.inc(VE.drain(), "fd_q", 1)

                fd_pre(0)
                fd_pre(1)
                for k in range(22):
                    fd_post(k)
                    if k + 2 < 22:
                        fd_pre(k + 2)
                em.inc(VE.drain(), "fd_qq", 1)
            nl = len(tcs)
            for po2 in range(8):
                wsl = po2 % 2
                tgt = baseT + nl * (po2 - 1)
                if tgt > 0:
                    em.wait("sync", "fd_T", tgt)
                em.dma("sync", WB[:, wsl * 2816:wsl * 2816 + 2816],
                       wtile_load_ap(d_wdq, 22, po2), ch=4 + po2 % 4)
                em.dwait("tensor", "sync", ch=4 + po2 % 4)
                for ti, tc in enumerate(tcs):
                    buf = A1 if ti == 0 else A2
                    i3 = baseT + po2 * nl + ti
                    reg = 512 * (i3 % 4)
                    if i3 >= 4:
                        em.wait("tensor", "fd_S", i3 - 3)
                    if po2 == 0:
                        em.wait("tensor", "fd_qq", baseQ + ti + 1)
                    for k in range(22):
                        TE.matmul(PP[:, reg:reg + 512],
                                  WB[:, wsl * 2816 + 128 * k:wsl * 2816 + 128 * (k + 1)],
                                  buf[:, 512 * k:512 * (k + 1)], start=(k == 0),
                                  stop=(k == 21))
                    em.inc(TE.drain(), "fd_T", 1)
                    em.wait("scalar", "fd_T", em.cnt["fd_T"])
                    em.dwait("scalar", "sync", ch=i3 % 4)
                    act(y2s[:, 512 * (i3 % 4):512 * (i3 % 4) + 512], PP[:, reg:reg + 512],
                        AF.Silu)
                    em.inc(SE.drain(), "fd_S", 1)
                    em.wait("sync", "fd_S", em.cnt["fd_S"])
                    em.dma("sync",
                           d_f[0].ap()[128 * po2:128 * (po2 + 1), 512 * tc:512 * (tc + 1)],
                           y2s[:, 512 * (i3 % 4):512 * (i3 % 4) + 512], ch=i3 % 4)
        em.barrier()

    def emit_combine(l, resid_dram, out_dram):
        for po in range(8):
            if po >= 2:
                em.wait("gpsimd", "cb%d_v" % l, po - 1)
            em.dma("sync", FA[:, 0:T], resid_dram.ap()[128 * po:128 * (po + 1), :], ch=0)
            em.dma("gpsimd", slot(A1, po % 2), d_s[l].ap()[128 * po:128 * (po + 1), :], ch=po % 2)
            em.dma("gpsimd", slot(A2, po % 2), d_f[l].ap()[128 * po:128 * (po + 1), :], ch=2 + po % 2)
            em.dwait("vector", "sync", ch=0)
            em.dwait("vector", "gpsimd", ch=po % 2)
            em.dwait("vector", "gpsimd", ch=2 + po % 2)
            vtt(FY[:, 0:T], slot(A1, po % 2), slot(A2, po % 2), ALU.mult)
            vtt(FY[:, 0:T], FY[:, 0:T], FA[:, 0:T], ALU.add)
            em.inc(VE.drain(), "cb%d_v" % l, 1)
            em.wait("sync", "cb%d_v" % l, po + 1)
            em.dma("sync", out_dram.ap()[128 * po:128 * (po + 1), :], FY[:, 0:T], ch=1)
            em.dwait("vector", "sync", ch=1)
        em.barrier()

    def emit_inproj1():
        for k in range(8):
            ar = A1 if k < 4 else A2
            em.dma("gpsimd", slot(ar, k % 4), d_xn[1].ap()[128 * k:128 * (k + 1), :], ch=k % 4)
        for c in range(4):
            em.dwait("tensor", "gpsimd", ch=c)
        for po in range(16):
            wsl = po % 2
            if po >= 2:
                em.wait("gpsimd", "ip_T", (po - 1) * NT)
            for k in range(8):
                em.dma("gpsimd", WB[:, wsl * 1024 + 128 * k:wsl * 1024 + 128 * (k + 1)],
                       w1_in.ap()[128 * k:128 * (k + 1), 128 * po:128 * (po + 1)], ch=4 + wsl)
            em.dwait("tensor", "gpsimd", ch=4 + wsl)
            for tc in range(NT):
                i = po * NT + tc
                reg = 512 * (i % 4)
                if em.cnt.get("ip_E", 0) >= 4:
                    em.wait("tensor", "ip_E", em.cnt["ip_E"] - 3)
                for k in range(8):
                    ar = A1 if k < 4 else A2
                    TE.matmul(PP[:, reg:reg + 512],
                              WB[:, wsl * 1024 + 128 * k:wsl * 1024 + 128 * (k + 1)],
                              slot(ar, k % 4)[:, 512 * tc:512 * (tc + 1)],
                              start=(k == 0), stop=(k == 7))
                em.inc(TE.drain(), "ip_T", 1)
                em.wait("vector", "ip_T", em.cnt["ip_T"])
                vcp(slotn(A3, 2, 512 * tc, 512), PP[:, reg:reg + 512])
                em.inc(VE.drain(), "ip_E", 1)
            em.inc(VE.drain(), "ip_V", 1)
            if po < 8:
                wc = convw1[:, 4 * po:4 * po + 4]
                em.dwait("vector", "sync", ch=6)
                VE.memset(A3[:, 2 * SW:2 * SW + PAD], 0.0)
                VE.drain()
                VE.memset(A3[:, 2 * SW + PAD + T:3 * SW], 0.0)
                VE.drain()
                vts(slot(A3, 0), slotn(A3, 2, 0, T), wc[:, 3:4], None, ALU.mult)
                vstt(slot(A3, 1), slotn(A3, 2, -1, T), wc[:, 2:3], slot(A3, 0), ALU.mult, ALU.add)
                vstt(slot(A3, 0), slotn(A3, 2, -2, T), wc[:, 1:2], slot(A3, 1), ALU.mult, ALU.add)
                vstt(slot(A3, 1), slotn(A3, 2, -3, T), wc[:, 0:1], slot(A3, 0), ALU.mult, ALU.add)
                vts(slot(A3, 1), slot(A3, 1), convb1[:, po:po + 1], None, ALU.add)
                em.inc(VE.drain(), "ip_Vc", 1)
                em.wait("scalar", "ip_Vc", em.cnt["ip_Vc"])
                act(slot(A3, 0), slot(A3, 1), AF.Silu)
                em.inc(SE.drain(), "ip_S", 1)
                em.wait("sync", "ip_S", em.cnt["ip_S"])
                em.dma("sync", d_xc1.ap()[128 * po:128 * (po + 1), :], slot(A3, 0), ch=6)
                em.dwait("vector", "sync", ch=6)
            else:
                em.wait("sync", "ip_V", po + 1)
                em.dma("sync", d_z1.ap()[128 * (po - 8):128 * (po - 7), :], slotn(A3, 2, 0, T), ch=7)
                em.dwait("vector", "sync", ch=7)
        em.barrier()

    def emit_xproj1():
        for k in range(8):
            em.dma("gpsimd", WA[:, 1056 * k:1056 * k + 1056],
                   w1_x.ap()[128 * k:128 * (k + 1), :], ch=k % 4)
        for c in range(4):
            em.dwait("tensor", "gpsimd", ch=c)
        for tc in range(NT):
            sl = tc % 2
            ar = A1 if sl == 0 else A2
            if tc >= 2:
                em.wait("sync", "xj_T", tc - 1)
            for k in range(8):
                em.dma("sync", ar[:, 512 * k:512 * k + 512],
                       d_xc1.ap()[128 * k:128 * (k + 1), 512 * tc:512 * (tc + 1)], ch=sl)
            em.dwait("tensor", "sync", ch=sl)
            for po in range(8):
                reg = 512 * (po % 4)
                if tc > 0 or po >= 4:
                    em.wait("tensor", "xj_V", em.cnt.get("xj_V", 0) - 3)
                for k in range(8):
                    TE.matmul(PP[:, reg:reg + 512],
                              WA[:, 1056 * k + 128 * po:1056 * k + 128 * (po + 1)],
                              ar[:, 512 * k:512 * k + 512], start=(k == 0), stop=(k == 7))
                em.inc(TE.drain(), "xj_T2", 1)
                em.wait("scalar", "xj_T2", em.cnt["xj_T2"])
                em.wait("scalar", "xj_Ssp", em.cnt.get("xj_Ssp", 0))
                act(sgn1[:, 0:512], PP[:, reg:reg + 512], AF.Abs)
                act(sgn1[:, 0:512], sgn1[:, 0:512], AF.Exp, scale=-1.0)
                em.inc(SE.drain(), "xj_Sa", 1)
                em.wait("vector", "xj_Sa", em.cnt["xj_Sa"])
                vts(sgn1[:, 0:512], sgn1[:, 0:512], 1.0, None, ALU.add)
                em.inc(VE.drain(), "xj_Va", 1)
                em.wait("scalar", "xj_Va", em.cnt["xj_Va"])
                act(sgn1[:, 0:512], sgn1[:, 0:512], AF.Ln)
                em.inc(SE.drain(), "xj_S", 1)
                em.wait("vector", "xj_S", em.cnt["xj_S"])
                em.dwait("vector", "sync", ch=4 + po % 2)
                vstt(FA[:, 512 * (po % 2):512 * (po % 2) + 512], PP[:, reg:reg + 512],
                     0.0, sgn1[:, 0:512], ALU.max, ALU.add)
                em.inc(VE.drain(), "xj_Ssp", 1)
                em.dwait("vector", "sync", ch=2 + po % 2)
                vtt(y2s[:, 512 * (po % 2):512 * (po % 2) + 512],
                    ar[:, 512 * po:512 * po + 512],
                    FA[:, 512 * (po % 2):512 * (po % 2) + 512], ALU.mult)
                em.inc(VE.drain(), "xj_V", 1)
                em.wait("sync", "xj_V", em.cnt["xj_V"])
                em.dma("sync", d_u1.ap()[128 * po:128 * (po + 1), 512 * tc:512 * (tc + 1)],
                       y2s[:, 512 * (po % 2):512 * (po % 2) + 512], ch=2 + po % 2)
                em.wait("sync", "xj_Ssp", em.cnt["xj_Ssp"])
                em.dma("sync", d_dt1.ap()[128 * po:128 * (po + 1), 512 * tc:512 * (tc + 1)],
                       FA[:, 512 * (po % 2):512 * (po % 2) + 512], ch=4 + po % 2)
            for k in range(8):
                TE.matmul(PP[0:32, 2048:2560], WA[:, 1056 * k + 1024:1056 * k + 1056],
                          ar[:, 512 * k:512 * k + 512], start=(k == 0), stop=(k == 7))
            em.inc(TE.drain(), "xj_T", 1)
            em.wait("vector", "xj_T", tc + 1)
            vcp(y2s[0:32, 1024:1536], PP[0:32, 2048:2560])
            em.inc(VE.drain(), "xj_Vb", 1)
            em.wait("sync", "xj_Vb", tc + 1)
            em.dma("sync", d_bc.ap()[0:32, 512 * tc:512 * (tc + 1)], y2s[0:32, 1024:1536], ch=6)
            em.dwait("vector", "sync", ch=6)
        em.barrier()

    def emit_scan1():
        FBb = FA[:, 1024:2048]
        for tq in range(4):
            for n in range(16):
                em.dma("sync", A1[:, TQ * n:TQ * (n + 1)],
                       bass.AP(d_bc, n * T + tq * TQ, [[0, 128], [1, TQ]]), ch=n % 4)
                em.dma("sync", A2[:, TQ * n:TQ * (n + 1)],
                       bass.AP(d_bc, (16 + n) * T + tq * TQ, [[0, 128], [1, TQ]]), ch=n % 4)
            for c in range(4):
                em.dwait("vector", "sync", ch=c)
                em.dwait("scalar", "sync", ch=c)
                em.dwait("gpsimd", "sync", ch=c)
            for dh in range(8):
                if em.cnt.get("s1_o", 0):
                    em.wait("gpsimd", "s1_o", em.cnt["s1_o"])
                    em.wait("sync", "s1_o", em.cnt["s1_o"])
                em.dma("sync", FA[:, 0:TQ],
                       d_dt1.ap()[128 * dh:128 * (dh + 1), tq * TQ:(tq + 1) * TQ], ch=4)
                em.dma("gpsimd", A3[:, 0:TQ],
                       d_u1.ap()[128 * dh:128 * (dh + 1), tq * TQ:(tq + 1) * TQ], ch=0)
                em.dma("gpsimd", A3[:, TQ:2 * TQ],
                       d_xc1.ap()[128 * dh:128 * (dh + 1), tq * TQ:(tq + 1) * TQ], ch=1)
                em.dwait("scalar", "sync", ch=4)
                em.dwait("vector", "gpsimd", ch=0)
                em.dwait("vector", "gpsimd", ch=1)
                GE.memset(FY[:, 0:TQ], 0.0)
                em.inc(GE.drain(), "s1_z", 1)
                for n in range(16):
                    reg = 1024 * (n % 2)
                    idx = dh * 16 + n
                    fb = FB if n % 2 == 0 else FBb
                    if n >= 2:
                        em.wait("scalar", "s1_sc", em.cnt.get("s1_sc", 0) - 1)
                    act(PP[:, reg:reg + TQ], FA[:, 0:TQ], AF.Exp, scale=-(float(n + 1)))
                    em.inc(SE.drain(), "s1_a", 1)
                    vtt(y2s[:, 0:TQ], A3[:, 0:TQ], A1[:, TQ * n:TQ * n + TQ], ALU.mult)
                    em.wait("vector", "s1_a", em.cnt["s1_a"])
                    if n >= 2:
                        em.wait("vector", "s1_gy", em.cnt.get("s1_gy", 0) - 1)
                    if tq == 0 and n == 0:
                        em.wait("vector", "s1_z", em.cnt["s1_z"])
                    if tq == 0:
                        VE.tensor_tensor_scan(fb[:, 0:TQ], PP[:, reg:reg + TQ],
                                              y2s[:, 0:TQ], 0.0, ALU.mult, ALU.add)
                    else:
                        VE.tensor_tensor_scan(fb[:, 0:TQ], PP[:, reg:reg + TQ],
                                              y2s[:, 0:TQ], carry[:, idx:idx + 1],
                                              ALU.mult, ALU.add)
                    VE.drain()
                    em.inc(VE.drain(), "s1_sc", 1)
                    em.wait("scalar", "s1_sc", em.cnt["s1_sc"])
                    act(carry[:, idx:idx + 1], fb[:, TQ - 1:TQ], AF.Copy)
                    em.inc(SE.drain(), "s1_cv", 1)
                    em.wait("gpsimd", "s1_cv", em.cnt["s1_cv"])
                    if n == 0:
                        em.wait("gpsimd", "s1_z", em.cnt["s1_z"])
                    GE.tensor_tensor(out=y2s[:, 1024:1024 + TQ], in0=fb[:, 0:TQ],
                                     in1=A2[:, TQ * n:TQ * n + TQ], op=ALU.mult)
                    GE.drain()
                    GE.tensor_tensor(out=FY[:, 0:TQ], in0=FY[:, 0:TQ],
                                     in1=y2s[:, 1024:1024 + TQ], op=ALU.add)
                    em.inc(GE.drain(), "s1_gy", 1)
                em.wait("vector", "s1_gy", em.cnt["s1_gy"])
                vstt(FY[:, 0:TQ], A3[:, TQ:2 * TQ], d1col[:, dh:dh + 1], FY[:, 0:TQ],
                     ALU.mult, ALU.add)
                vcp(y2s[:, 0:TQ], FY[:, 0:TQ])
                em.inc(VE.drain(), "s1_o", 1)
                em.wait("sync", "s1_o", em.cnt["s1_o"])
                em.dma("sync", d_y1.ap()[128 * dh:128 * (dh + 1), tq * TQ:(tq + 1) * TQ],
                       y2s[:, 0:TQ], ch=5)
                em.dwait("vector", "sync", ch=5)
                em.dwait("gpsimd", "sync", ch=4)
        em.barrier()

    def emit_gate_wout1():
        for k in range(8):
            em.dma("gpsimd", WA[:, 1024 * k:1024 * (k + 1)],
                   w1_out.ap()[128 * k:128 * (k + 1), :], ch=k % 4)
        for c in range(4):
            em.dwait("tensor", "gpsimd", ch=c)
        for tc in range(NT):
            if tc > 0:
                em.wait("scalar", "g1_To", 8 * tc)
            for k in range(8):
                par = k % 2
                if em.cnt.get("g1_S", 0) >= 2:
                    em.wait("gpsimd", "g1_S", em.cnt["g1_S"] - 1)
                em.dma("gpsimd", y2s[:, 512 * par:512 * (par + 1)],
                       d_z1.ap()[128 * k:128 * (k + 1), 512 * tc:512 * (tc + 1)], ch=4 + par)
                em.dwait("scalar", "gpsimd", ch=4 + par)
                act(A3[:, 512 * k:512 * (k + 1)], y2s[:, 512 * par:512 * (par + 1)], AF.Silu)
                em.inc(SE.drain(), "g1_S", 1)
            em.wait("vector", "g1_S", em.cnt["g1_S"])
            for k in range(8):
                par = k % 2
                if em.cnt.get("g1_V", 0) >= 2:
                    em.wait("gpsimd", "g1_V", em.cnt["g1_V"] - 1)
                em.dma("gpsimd", y2s[:, 1024 + 512 * par:1024 + 512 * (par + 1)],
                       d_y1.ap()[128 * k:128 * (k + 1), 512 * tc:512 * (tc + 1)], ch=6 + par)
                em.dwait("vector", "gpsimd", ch=6 + par)
                vtt(A3[:, 512 * k:512 * (k + 1)], A3[:, 512 * k:512 * (k + 1)],
                    y2s[:, 1024 + 512 * par:1024 + 512 * (par + 1)], ALU.mult)
                em.inc(VE.drain(), "g1_V", 1)
            em.wait("tensor", "g1_V", em.cnt["g1_V"])
            for po2 in range(8):
                reg = 2048 + 512 * (po2 % 2)
                if tc > 0 or po2 >= 2:
                    em.wait("tensor", "g1_Vo", em.cnt.get("g1_Vo", 0) - 1)
                for k in range(8):
                    TE.matmul(PP[:, reg:reg + 512],
                              WA[:, 1024 * k + 128 * po2:1024 * k + 128 * (po2 + 1)],
                              A3[:, 512 * k:512 * (k + 1)], start=(k == 0), stop=(k == 7))
                em.inc(TE.drain(), "g1_To", 1)
                em.wait("vector", "g1_To", em.cnt["g1_To"])
                vcp(trst[:, 0:512] if po2 % 2 == 0 else trst[:, 0:512], PP[:, reg:reg + 512])
                em.inc(VE.drain(), "g1_Vo", 1)
                em.wait("sync", "g1_Vo", em.cnt["g1_Vo"])
                em.dma("sync", d_s[1].ap()[128 * po2:128 * (po2 + 1), 512 * tc:512 * (tc + 1)],
                       trst[:, 0:512], ch=po2 % 2)
                em.dwait("vector", "sync", ch=po2 % 2)
        em.barrier()

    def emit_wquant_mean(w_dram, dst_dram, K, PO, name):
        nK = K // 128
        nH = (PO + 2047) // 2048
        for k in range(nK):
            ch = k % 2
            for hh in range(nH):
                w0c, w1c = 2048 * hh, min(PO, 2048 * (hh + 1))
                ww = w1c - w0c
                if em.cnt.get("%s_a" % name, 0):
                    em.wait("sync", "%s_a" % name, em.cnt["%s_a" % name])
                em.dma("sync", FA[:, 0:ww], w_dram.ap()[128 * k:128 * (k + 1), w0c:w1c], ch=ch)
                em.dwait("scalar", "sync", ch=ch)
                act(A3[:, 0:ww], FA[:, 0:ww], AF.Abs)
                em.inc(SE.drain(), "%s_a" % name, 1)
                em.wait("tensor", "%s_a" % name, em.cnt["%s_a" % name])
                first = (k == 0 and hh == 0)
                last = (k == nK - 1 and hh == nH - 1)
                for b in range(ww // 512):
                    TE.matmul(PP[0:1, 1024 + 512 * b:1024 + 512 * (b + 1)], onesb[:, 4:5],
                              A3[:, 512 * b:512 * (b + 1)], start=first, stop=last)
                em.inc(TE.drain(), "%s_t" % name, 1)
                em.wait("scalar", "%s_t" % name, em.cnt["%s_t" % name])
        em.wait("vector", "%s_t" % name, em.cnt["%s_t" % name])
        VE.tensor_reduce(scal[0:1, 24:25], PP[0:1, 1024:1024 + min(PO, 2048)], AX.X, ALU.add)
        VE.drain()
        vts(scal[0:1, 24:25], scal[0:1, 24:25], 1.0 / (K * PO), None, ALU.mult)
        vts(scal[0:1, 24:25], scal[0:1, 24:25], 1e-5, None, ALU.max)
        VE.reciprocal(scal[0:1, 25:26], scal[0:1, 24:25])
        VE.drain()
        em.inc(VE.drain(), "%s_v" % name, 1)
        em.wait("tensor", "%s_v" % name, 1)
        col_from_scalar(4, scal[0:1, 24:25])
        col_from_scalar(5, scal[0:1, 25:26])
        em.barrier()
        for k in range(nK):
            ch = k % 2
            for hh in range(nH):
                w0c, w1c = 2048 * hh, min(PO, 2048 * (hh + 1))
                ww = w1c - w0c
                em.dma("sync", FA[:, 0:ww], w_dram.ap()[128 * k:128 * (k + 1), w0c:w1c], ch=ch)
                em.dwait("vector", "sync", ch=ch)
                for b in range(ww // 512):
                    par = b % 2
                    em.dwait("vector", "sync", ch=2 + par)
                    vts(ytl[par][:, 0:512], FA[:, 512 * b:512 * (b + 1)], colsc[:, 5:6],
                        None, ALU.mult)
                    qround(y2s[:, 512 * par:512 * (par + 1)], par, -1.0, 1.0,
                           post_col=colsc[:, 4:5])
                    em.inc(VE.drain(), "%s_q" % name, 1)
                    em.wait("sync", "%s_q" % name, em.cnt["%s_q" % name])
                    em.dma("sync", wtile_store_ap(dst_dram, nK, (w0c + 512 * b) // 128, 4, k),
                           y2s[:, 512 * par:512 * (par + 1)].rearrange("p (j f) -> p j f", f=128),
                           ch=2 + par)
                em.inc(VE.drain(), "%s_fk" % name, 1)
                em.wait("sync", "%s_fk" % name, em.cnt["%s_fk" % name])
        em.barrier()

    def emit_token_scale_rows(name):
        for b in range(T // 128):
            if em.cnt.get("%s_c" % name, 0) >= 2:
                em.wait("tensor", "%s_c" % name, em.cnt["%s_c" % name] - 1)
            TE.transpose(PP[:, 2048 + 128 * (b % 2):2048 + 128 * (b % 2) + 128],
                         FA[:, 128 * b:128 * (b + 1)], eyef[:])
            em.inc(TE.drain(), "%s_t" % name, 1)
            em.wait("vector", "%s_t" % name, em.cnt["%s_t" % name])
            VE.tensor_reduce(gmaxc[:, 4 + b % 2:5 + b % 2],
                             PP[:, 2048 + 128 * (b % 2):2048 + 128 * (b % 2) + 128],
                             AX.X, ALU.max)
            VE.drain()
            vcp(sgn[0][:, b:b + 1], gmaxc[:, 4 + b % 2:5 + b % 2])
            em.inc(VE.drain(), "%s_c" % name, 1)
            if b + 2 < T // 128:
                em.wait("tensor", "%s_c" % name, em.cnt["%s_c" % name] - 1)
        em.wait("tensor", "%s_c" % name, em.cnt["%s_c" % name])
        nblk = T // 128
        for g in range((nblk + 127) // 128):
            w = min(128, nblk - 128 * g)
            TE.transpose(PP[0:w, 2048:2176], sgn[0][:, 128 * g:128 * g + w], eyef[:])
            em.inc(TE.drain(), "%s_t2" % name, 1)
            em.wait("vector", "%s_t2" % name, em.cnt["%s_t2" % name])
            vcp(accs[0:w, 0:128], PP[0:w, 2048:2176])
            em.inc(VE.drain(), "%s_v2" % name, 1)
            em.wait("gpsimd", "%s_v2" % name, em.cnt["%s_v2" % name])
            em.dma("gpsimd", bass.AP(ROWS, 128 * 128 * g, [[FW, 1], [1, w * 128]]),
                   accs[0:w, 0:128], ch=6)
            em.dwait("vector", "gpsimd", ch=6)
        vts(qscr[0:1, 0:T], qscr[0:1, 0:T], 1e-5, None, ALU.max)
        vts(qinr[0:1, 0:T], qscr[0:1, 0:T], 1.0 / 127.0, None, ALU.mult)
        VE.reciprocal(qscr[0:1, 0:T], qinr[0:1, 0:T])
        VE.drain()
        em.inc(VE.drain(), "%s_fin" % name, 1)
        em.wait("tensor", "%s_fin" % name, 1)
        em.wait("scalar", "%s_fin" % name, 1)

    def emit_quant_tok(src, dst, R, name):
        nR = R // 128
        for tc in range(NT):
            TE.matmul(PP[:, 3072:3584], onesf[0:1, 0:128],
                      qscr[0:1, 512 * tc:512 * (tc + 1)], start=True, stop=True)
            TE.matmul(PP[:, 2560:3072], ones33f[32:33, 0:128],
                      qinr[0:1, 512 * tc:512 * (tc + 1)], start=True, stop=True)
            em.inc(TE.drain(), "%s_r" % name, 1)
            em.wait("vector", "%s_r" % name, em.cnt["%s_r" % name])
            for po in range(nR):
                par = po % 2
                if em.cnt.get("%s_q" % name, 0) >= 2:
                    em.wait("gpsimd", "%s_q" % name, em.cnt["%s_q" % name] - 1)
                em.dma("gpsimd", ytl[par][:, 0:512],
                       src.ap()[128 * po:128 * (po + 1), 512 * tc:512 * (tc + 1)], ch=par)
                em.dwait("vector", "gpsimd", ch=par)
                em.dwait("vector", "sync", ch=par)
                vtt(ytl[par][:, 0:512], ytl[par][:, 0:512], PP[:, 3072:3584], ALU.mult)
                qround(y2s[:, 512 * par:512 * (par + 1)], par, -128.0, 127.0,
                       post_rep=PP[:, 2560:3072])
                em.inc(VE.drain(), "%s_q" % name, 1)
                em.wait("sync", "%s_q" % name, em.cnt["%s_q" % name])
                em.dma("sync", dst.ap()[128 * po:128 * (po + 1), 512 * tc:512 * (tc + 1)],
                       y2s[:, 512 * par:512 * (par + 1)], ch=par)
                if po >= 2:
                    em.dwait("gpsimd", "sync", ch=par)
            em.inc(VE.drain(), "%s_tc" % name, 1)
            em.wait("tensor", "%s_tc" % name, tc + 1)
        em.barrier()

    d_xq1 = d_y2f0

    def emit_xq1():
        for k in range(8):
            sl = k % 2
            em.dma("gpsimd", slot(A1, sl), d_xn[1].ap()[128 * k:128 * (k + 1), :], ch=sl)
            em.dwait("scalar", "gpsimd", ch=sl)
            em.wait("scalar", "xq1_f", em.cnt.get("xq1_f", 0))
            act(slot(A3, 0), slot(A1, sl), AF.Abs)
            em.inc(SE.drain(), "xq1_fa", 1)
            em.wait("vector", "xq1_fa", em.cnt["xq1_fa"])
            if k == 0:
                vcp(FA[:, 0:T], slot(A3, 0))
            else:
                vtt(FA[:, 0:T], FA[:, 0:T], slot(A3, 0), ALU.max)
            em.inc(VE.drain(), "xq1_f", 1)
            if k + 2 <= 8:
                em.wait("gpsimd", "xq1_f", max(1, em.cnt["xq1_f"] - 1))
        em.wait("tensor", "xq1_f", 8)
        em.wait("scalar", "xq1_f", 8)
        emit_token_scale_rows("ts1")
        emit_quant_wide(d_xn[1], d_xq1, D, "qt1", -128.0, 127.0, "row")

    def emit_ffn1_w1():
        for k in range(8):
            ar = A1 if k < 4 else A2
            em.dma("gpsimd", slot(ar, k % 4), d_xq1.ap()[128 * k:128 * (k + 1), :], ch=k % 4)
        for c in range(4):
            em.dwait("tensor", "gpsimd", ch=c)
        GE.memset(FA[:, 0:T], 0.0)
        em.inc(GE.drain(), "f1_g", 1)
        em.wait("vector", "f1_g", 1)
        VE.memset(A3[:, 2 * SW:2 * SW + PAD], 0.0)
        VE.drain()
        VE.memset(A3[:, 2 * SW + PAD + T:3 * SW], 0.0)
        em.inc(VE.drain(), "f1_p", 1)
        for po in range(32):
            wsl = po % 2
            if po >= 2:
                em.wait("sync", "f1_T", (po - 1) * NT)
            em.dma("sync", WB[:, wsl * 1024:wsl * 1024 + 1024],
                   wtile_load_ap(d_w1q, 8, po), ch=4 + wsl)
            em.dwait("tensor", "sync", ch=4 + wsl)
            for tc in range(NT):
                i = po * NT + tc
                reg = 512 * (i % 4)
                if em.cnt.get("f1_E", 0) >= 4:
                    em.wait("tensor", "f1_E", em.cnt["f1_E"] - 3)
                for k in range(8):
                    ar = A1 if k < 4 else A2
                    TE.matmul(PP[:, reg:reg + 512],
                              WB[:, wsl * 1024 + 128 * k:wsl * 1024 + 128 * (k + 1)],
                              slot(ar, k % 4)[:, 512 * tc:512 * (tc + 1)],
                              start=(k == 0), stop=(k == 7))
                em.inc(TE.drain(), "f1_T", 1)
                em.wait("scalar", "f1_T", em.cnt["f1_T"])
                act(slotn(A3, 2, 512 * tc, 512), PP[:, reg:reg + 512], AF.Copy)
                em.inc(SE.drain(), "f1_E", 1)
            wc = convw2[:, 3 * po:3 * po + 3]
            em.wait("vector", "f1_E", em.cnt["f1_E"])
            if po > 0:
                em.wait("vector", "f1_Sa", po)
            vts(slot(A3, 0), slotn(A3, 2, -1, T), wc[:, 0:1], None, ALU.mult)
            vstt(slot(A3, 1), slotn(A3, 2, 0, T), wc[:, 1:2], slot(A3, 0), ALU.mult, ALU.add)
            vstt(slot(A3, 0), slotn(A3, 2, 1, T), wc[:, 2:3], slot(A3, 1), ALU.mult, ALU.add)
            vts(slot(A3, 0), slot(A3, 0), convb2[:, po:po + 1], None, ALU.add)
            em.inc(VE.drain(), "f1_Vc", 1)
            em.wait("scalar", "f1_Vc", po + 1)
            em.dwait("scalar", "sync", ch=6)
            em.dwait("scalar", "sync", ch=7)
            fbuf = A3[:, EXT:EXT + T]
            act(fbuf, slot(A3, 0), AF.Silu)
            em.inc(SE.drain(), "f1_S", 1)
            em.wait("sync", "f1_S", po + 1)
            em.dma("sync", d_h1.ap()[128 * po:128 * (po + 1), 0:T // 2],
                   fbuf[:, 0:T // 2], ch=6)
            em.dma("sync", d_h1.ap()[128 * po:128 * (po + 1), T // 2:T],
                   fbuf[:, T // 2:T], ch=7)
            act(slot(A3, 1), fbuf, AF.Abs)
            em.inc(SE.drain(), "f1_Sa", 1)
            em.wait("vector", "f1_Sa", po + 1)
            vtt(FA[:, 0:T], FA[:, 0:T], slot(A3, 1), ALU.max)
            em.inc(VE.drain(), "f1_Vm", 1)
        em.barrier()
        emit_token_scale_rows("ts2")
        emit_quant_wide(d_h1, d_h1q, FF1, "qt2", -128.0, 127.0, "row")

    def emit_ffn1_w2():
        groups = [list(range(g, min(g + 2, NT))) for g in range(0, NT, 2)]
        for gi, tcs in enumerate(groups):
            baseT = em.cnt.get("f2_T", 0)
            for ti, tc in enumerate(tcs):
                buf = A1 if ti == 0 else A2
                if gi >= 1 and ti == 0:
                    em.wait("gpsimd", "f2_T", baseT)
                for k in range(32):
                    em.dma("gpsimd", buf[:, 512 * k:512 * (k + 1)],
                           d_h1q.ap()[128 * k:128 * (k + 1), 512 * tc:512 * (tc + 1)],
                           ch=k % 4)
            for c in range(4):
                em.dwait("tensor", "gpsimd", ch=c)
            nl = len(tcs)
            for po2 in range(8):
                wsl = po2 % 2
                tgt = baseT + nl * (po2 - 1)
                if tgt > 0:
                    em.wait("sync", "f2_T", tgt)
                em.dma("sync", WA[:, wsl * 4096:wsl * 4096 + 4096],
                       wtile_load_ap(d_w2q, 32, po2), ch=4 + po2 % 4)
                em.dwait("tensor", "sync", ch=4 + po2 % 4)
                for ti, tc in enumerate(tcs):
                    buf = A1 if ti == 0 else A2
                    i3 = baseT + po2 * nl + ti
                    reg = 512 * (i3 % 4)
                    if i3 >= 4:
                        em.wait("tensor", "f2_S", i3 - 3)
                    for k in range(32):
                        TE.matmul(PP[:, reg:reg + 512],
                                  WA[:, wsl * 4096 + 128 * k:wsl * 4096 + 128 * (k + 1)],
                                  buf[:, 512 * k:512 * (k + 1)], start=(k == 0),
                                  stop=(k == 31))
                    em.inc(TE.drain(), "f2_T", 1)
                    em.wait("scalar", "f2_T", em.cnt["f2_T"])
                    em.dwait("scalar", "sync", ch=i3 % 4)
                    act(y2s[:, 512 * (i3 % 4):512 * (i3 % 4) + 512], PP[:, reg:reg + 512],
                        AF.Silu)
                    em.inc(SE.drain(), "f2_S", 1)
                    em.wait("sync", "f2_S", em.cnt["f2_S"])
                    em.dma("sync",
                           d_f[1].ap()[128 * po2:128 * (po2 + 1), 512 * tc:512 * (tc + 1)],
                           y2s[:, 512 * (i3 % 4):512 * (i3 % 4) + 512], ch=i3 % 4)
        em.barrier()

    _stage_list = [
        ("ln0", lambda: emit_ln(0, xT)),
        ("scale0", emit_scale0),
        ("winx_conv", emit_winx_conv),
        ("wx", emit_wx),
        ("scan0", emit_scan0),
        ("xq0", emit_xq0),
        ("gate_wout0", emit_gate_wout0),
        ("wq_wg", lambda: emit_wquant_row(f0_wg, d_wgq, D, FF0, "wg")),
        ("wq_wu", lambda: emit_wquant_row(f0_wu, d_wuq, D, FF0, "wu")),
        ("wq_wd", lambda: emit_wquant_row(f0_wd, d_wdq, FF0, D, "wd")),
        ("ffn0_gateup", emit_ffn0_gateup),
        ("scaleh", emit_scaleh),
        ("ffn0_down", emit_ffn0_down),
        ("combine0", lambda: emit_combine(0, xT, d_x1)),
        ("ln1", lambda: emit_ln(1, d_x1)),
        ("inproj1", emit_inproj1),
        ("xproj1", emit_xproj1),
        ("scan1", emit_scan1),
        ("gate_wout1", emit_gate_wout1),
        ("wq_m1", lambda: emit_wquant_mean(f1_w1, d_w1q, D, FF1, "m1")),
        ("wq_m2", lambda: emit_wquant_mean(f1_w2, d_w2q, FF1, D, "m2")),
        ("xq1", emit_xq1),
        ("ffn1_w1", emit_ffn1_w1),
        ("ffn1_w2", emit_ffn1_w2),
        ("combine1", lambda: emit_combine(1, d_x1, out)),
    ]
    global STAGE_RANGES
    STAGE_RANGES = []
    for _sname, _sfn in _stage_list:
        _n0 = len(_BB.instructions)
        _sfn()
        STAGE_RANGES.append((_sname, _n0, len(_BB.instructions)))

    em.finalize()
    return nc



def mask_const():
    m = np.zeros((32, 32), np.float32)
    m[np.tril_indices(32, -1)] = BIG_NEG
    return np.ascontiguousarray(m.reshape(1, 1024))


def prepare_core_inputs(inputs, b):
    c = lambda a: np.ascontiguousarray(np.asarray(a), dtype=np.float32)
    xp = np.asarray(inputs["L1_attn_x_proj_w"])
    xp_re = np.concatenate([xp[32:], xp[:16], xp[16:32]], axis=0)
    return {
        "xT": c(np.asarray(inputs["L0_attn_x"])[b].T),
        "w0_in": c(np.asarray(inputs["L0_attn_W_in"]).T),
        "w0_conv": c(np.asarray(inputs["L0_attn_conv_w"])[:, 0, :]),
        "w0_convb": c(np.asarray(inputs["L0_attn_conv_b"])[:, None]),
        "w0_x": c(np.asarray(inputs["L0_attn_W_x"])[list(range(64, 128)) + list(range(0, 64)) + [128]].T),
        "w0_A": c(np.asarray(inputs["L0_attn_A_log"])[None, :]),
        "w0_D": c(np.asarray(inputs["L0_attn_D"])[None, :]),
        "w0_out": c(np.asarray(inputs["L0_attn_W_out"]).T),
        "f0_clipx": c(np.asarray(inputs["L0_ffn_clip_x"])[:, None]),
        "f0_clipd": c(np.asarray(inputs["L0_ffn_clip_down"])[:, None]),
        "f0_wg": c(np.asarray(inputs["L0_ffn_w_gate"]).T),
        "f0_wu": c(np.asarray(inputs["L0_ffn_w_up"]).T),
        "f0_wd": c(np.asarray(inputs["L0_ffn_w_down"]).T),
        "w1_in": c(np.asarray(inputs["L1_attn_in_proj_w"]).T),
        "w1_conv": c(np.asarray(inputs["L1_attn_conv_w"])[:, 0, :]),
        "w1_convb": c(np.asarray(inputs["L1_attn_conv_b"])[:, None]),
        "w1_x": c(xp_re.T),
        "w1_D": c(np.asarray(inputs["L1_attn_D_param"])[None, :]),
        "w1_out": c(np.asarray(inputs["L1_attn_out_proj_w"]).T),
        "f1_w1": c(np.asarray(inputs["L1_ffn_w1"]).T),
        "f1_conv": c(np.asarray(inputs["L1_ffn_conv_w"])[:, 0, :]),
        "f1_convb": c(np.asarray(inputs["L1_ffn_conv_b"])[:, None]),
        "f1_w2": c(np.asarray(inputs["L1_ffn_w2"]).T),
        "cst_mask": mask_const(),
        "cst_eyeb": np.eye(128, dtype=np.float32),
    }


def kernel(**inputs):
    nc = build_program(T=4096)
    in_maps = [prepare_core_inputs(inputs, b) for b in range(8)]
    res = run_bass_kernel_spmd(nc, in_maps, core_ids=list(range(8)))
    return np.stack([np.ascontiguousarray(r["out"].T) for r in res.results]).astype(np.float32)
